# version 01
# baseline (speedup 1.0000x reference)
import numpy as np
import ml_dtypes
from contextlib import ExitStack

import concourse.bass as bass
import concourse.mybir as mybir
from concourse.bass_utils import run_bass_kernel_spmd

F32 = mybir.dt.float32
BF16 = mybir.dt.bfloat16
I8 = mybir.dt.int8
AF = mybir.ActivationFunctionType
ALU = mybir.AluOpType
AX = mybir.AxisListType

NCORES = 8
BPC = 2
S = 2048
D = 1024
H = 16
FF = 4096
NT = S // 128


class Chan:
    __slots__ = ("sem", "count")

    def __init__(self, nc, stack, name):
        self.sem = stack.enter_context(nc.semaphore(name))
        self.count = 0

    def tok(self):
        return (self.sem, self.count)


class Prog:
    ENGINES = ("tensor", "vector", "scalar", "gpsimd", "sync")

    def __init__(self, nc, stack):
        self.nc = nc
        self.stack = stack
        self.ops = []
        self.chans = {}

    def chan(self, name):
        if name not in self.chans:
            self.chans[name] = Chan(self.nc, self.stack, name)
        return self.chans[name]

    def op(self, engine, fn, after=(), inc=(), dma=False):
        waits = []
        for a in after:
            if isinstance(a, Chan):
                if a.count > 0:
                    waits.append((a.sem, a.count))
            else:
                sem, val = a
                if val > 0:
                    waits.append((sem, val))
        wd = {}
        for sem, val in waits:
            k = id(sem)
            if k not in wd or wd[k][1] < val:
                wd[k] = (sem, val)
        incs = []
        toks = {}
        for c in inc:
            n = 16 if dma else 1
            c.count += n
            incs.append((c.sem, n))
            toks[c] = c.tok()
        self.ops.append((engine, fn, list(wd.values()), incs))
        return toks

    def join(self):
        toks = [(c.sem, c.count) for c in self.chans.values() if c.count > 0]
        for e in self.ENGINES:
            self.ops.append((e, None, toks, []))

    def emit(self):
        nc = self.nc
        with nc.Block() as block:
            for eng_name in self.ENGINES:
                section = getattr(block, eng_name)

                def make(name):
                    def body(eng):
                        for engine, fn, waits, incs in self.ops:
                            if engine != name:
                                continue
                            for sem, val in waits:
                                eng.wait_ge(sem, val)
                            inst = fn(eng) if fn is not None else None
                            if incs:
                                if inst is None:
                                    inst = eng.nop()
                                for sem, n in incs:
                                    inst.then_inc(sem, n)
                    return body

                section(make(eng_name))


def build_program(upto=4, dbg=False):
    nc = bass.Bass("TRN2", target_bir_lowering=False)
    stack = ExitStack()
    P = Prog(nc, stack)
    ld, st = P.chan("ld"), P.chan("st")
    pe, dve, act, gp = P.chan("pe"), P.chan("dve"), P.chan("act"), P.chan("gp")

    dram = lambda n, sh, dt, k="ExternalInput": nc.dram_tensor(n, sh, dt, kind=k).ap()
    x_in = dram("x", [BPC, S, D], F32)
    out = dram("out", [BPC, S, D], F32, "ExternalOutput")
    ident = dram("ident", [128, 128], BF16)
    qWT = dram("qWT", [D, D], F32)
    kWT = dram("kWT", [D, D], F32)
    vWT = dram("vWT", [D, D], F32)
    oWT = dram("oWT", [D, D], F32)
    qbT = dram("qbT", [128, 8], F32)
    kbT = dram("kbT", [128, 8], F32)
    vb_bc = dram("vb_bc", [128, D], F32)
    ob_bc = dram("ob_bc", [128, D], F32)
    skT = dram("skT", [128, 8, 8], F32)
    wgT = dram("wgT", [D, FF], F32)
    wuT = dram("wuT", [D, FF], F32)
    wdT = dram("wdT", [FF, D], F32)
    bWq = dram("bWq", [D, D], F32)
    bWk = dram("bWk", [D, D], F32)
    bWv = dram("bWv", [D, D], F32)
    bWo = dram("bWo", [D, D], F32)
    bW1 = dram("bW1", [D, FF], F32)
    bW2 = dram("bW2", [FF, D], F32)
    bia = {nm: dram(nm, [128, FF if nm == "b1_bc" else D], F32)
           for nm in ("bq_bc", "bk_bc", "bv_bc", "bo_bc", "b1_bc", "b2_bc",
                      "g1_bc", "be1_bc", "g2_bc", "be2_bc")}

    xres = dram("xres", [BPC, S, D], F32, "Internal")
    ao_scr = dram("ao_scr", [BPC, S, D], F32, "ExternalOutput" if dbg else "Internal")
    if dbg:
        qn_dbg = dram("qn_dbg", [BPC, S, D], F32, "ExternalOutput")
        kv_dbg = dram("kv_dbg", [BPC, 128, 512], F32, "ExternalOutput")
        ctx_dbg = dram("ctx_dbg", [BPC, S, D], F32, "ExternalOutput")
    tq_scr = dram("tq_scr", [BPC, S, FF], BF16, "Internal")
    wq_scr = {"g": dram("wgq_scr", [D, FF], BF16, "Internal"),
              "u": dram("wuq_scr", [D, FF], BF16, "Internal"),
              "d": dram("wdq_scr", [FF, D], BF16, "Internal")}

    sb = lambda stk, n, sh, dt: stk.enter_context(nc.sbuf_tensor(n, sh, dt))
    pst = lambda stk, n, sh, dt=F32: stk.enter_context(nc.psum_tensor(n, sh, dt))

    id_t = sb(stack, "id_t", [128, 128], BF16)
    P.op("sync", lambda e: e.dma_start(out=id_t[:], in_=ident), inc=[ld], dma=True)
    stat = sb(stack, "stat", [128, 32], F32)
    w_scales = sb(stack, "w_scales", [128, 4], F32)
    stq = sb(stack, "stq", [128, 2 * NT], F32)
    junk = sb(stack, "junk", [128, 2 * D], BF16)
    eps5 = sb(stack, "eps5", [128, 1], F32)
    eps12 = sb(stack, "eps12", [128, 1], F32)

    P.op("vector", lambda e: e.memset(eps5[:], 1e-5), inc=[dve])
    P.op("vector", lambda e: e.memset(eps12[:], 1e-12), inc=[dve])

    big_ps = pst(stack, "big_ps", [128, 2048])
    o_ps = pst(stack, "o_ps", [128, 1024])
    tr_ps = pst(stack, "tr_ps", [128, 2048], BF16)

    def emit_transposes(src_bf, n_chunks, dst, dst_fn, extra_first=()):
        for m in range(n_chunks):
            g4 = m // 4
            bank = g4 % 2
            aft = [pe]
            if m % 4 == 0:
                aft = [pe, act, dve, gp] + list(extra_first)
            P.op("tensor", lambda e, m=m, bank=bank: e.transpose(
                tr_ps[:, bank * 1024 + (m % 4) * 128: bank * 1024 + (m % 4) * 128 + 128],
                src_bf[:, m * 128:(m + 1) * 128], id_t[:]), after=aft, inc=[pe])
            if m % 4 == 3:
                P.op("scalar", lambda e, g4=g4, bank=bank: e.copy(
                    dst_fn(g4),
                    tr_ps[:, bank * 1024: bank * 1024 + 512]
                    .rearrange("p (a b) -> p a b", a=4)),
                    after=[pe, act], inc=[act])

    def emit_ln(in_ap, out_ap, width, eps, dep=()):
        eps_ap = (eps5 if eps > 1e-9 else eps12)[:]
        inv = 1.0 / width
        P.op("vector", lambda e: e.tensor_scalar(
            out=junk[:, 0:width], in0=in_ap, scalar1=inv, scalar2=0.0, op0=ALU.mult,
            op1=ALU.add, accum_out=stat[:, 0:1]), after=(dve, act) + tuple(dep), inc=[dve])
        sq = float(np.sqrt(inv))
        P.op("scalar", lambda e: e.activation(
            out=junk[:, D:D + width], in_=in_ap, func=AF.Square, bias=0.0, scale=sq,
            accum_out=stat[:, 1:2]), after=(act, dve) + tuple(dep), inc=[act])
        P.op("vector", lambda e: e.scalar_tensor_tensor(
            out=stat[:, 3:4], in0=stat[:, 0:1], scalar=-1.0, in1=stat[:, 0:1],
            op0=ALU.mult, op1=ALU.mult), after=(dve, act), inc=[dve])
        P.op("vector", lambda e: e.tensor_tensor(
            out=stat[:, 1:2], in0=stat[:, 1:2], in1=stat[:, 3:4], op=ALU.add),
            after=(dve,), inc=[dve])
        P.op("scalar", lambda e: e.activation(
            out=stat[:, 2:3], in_=stat[:, 1:2], func=AF.Ln, bias=eps_ap, scale=1.0),
            after=(act, dve), inc=[act])
        P.op("scalar", lambda e: e.activation(
            out=stat[:, 2:3], in_=stat[:, 2:3], func=AF.Exp, bias=0.0, scale=-0.5),
            after=(act,), inc=[act])
        return P.op("vector", lambda e: e.tensor_scalar(
            out=out_ap, in0=in_ap, scalar1=stat[:, 0:1], scalar2=stat[:, 2:3],
            op0=ALU.subtract, op1=ALU.mult), after=(dve, act, pe, gp), inc=[dve])

    def conv_w(dst_b, src, wstage, rows, cols):
        for c in range(rows // 128):
            P.op("sync", lambda e, c=c: e.dma_start(
                out=wstage[:, 0:cols], in_=src[c * 128:(c + 1) * 128, :]),
                after=[dve], inc=[ld], dma=True)
            P.op("vector", lambda e, c=c: e.tensor_copy(dst_b[:, c, :], wstage[:, 0:cols]),
                 after=[ld, pe, act], inc=[dve])

    p0 = ExitStack()
    wstage0 = sb(p0, "wstage0", [128, FF], F32)
    qi8 = sb(p0, "qi8", [128, FF], I8)
    qbf = sb(p0, "qbf", [128, FF], BF16)
    ones_col = sb(p0, "ones_col", [128, 1], F32)
    ones_row = sb(p0, "ones_row", [1, 128], F32)
    P.op("vector", lambda e: e.memset(ones_col[:], 1.0), inc=[dve])
    P.op("vector", lambda e: e.memset(ones_row[:], 1.0), inc=[dve])

    def quant_weight(src, rows, cols, scr, scol):
        nch = rows // 128
        acc = stat[:, 8:9]
        red = stat[:, 9:10]
        P.op("vector", lambda e: e.memset(acc, 0.0), after=[dve], inc=[dve])
        for c in range(nch):
            P.op("sync", lambda e, c=c: e.dma_start(
                out=wstage0[:, 0:cols], in_=src[c * 128:(c + 1) * 128, :]),
                after=[dve], inc=[ld], dma=True)
            P.op("vector", lambda e: e.reduce_sum(out=red, in_=wstage0[:, 0:cols],
                                                  axis=AX.X, apply_absolute_value=True),
                 after=[ld, dve], inc=[dve])
            P.op("vector", lambda e: e.tensor_tensor(out=acc, in0=acc, in1=red, op=ALU.add),
                 after=[dve], inc=[dve])
        P.op("tensor", lambda e: e.matmul(big_ps[0:1, 0:1], acc, ones_col[:],
                                          start=True, stop=True),
             after=[dve, pe, act], inc=[pe])
        P.op("vector", lambda e: e.tensor_scalar(
            out=stat[0:1, 10:11], in0=big_ps[0:1, 0:1], scalar1=1.0 / (rows * cols),
            scalar2=1e-5, op0=ALU.mult, op1=ALU.max), after=[pe, dve], inc=[dve])
        P.op("tensor", lambda e: e.matmul(big_ps[:, 1:2], ones_row[:], stat[0:1, 10:11],
                                          start=True, stop=True),
             after=[dve, pe], inc=[pe])
        P.op("vector", lambda e, scol=scol: e.tensor_copy(w_scales[:, scol:scol + 1],
                                                          big_ps[:, 1:2]),
             after=[pe, dve], inc=[dve])
        P.op("vector", lambda e: e.reciprocal(out=stat[:, 11:12],
                                              in_=w_scales[:, scol:scol + 1]),
             after=[dve], inc=[dve])
        for c in range(nch):
            P.op("sync", lambda e, c=c: e.dma_start(
                out=wstage0[:, 0:cols], in_=src[c * 128:(c + 1) * 128, :]),
                after=[dve, st], inc=[ld], dma=True)
            P.op("vector", lambda e: e.tensor_scalar(
                out=qi8[:, 0:cols], in0=wstage0[:, 0:cols], scalar1=stat[:, 11:12],
                scalar2=None, op0=ALU.mult), after=[ld, dve, st], inc=[dve])
            P.op("vector", lambda e: e.tensor_copy(qbf[:, 0:cols], qi8[:, 0:cols]),
                 after=[dve], inc=[dve])
            P.op("vector", lambda e: e.tensor_scalar(
                out=qbf[:, 0:cols], in0=qbf[:, 0:cols], scalar1=1.0, scalar2=-1.0,
                op0=ALU.min, op1=ALU.max), after=[dve], inc=[dve])
            P.op("sync", lambda e, c=c: e.dma_start(
                out=scr[c * 128:(c + 1) * 128, :], in_=qbf[:, 0:cols]),
                after=[dve], inc=[st], dma=True)

    quant_weight(wgT, D, FF, wq_scr["g"], 0)
    quant_weight(wuT, D, FF, wq_scr["u"], 1)
    quant_weight(wdT, FF, D, wq_scr["d"], 2)
    P.join()
    p0.close()

    def emit_attn_a(dst):
        p1 = ExitStack()
        wstage = junk[:].bitcast(F32)
        w0_b = sb(p1, "w0_b", [128, 8, D], BF16)
        kWT_b = sb(p1, "kWT_b", [128, 8, D], BF16)
        vWT_b = sb(p1, "vWT_b", [128, 8, D], BF16)
        qbT_t = sb(p1, "qbT_t", [128, 8], F32)
        kbT_t = sb(p1, "kbT_t", [128, 8], F32)
        skT_t = sb(p1, "skT_t", [128, 8, 8], F32)
        vb_t = sb(p1, "vb_t", [128, D], BF16)
        ob_t = sb(p1, "ob_t", [128, D], F32)
        QT = sb(p1, "QT", [128, 8, S], BF16)
        KT = sb(p1, "KT", [128, 8, S], BF16)
        Vt = sb(p1, "Vt", [128, NT, D], BF16)
        xt_f = [sb(p1, f"xt_f{i}", [128, D], F32) for i in range(2)]
        xhat_b = sb(p1, "xhat_b", [128, D], BF16)
        PTb = sb(p1, "PTb", [128, NT, 256], BF16)
        xTst = PTb[:].rearrange("p a b -> p (a b)").rearrange("p (a b) -> p a b", a=8)
        e_sb = sb(p1, "e_sb", [128, S], F32)
        m_sb = sb(p1, "m_sb", [128, S], F32)
        e2_sb = m_sb
        p_sb = [sb(p1, f"p_sb{i}", [128, S], BF16) for i in range(2)]
        tb16 = sb(p1, "tb16", [128, 16], F32)
        ctxT = sb(p1, "ctxT", [128, 8, 256], BF16)
        o_f = e_sb

        conv_w(kWT_b, kWT, wstage, D, D)
        conv_w(vWT_b, vWT, wstage, D, D)
        P.op("sync", lambda e: e.dma_start(out=wstage[:, 0:D], in_=vb_bc),
             after=[dve], inc=[ld], dma=True)
        P.op("vector", lambda e: e.tensor_copy(vb_t[:], wstage[:, 0:D]),
             after=[ld], inc=[dve])
        P.op("sync", lambda e: e.dma_start(out=qbT_t[:], in_=qbT), inc=[ld], dma=True)
        P.op("sync", lambda e: e.dma_start(out=kbT_t[:], in_=kbT), inc=[ld], dma=True)
        P.op("sync", lambda e: e.dma_start(out=skT_t[:], in_=skT), inc=[ld], dma=True)
        P.op("sync", lambda e: e.dma_start(out=ob_t[:], in_=ob_bc), inc=[ld], dma=True)
        P.op("vector", lambda e: e.tensor_scalar(out=qbT_t[:], in0=qbT_t[:], scalar1=0.125,
                                                 scalar2=None, op0=ALU.mult),
             after=[ld, dve], inc=[dve])

        for b in range(BPC):
            conv_w(w0_b, qWT, wstage, D, D)
            for sst in range(4):
                for sub in range(4):
                    tt = sst * 4 + sub
                    xt = xt_f[tt % 2]
                    P.op("sync", lambda e, b=b, tt=tt, xt=xt: e.dma_start(
                        out=xt[:], in_=x_in[b, tt * 128:(tt + 1) * 128, :]),
                        after=[dve, act], inc=[ld], dma=True)
                    emit_ln(xt[:], xhat_b[:], D, 1e-5, dep=(ld,))
                    emit_transposes(
                        xhat_b, 8, None,
                        lambda g4, sub=sub: xTst[:, g4 * 4:(g4 + 1) * 4,
                                                 sub * 128:(sub + 1) * 128])
                for WTb, bias, scale, DST in ((w0_b, qbT_t, 0.125, QT),
                                              (kWT_b, kbT_t, 1.0, KT)):
                    for m in range(8):
                        ps = big_ps[:, (m % 2) * 512:(m % 2) * 512 + 512]
                        for k in range(8):
                            aft = [act, pe, dve] if k == 0 else [pe]
                            P.op("tensor", lambda e, m=m, k=k, WTb=WTb, ps=ps: e.matmul(
                                ps, WTb[:, k, m * 128:(m + 1) * 128], xTst[:, k, :],
                                start=(k == 0), stop=(k == 7)), after=aft, inc=[pe])
                        P.op("scalar", lambda e, m=m, ps=ps, DST=DST, bias=bias,
                             scale=scale, sst=sst: e.activation(
                                 out=DST[:, m, sst * 512:(sst + 1) * 512], in_=ps,
                                 func=AF.Identity, bias=bias[:, m:m + 1], scale=scale),
                             after=[pe, act], inc=[act])
                for sub in range(4):
                    chunk = sst * 4 + sub
                    for n in range(2):
                        ps = big_ps[:, 1024 + n * 512:1024 + n * 512 + 512]
                        for k in range(8):
                            aft = [dve, pe, act] if k == 0 else [pe]
                            P.op("tensor", lambda e, sub=sub, n=n, k=k, ps=ps: e.matmul(
                                ps, xTst[:, k, sub * 128:(sub + 1) * 128],
                                vWT_b[:, k, n * 512:(n + 1) * 512],
                                start=(k == 0), stop=(k == 7)), after=aft, inc=[pe])
                        P.op("vector", lambda e, chunk=chunk, n=n, ps=ps: e.tensor_tensor(
                            out=Vt[:, chunk, n * 512:(n + 1) * 512], in0=ps,
                            in1=vb_t[:, n * 512:(n + 1) * 512], op=ALU.add),
                            after=[pe, dve], inc=[dve])
            P.op("vector", lambda e: e.tensor_copy(
                KT[:, :, 0:8], skT_t[:]), after=[dve, act, pe], inc=[dve])

            conv_w(w0_b, oWT, wstage, D, D)
            oWT_b = w0_b
            pv_tok = None
            for g in range(8):
                for h in range(H):
                    hrow = (h % 2) * 64
                    hm = h // 2
                    for qt in range(2):
                        qg = g * 2 + qt
                        for n in range(4):
                            aft = [act, pe, dve] if n == 0 else [pe]
                            P.op("tensor", lambda e, hrow=hrow, hm=hm, qg=qg, n=n: e.matmul(
                                big_ps[:, n * 512:(n + 1) * 512],
                                QT[hrow:hrow + 64, hm, qg * 128:(qg + 1) * 128],
                                KT[hrow:hrow + 64, hm, n * 512:(n + 1) * 512],
                                start=True, stop=True), after=aft, inc=[pe])
                        P.op("scalar", lambda e: e.activation(
                            out=e_sb[:], in_=big_ps[:], func=AF.Exp, bias=0.0, scale=1.0),
                            after=[pe, act, dve, gp], inc=[act])
                        P.op("vector", lambda e: e.max(out=tb16[:, 0:8], in_=e_sb[:]),
                             after=[act, dve], inc=[dve])
                        P.op("vector", lambda e: e.match_replace(
                            out=e2_sb[:], in_to_replace=tb16[:, 0:8], in_values=e_sb[:],
                            imm_value=0.0), after=[dve, gp], inc=[dve])
                        P.op("vector", lambda e: e.max(out=tb16[:, 8:16], in_=e2_sb[:]),
                             after=[dve], inc=[dve])
                        P.op("vector", lambda e: e.reduce_sum(
                            out=stat[:, 4:5], in_=tb16[:], axis=AX.X),
                            after=[dve], inc=[dve])
                        P.op("vector", lambda e: e.reciprocal(out=stat[:, 5:6],
                                                              in_=stat[:, 4:5]),
                             after=[dve], inc=[dve])
                        P.op("vector", lambda e: e.tensor_scalar(
                            out=m_sb[:], in0=e_sb[:], scalar1=tb16[:, 15:16],
                            scalar2=stat[:, 5:6], op0=ALU.is_ge, op1=ALU.mult),
                            after=[dve, gp], inc=[dve])
                        pbuf = p_sb[qt % 2]
                        P.op("gpsimd", lambda e, pbuf=pbuf: e.tensor_tensor(
                            out=pbuf[:], in0=m_sb[:], in1=e_sb[:], op=ALU.mult),
                            after=[dve, gp, pe], inc=[gp])
                        extra = [pv_tok[pe]] if pv_tok is not None else []
                        emit_transposes(
                            pbuf, 16, None,
                            lambda g4, qt=qt: PTb[:, g4 * 4:(g4 + 1) * 4,
                                                  qt * 128:(qt + 1) * 128],
                            extra_first=extra)
                    for cc in range(NT):
                        aft = [act, pe] if cc == 0 else [pe]
                        P.op("tensor", lambda e, cc=cc, hrow=hrow, h=h: e.matmul(
                            o_ps[hrow:hrow + 64, 0:256], Vt[:, cc, h * 64:(h + 1) * 64],
                            PTb[:, cc, :], start=(cc == 0), stop=(cc == 15)),
                            after=aft, inc=[pe])
                    pv_tok = P.op("tensor", lambda e: e.nop(), after=[pe], inc=[pe])
                    if h % 2 == 1:
                        P.op("scalar", lambda e, hm=hm: e.copy(ctxT[:, hm, :],
                                                               o_ps[:, 0:256]),
                             after=[pe, act], inc=[act])
                for qt in range(2):
                    qg = g * 2 + qt
                    for n in range(2):
                        for k in range(8):
                            aft = [act, pe, dve] if k == 0 else [pe]
                            P.op("tensor", lambda e, n=n, k=k, qt=qt: e.matmul(
                                o_ps[:, n * 512:(n + 1) * 512],
                                ctxT[:, k, qt * 128:(qt + 1) * 128],
                                oWT_b[:, k, n * 512:(n + 1) * 512],
                                start=(k == 0), stop=(k == 7)), after=aft, inc=[pe])
                    xt = xt_f[qt % 2]
                    P.op("sync", lambda e, b=b, qg=qg, xt=xt: e.dma_start(
                        out=xt[:], in_=x_in[b, qg * 128:(qg + 1) * 128, :]),
                        after=[gp, dve], inc=[ld], dma=True)
                    P.op("gpsimd", lambda e, xt=xt: e.tensor_tensor(
                        out=xt[:], in0=xt[:], in1=ob_t[:], op=ALU.add),
                        after=[ld, gp], inc=[gp])
                    P.op("scalar", lambda e: e.copy(o_f[:, 0:D], o_ps[:]),
                         after=[pe, act, dve, gp], inc=[act])
                    P.op("vector", lambda e, xt=xt: e.tensor_tensor(
                        out=xt[:], in0=xt[:], in1=o_f[:, 0:D], op=ALU.add),
                        after=[act, gp, dve], inc=[dve])
                    P.op("sync", lambda e, b=b, qg=qg, xt=xt: e.dma_start(
                        out=dst[b, qg * 128:(qg + 1) * 128, :], in_=xt[:]),
                        after=[dve], inc=[st], dma=True)
        P.join()
        p1.close()

    emit_attn_a(out if upto == 1 else xres)
    if upto == 1:
        P.join()
        P.emit()
        return nc, stack

    def emit_ffn(src, dst, tag):
        f1 = ExitStack()
        wgq_b = sb(f1, f"wgq_b{tag}", [128, 8, FF], BF16)
        wuq_b = sb(f1, f"wuq_b{tag}", [128, 8, FF], BF16)
        xt_f = [sb(f1, f"fx{tag}{i}", [128, D], F32) for i in range(2)]
        xhat_f = sb(f1, f"xhat_f{tag}", [128, D], F32)
        xq_i8 = sb(f1, f"xq_i8{tag}", [128, D], I8)
        xq_bf = sb(f1, f"xq_bf{tag}", [128, D], BF16)
        xqT = sb(f1, f"xqT{tag}", [128, 8, 128], BF16)
        g_f = sb(f1, f"g_f{tag}", [128, FF], F32)
        u_f = sb(f1, f"u_f{tag}", [128, FF], F32)
        tq_i8 = sb(f1, f"tq_i8{tag}", [128, FF], I8)
        tq_bf = [sb(f1, f"tq_bf{tag}{i}", [128, FF], BF16) for i in range(2)]
        for c in range(8):
            P.op("sync", lambda e, c=c: e.dma_start(
                out=wgq_b[:, c, :], in_=wq_scr["g"][c * 128:(c + 1) * 128, :]),
                inc=[ld], dma=True)
            P.op("sync", lambda e, c=c: e.dma_start(
                out=wuq_b[:, c, :], in_=wq_scr["u"][c * 128:(c + 1) * 128, :]),
                inc=[ld], dma=True)
        ld_tok = ld.tok()
        for b in range(BPC):
            for tt in range(NT):
                xt = xt_f[tt % 2]
                P.op("sync", lambda e, b=b, tt=tt, xt=xt: e.dma_start(
                    out=xt[:], in_=src[b, tt * 128:(tt + 1) * 128, :]),
                    after=[dve, act], inc=[ld], dma=True)
                emit_ln(xt[:], xhat_f[:], D, 1e-5, dep=(ld,))
                P.op("vector", lambda e: e.reduce_max(
                    out=stat[:, 6:7], in_=xhat_f[:], axis=AX.X,
                    apply_absolute_value=True), after=[dve], inc=[dve])
                P.op("vector", lambda e: e.tensor_scalar(
                    out=stat[:, 6:7], in0=stat[:, 6:7], scalar1=1e-5, scalar2=None,
                    op0=ALU.max), after=[dve], inc=[dve])
                P.op("vector", lambda e: e.reciprocal(out=stat[:, 7:8], in_=stat[:, 6:7]),
                     after=[dve], inc=[dve])
                P.op("vector", lambda e: e.tensor_scalar(
                    out=stat[:, 7:8], in0=stat[:, 7:8], scalar1=127.0, scalar2=None,
                    op0=ALU.mult), after=[dve], inc=[dve])
                P.op("vector", lambda e: e.tensor_scalar(
                    out=xq_i8[:], in0=xhat_f[:], scalar1=stat[:, 7:8], scalar2=None,
                    op0=ALU.mult), after=[dve], inc=[dve])
                P.op("vector", lambda e: e.tensor_copy(xq_bf[:], xq_i8[:]),
                     after=[dve, pe], inc=[dve])
                P.op("vector", lambda e: e.tensor_scalar(
                    out=stat[:, 8:9], in0=stat[:, 6:7], scalar1=1.0 / 127.0, scalar2=None,
                    op0=ALU.mult), after=[dve], inc=[dve])
                P.op("vector", lambda e: e.tensor_tensor(
                    out=stat[:, 12:13], in0=stat[:, 8:9], in1=w_scales[:, 0:1],
                    op=ALU.mult), after=[dve, act], inc=[dve])
                P.op("vector", lambda e: e.tensor_tensor(
                    out=stat[:, 13:14], in0=stat[:, 8:9], in1=w_scales[:, 1:2],
                    op=ALU.mult), after=[dve], inc=[dve])
                emit_transposes(xq_bf, 8, None,
                                lambda g4: xqT[:, g4 * 4:(g4 + 1) * 4, :],
                                extra_first=[ld_tok])
                for which, wq_b, dstf, scol, fn in (
                        (0, wgq_b, g_f, 12, AF.Silu), (1, wuq_b, u_f, 13, AF.Copy)):
                    for n in range(8):
                        ps = big_ps[:, (n % 4) * 512:(n % 4) * 512 + 512]
                        for k in range(8):
                            aft = [act, pe] if k == 0 else [pe]
                            P.op("tensor", lambda e, n=n, k=k, wq_b=wq_b, ps=ps: e.matmul(
                                ps, xqT[:, k, :], wq_b[:, k, n * 512:(n + 1) * 512],
                                start=(k == 0), stop=(k == 7)), after=aft, inc=[pe])
                        P.op("scalar", lambda e, n=n, ps=ps, dstf=dstf, scol=scol, fn=fn:
                             e.activation(out=dstf[:, n * 512:(n + 1) * 512], in_=ps,
                                          func=fn, bias=0.0, scale=stat[:, scol:scol + 1]),
                             after=[pe, act, dve], inc=[act])
                P.op("vector", lambda e: e.tensor_tensor(out=g_f[:], in0=g_f[:], in1=u_f[:],
                                                         op=ALU.mult),
                     after=[act, dve], inc=[dve])
                P.op("vector", lambda e: e.reduce_max(
                    out=stat[:, 6:7], in_=g_f[:], axis=AX.X, apply_absolute_value=True),
                    after=[dve], inc=[dve])
                P.op("vector", lambda e: e.tensor_scalar(
                    out=stat[:, 6:7], in0=stat[:, 6:7], scalar1=1e-5, scalar2=None,
                    op0=ALU.max), after=[dve], inc=[dve])
                P.op("vector", lambda e, b=b, tt=tt: e.tensor_scalar(
                    out=stq[:, b * NT + tt:b * NT + tt + 1], in0=stat[:, 6:7],
                    scalar1=1.0 / 127.0, scalar2=None, op0=ALU.mult),
                    after=[dve], inc=[dve])
                P.op("vector", lambda e: e.reciprocal(out=stat[:, 7:8], in_=stat[:, 6:7]),
                     after=[dve], inc=[dve])
                P.op("vector", lambda e: e.tensor_scalar(
                    out=stat[:, 7:8], in0=stat[:, 7:8], scalar1=127.0, scalar2=None,
                    op0=ALU.mult), after=[dve], inc=[dve])
                P.op("vector", lambda e: e.tensor_scalar(
                    out=tq_i8[:], in0=g_f[:], scalar1=stat[:, 7:8], scalar2=None,
                    op0=ALU.mult), after=[dve], inc=[dve])
                tqb = tq_bf[tt % 2]
                P.op("vector", lambda e, tqb=tqb: e.tensor_copy(tqb[:], tq_i8[:]),
                     after=[dve, st], inc=[dve])
                P.op("sync", lambda e, b=b, tt=tt, tqb=tqb: e.dma_start(
                    out=tq_scr[b, tt * 128:(tt + 1) * 128, :], in_=tqb[:]),
                    after=[dve], inc=[st], dma=True)
        P.join()
        f1.close()
        f2 = ExitStack()
        wdq_b = sb(f2, f"wdq_b{tag}", [128, 32, D], BF16)
        tqi = [sb(f2, f"tqi{tag}{i}", [128, FF], BF16) for i in range(2)]
        tqT = sb(f2, f"tqT{tag}", [128, 32, 128], BF16)
        xt_f2 = [sb(f2, f"fx2{tag}{i}", [128, D], F32) for i in range(2)]
        d_f = sb(f2, f"d_f{tag}", [128, D], F32)
        for c in range(32):
            P.op("sync", lambda e, c=c: e.dma_start(
                out=wdq_b[:, c, :], in_=wq_scr["d"][c * 128:(c + 1) * 128, :]),
                inc=[ld], dma=True)
        ld_tok2 = ld.tok()
        for b in range(BPC):
            for tt in range(NT):
                tqb = tqi[tt % 2]
                P.op("sync", lambda e, b=b, tt=tt, tqb=tqb: e.dma_start(
                    out=tqb[:], in_=tq_scr[b, tt * 128:(tt + 1) * 128, :]),
                    after=[pe], inc=[ld], dma=True)
                emit_transposes(tqb, 32, None,
                                lambda g4: tqT[:, g4 * 4:(g4 + 1) * 4, :],
                                extra_first=[ld, ld_tok2])
                for n in range(2):
                    for k in range(32):
                        aft = [act, pe, dve] if k == 0 else [pe]
                        P.op("tensor", lambda e, n=n, k=k: e.matmul(
                            o_ps[:, n * 512:(n + 1) * 512], tqT[:, k, :],
                            wdq_b[:, k, n * 512:(n + 1) * 512],
                            start=(k == 0), stop=(k == 31)), after=aft, inc=[pe])
                P.op("vector", lambda e, b=b, tt=tt: e.tensor_tensor(
                    out=stat[:, 14:15], in0=stq[:, b * NT + tt:b * NT + tt + 1],
                    in1=w_scales[:, 2:3], op=ALU.mult), after=[dve, act], inc=[dve])
                P.op("scalar", lambda e: e.activation(
                    out=d_f[:], in_=o_ps[:], func=AF.Copy, bias=0.0,
                    scale=stat[:, 14:15]), after=[pe, act, dve], inc=[act])
                xt = xt_f2[tt % 2]
                P.op("sync", lambda e, b=b, tt=tt, xt=xt: e.dma_start(
                    out=xt[:], in_=src[b, tt * 128:(tt + 1) * 128, :]),
                    after=[dve], inc=[ld], dma=True)
                P.op("vector", lambda e, xt=xt: e.tensor_tensor(
                    out=xt[:], in0=xt[:], in1=d_f[:], op=ALU.add),
                    after=[ld, act, dve], inc=[dve])
                P.op("sync", lambda e, b=b, tt=tt, xt=xt: e.dma_start(
                    out=dst[b, tt * 128:(tt + 1) * 128, :], in_=xt[:]),
                    after=[dve], inc=[st], dma=True)
        P.join()
        f2.close()

    emit_ffn(xres, out if upto == 2 else xres, 0)
    if upto == 2:
        P.join()
        P.emit()
        return nc, stack

    def emit_attn_b(dst):
        b3 = ExitStack()
        bias_t = {}
        for nm in ("bq_bc", "bk_bc", "bv_bc", "bo_bc", "b2_bc", "g1_bc", "be1_bc",
                   "g2_bc", "be2_bc"):
            bias_t[nm] = sb(b3, nm + "_t", [128, D], F32)
            P.op("sync", lambda e, nm=nm: e.dma_start(out=bias_t[nm][:], in_=bia[nm]),
                 inc=[ld], dma=True)
        xt_f = [sb(b3, f"bx{i}", [128, D], F32) for i in range(2)]
        ao_keep = sb(b3, "ao_keep", [128, NT, D], BF16)
        xh_bf = sb(b3, "xh_bf", [128, D], BF16)
        xhT = sb(b3, "xhT", [128, 8, 128], BF16)
        o_f = sb(b3, "o_f3", [128, D], F32)
        ao_f = sb(b3, "ao_f", [128, D], F32)

        kv_ps = big_ps[:, 1536:2048]

        def emit_elunorm(src_f, dst_bf, tmp_f, eq_f, sq_f, s16):
            P.op("vector", lambda e, src_f=src_f: e.tensor_scalar(
                out=tmp_f[:], in0=src_f[:], scalar1=0.0, scalar2=None, op0=ALU.min),
                after=[dve, act], inc=[dve])
            P.op("scalar", lambda e: e.activation(out=eq_f[:], in_=tmp_f[:], func=AF.Exp,
                                                  bias=0.0, scale=1.0),
                 after=[act, dve], inc=[act])
            P.op("vector", lambda e, src_f=src_f: e.scalar_tensor_tensor(
                out=eq_f[:], in0=src_f[:], scalar=0.0, in1=eq_f[:], op0=ALU.max,
                op1=ALU.add), after=[dve, act], inc=[dve])
            P.op("vector", lambda e: e.tensor_scalar(
                out=eq_f[:], in0=eq_f[:], scalar1=1.0, scalar2=None, op0=ALU.subtract),
                after=[dve], inc=[dve])
            P.op("scalar", lambda e: e.activation(out=sq_f[:], in_=eq_f[:], func=AF.Square,
                                                  bias=0.0, scale=1.0),
                 after=[act, dve], inc=[act])
            P.op("vector", lambda e: e.tensor_reduce(
                out=s16[:], in_=sq_f[:].rearrange("p (h d) -> p h d", h=16),
                axis=AX.X, op=ALU.add), after=[dve, act], inc=[dve])
            P.op("scalar", lambda e: e.activation(out=s16[:], in_=s16[:], func=AF.Ln,
                                                  bias=0.0, scale=1.0),
                 after=[act, dve], inc=[act])
            P.op("scalar", lambda e: e.activation(out=s16[:], in_=s16[:], func=AF.Exp,
                                                  bias=0.0, scale=-0.5),
                 after=[act], inc=[act])
            return P.op("vector", lambda e, dst_bf=dst_bf: e.tensor_tensor(
                out=dst_bf[:].rearrange("p (h d) -> p h d", h=16),
                in0=eq_f[:].rearrange("p (h d) -> p h d", h=16),
                in1=s16[:].unsqueeze(2).to_broadcast([128, 16, 64]), op=ALU.mult),
                after=[dve, act, pe], inc=[dve])

        for b in range(BPC):
            b12 = ExitStack()
            wstage = sb(b12, f"wstage3{b}", [128, D], F32)
            w0b_b = sb(b12, f"w0b_b{b}", [128, 8, D], BF16)
            Wk_b = sb(b12, f"Wk_b{b}", [128, 8, D], BF16)
            Wv_b = sb(b12, f"Wv_b{b}", [128, 8, D], BF16)
            qn_keep = sb(b12, f"qn_keep{b}", [128, NT, D], BF16)
            q_f = sb(b12, f"q_f{b}", [128, D], F32)
            k_f = sb(b12, f"k_f{b}", [128, D], F32)
            v_bf = sb(b12, f"v_bf{b}", [128, D], BF16)
            xh_f = sb(b12, f"xh_f{b}", [128, D], F32)
            tmp_f = sb(b12, f"tmp_f{b}", [128, D], F32)
            eq_f = sb(b12, f"eq_f{b}", [128, D], F32)
            sq_f = sb(b12, f"sq_f{b}", [128, D], F32)
            s16 = sb(b12, f"s16{b}", [128, 16], F32)
            n_bf = sb(b12, f"n_bf{b}", [128, D], BF16)
            kv_bf = sb(b12, f"kv_bf{b}", [128, 512], BF16)
            ctxbT = sb(b12, f"ctxbT{b}", [128, 8, 128], BF16)
            elun = lambda sf, db: emit_elunorm(sf, db, tmp_f, eq_f, sq_f, s16)
            conv_w(Wk_b, bWk, wstage, D, D)
            conv_w(Wv_b, bWv, wstage, D, D)
            conv_w(w0b_b, bWq, wstage, D, D)
            for tt in range(NT):
                xt = xt_f[tt % 2]
                P.op("sync", lambda e, b=b, tt=tt, xt=xt: e.dma_start(
                    out=xt[:], in_=xres[b, tt * 128:(tt + 1) * 128, :]),
                    after=[dve, act], inc=[ld], dma=True)
                emit_ln(xt[:], xh_bf[:], D, 1e-5, dep=(ld,))
                emit_transposes(xh_bf, 8, None,
                                lambda g4: xhT[:, g4 * 4:(g4 + 1) * 4, :])
                for W_b, bias_nm, dstt in ((w0b_b, "bq_bc", q_f), (Wk_b, "bk_bc", k_f),
                                           (Wv_b, "bv_bc", v_bf)):
                    for n in range(2):
                        ps = big_ps[:, n * 512:n * 512 + 512]
                        for k in range(8):
                            aft = [act, pe, dve, gp] if k == 0 else [pe]
                            P.op("tensor", lambda e, n=n, k=k, W_b=W_b, ps=ps: e.matmul(
                                ps, xhT[:, k, :], W_b[:, k, n * 512:(n + 1) * 512],
                                start=(k == 0), stop=(k == 7)), after=aft, inc=[pe])
                    P.op("vector", lambda e, dstt=dstt, bias_nm=bias_nm: e.tensor_tensor(
                        out=dstt[:], in0=big_ps[:, 0:1024], in1=bias_t[bias_nm][:],
                        op=ALU.add), after=[pe, dve], inc=[dve])
                elun(q_f, n_bf)
                P.op("vector", lambda e, tt=tt: e.tensor_copy(qn_keep[:, tt, :], n_bf[:]),
                     after=[dve], inc=[dve])
                if dbg:
                    P.op("vector", lambda e: e.tensor_copy(tmp_f[:], n_bf[:]),
                         after=[dve], inc=[dve])
                    P.op("sync", lambda e, b=b, tt=tt: e.dma_start(
                        out=qn_dbg[b, tt * 128:(tt + 1) * 128, :], in_=tmp_f[:]),
                        after=[dve], inc=[st], dma=True)
                elun(k_f, n_bf)
                for h in range(H):
                    hrow = (h % 2) * 64
                    hm = h // 2
                    aft = [dve, pe] if h == 0 else [pe]
                    P.op("tensor", lambda e, h=h, hrow=hrow, hm=hm, tt=tt: e.matmul(
                        kv_ps[hrow:hrow + 64, hm * 64:(hm + 1) * 64],
                        n_bf[:, h * 64:(h + 1) * 64], v_bf[:, h * 64:(h + 1) * 64],
                        start=(tt == 0 and h < 2), stop=(tt == NT - 1)),
                        after=aft, inc=[pe])
            P.op("scalar", lambda e: e.activation(out=kv_bf[:], in_=kv_ps, func=AF.Copy,
                                                  bias=0.0, scale=0.125),
                 after=[pe, act], inc=[act])
            if dbg:
                P.op("vector", lambda e: e.tensor_copy(xh_f[:, 0:512], kv_bf[:]),
                     after=[act, dve], inc=[dve])
                P.op("sync", lambda e, b=b: e.dma_start(
                    out=kv_dbg[b], in_=xh_f[:, 0:512]), after=[dve], inc=[st], dma=True)
            conv_w(w0b_b, bWo, wstage, D, D)
            Wo_b = w0b_b
            for tt in range(NT):
                xt = xt_f[tt % 2]
                P.op("sync", lambda e, b=b, tt=tt, xt=xt: e.dma_start(
                    out=xt[:], in_=xres[b, tt * 128:(tt + 1) * 128, :]),
                    after=[dve, act], inc=[ld], dma=True)
                emit_ln(xt[:], xh_f[:], D, 1e-5, dep=(ld,))
                P.op("vector", lambda e, tt=tt: e.tensor_copy(xh_bf[:], qn_keep[:, tt, :]),
                     after=[dve, pe], inc=[dve])
                emit_transposes(xh_bf, 8, None,
                                lambda g4: xhT[:, g4 * 4:(g4 + 1) * 4, :])
                for h in range(H):
                    hrow = (h % 2) * 64
                    hm = h // 2
                    aft = [act, pe] if h == 0 else [pe]
                    P.op("tensor", lambda e, hrow=hrow, hm=hm, h=h: e.matmul(
                        big_ps[hrow:hrow + 64, hm * 128:hm * 128 + 128],
                        kv_bf[hrow:hrow + 64, hm * 64:(hm + 1) * 64],
                        xhT[hrow:hrow + 64, h // 2, :],
                        start=True, stop=True), after=aft, inc=[pe])
                P.op("scalar", lambda e: e.copy(
                    ctxbT[:].rearrange("p a b -> p (a b)"), big_ps[:, 0:1024]),
                    after=[pe, act], inc=[act])
                for n in range(2):
                    for k in range(8):
                        aft = [act, pe, dve] if k == 0 else [pe]
                        P.op("tensor", lambda e, n=n, k=k: e.matmul(
                            o_ps[:, n * 512:(n + 1) * 512], ctxbT[:, k, :],
                            Wo_b[:, k, n * 512:(n + 1) * 512],
                            start=(k == 0), stop=(k == 7)), after=aft, inc=[pe])
                P.op("scalar", lambda e: e.copy(o_f[:], o_ps[:]),
                     after=[pe, act, dve], inc=[act])
                if dbg:
                    P.op("sync", lambda e, b=b, tt=tt: e.dma_start(
                        out=ctx_dbg[b, tt * 128:(tt + 1) * 128, :], in_=o_f[:]),
                        after=[act, dve], inc=[st], dma=True)
                P.op("vector", lambda e: e.tensor_tensor(
                    out=o_f[:], in0=o_f[:], in1=bias_t["bo_bc"][:], op=ALU.add),
                    after=[act, dve, st], inc=[dve])
                P.op("vector", lambda e: e.tensor_tensor(
                    out=o_f[:], in0=o_f[:], in1=xh_f[:], op=ALU.add),
                    after=[dve], inc=[dve])
                emit_ln(o_f[:], ao_f[:], D, 1e-12)
                P.op("vector", lambda e: e.tensor_tensor(
                    out=ao_f[:], in0=ao_f[:], in1=bias_t["g1_bc"][:], op=ALU.mult),
                    after=[dve], inc=[dve])
                P.op("vector", lambda e: e.tensor_tensor(
                    out=ao_f[:], in0=ao_f[:], in1=bias_t["be1_bc"][:],
                    op=ALU.add), after=[dve, st], inc=[dve])
                P.op("sync", lambda e, b=b, tt=tt: e.dma_start(
                    out=ao_scr[b, tt * 128:(tt + 1) * 128, :], in_=ao_f[:]),
                    after=[dve], inc=[st], dma=True)
                P.op("vector", lambda e, tt=tt: e.tensor_copy(ao_keep[:, tt, :], ao_f[:]),
                     after=[dve], inc=[dve])
            P.join()
            b12.close()
            bfa = ExitStack()
            W1_b = sb(bfa, f"W1_b{b}", [128, 8, FF], BF16)
            b1_t = sb(bfa, f"b1_t{b}", [128, FF], BF16)
            wstage4 = sb(bfa, f"wstage4{b}", [128, 2048], F32)
            hg_bf = [sb(bfa, f"hg_bf{b}{i}", [128, FF], BF16) for i in range(2)]
            aoT = sb(bfa, f"aoT{b}", [128, 8, 128], BF16)
            hpre = sb(bfa, f"hpre{b}", [128, 2048], F32)
            for half in range(2):
                P.op("sync", lambda e, half=half: e.dma_start(
                    out=wstage4[:], in_=bia["b1_bc"][:, half * 2048:(half + 1) * 2048]),
                    after=[dve], inc=[ld], dma=True)
                P.op("vector", lambda e, half=half: e.tensor_copy(
                    b1_t[:, half * 2048:(half + 1) * 2048], wstage4[:]),
                    after=[ld], inc=[dve])
            for c in range(8):
                for half in range(2):
                    P.op("sync", lambda e, c=c, half=half: e.dma_start(
                        out=wstage4[:],
                        in_=bW1[c * 128:(c + 1) * 128, half * 2048:(half + 1) * 2048]),
                        after=[dve], inc=[ld], dma=True)
                    P.op("vector", lambda e, c=c, half=half: e.tensor_copy(
                        W1_b[:, c, half * 2048:(half + 1) * 2048], wstage4[:]),
                        after=[ld, pe, act], inc=[dve])
            for tt in range(NT):
                P.op("vector", lambda e, tt=tt: e.tensor_copy(xh_bf[:], ao_keep[:, tt, :]),
                     after=[dve, pe], inc=[dve])
                emit_transposes(xh_bf, 8, None,
                                lambda g4: aoT[:, g4 * 4:(g4 + 1) * 4, :])
                hg = hg_bf[tt % 2]
                for n in range(8):
                    ps = big_ps[:, (n % 4) * 512:(n % 4) * 512 + 512]
                    for k in range(8):
                        aft = [act, pe, dve] if k == 0 else [pe]
                        P.op("tensor", lambda e, n=n, k=k, ps=ps: e.matmul(
                            ps, aoT[:, k, :], W1_b[:, k, n * 512:(n + 1) * 512],
                            start=(k == 0), stop=(k == 7)), after=aft, inc=[pe])
                    hp = hpre[:, (n % 4) * 512:(n % 4) * 512 + 512]
                    P.op("vector", lambda e, n=n, ps=ps, hp=hp: e.tensor_tensor(
                        out=hp, in0=ps,
                        in1=b1_t[:, n * 512:(n + 1) * 512], op=ALU.add),
                        after=[pe, dve, act], inc=[dve])
                    P.op("scalar", lambda e, n=n, hg=hg, hp=hp: e.activation(
                        out=hg[:, n * 512:(n + 1) * 512],
                        in_=hp, func=AF.Gelu,
                        bias=0.0, scale=1.0), after=[dve, act, st], inc=[act])
                P.op("sync", lambda e, b=b, tt=tt, hg=hg: e.dma_start(
                    out=tq_scr[b, tt * 128:(tt + 1) * 128, :], in_=hg[:]),
                    after=[act], inc=[st], dma=True)
            P.join()
            bfa.close()
            bfb = ExitStack()
            W2_b = sb(bfb, f"W2_b{b}", [128, 32, D], BF16)
            wstage5 = sb(bfb, f"wstage5{b}", [128, D], F32)
            hgi = [sb(bfb, f"hgi{b}{i}", [128, FF], BF16) for i in range(2)]
            hgT = sb(bfb, f"hgT{b}", [128, 32, 128], BF16)
            d_f = sb(bfb, f"d_f3{b}", [128, D], F32)
            aor_f = [sb(bfb, f"aor_f{b}{i}", [128, D], F32) for i in range(2)]
            conv_w(W2_b, bW2, wstage5, FF, D)
            for tt in range(NT):
                hg = hgi[tt % 2]
                P.op("sync", lambda e, b=b, tt=tt, hg=hg: e.dma_start(
                    out=hg[:], in_=tq_scr[b, tt * 128:(tt + 1) * 128, :]),
                    after=[pe], inc=[ld], dma=True)
                P.op("sync", lambda e, b=b, tt=tt: e.dma_start(
                    out=aor_f[tt % 2][:], in_=ao_scr[b, tt * 128:(tt + 1) * 128, :]),
                    after=[dve], inc=[ld], dma=True)
                emit_transposes(hg, 32, None,
                                lambda g4: hgT[:, g4 * 4:(g4 + 1) * 4, :],
                                extra_first=[ld])
                for n in range(2):
                    for k in range(32):
                        aft = [act, pe, dve] if k == 0 else [pe]
                        P.op("tensor", lambda e, n=n, k=k: e.matmul(
                            o_ps[:, n * 512:(n + 1) * 512], hgT[:, k, :],
                            W2_b[:, k, n * 512:(n + 1) * 512],
                            start=(k == 0), stop=(k == 31)), after=aft, inc=[pe])
                P.op("scalar", lambda e: e.copy(d_f[:], o_ps[:]),
                     after=[pe, act, dve], inc=[act])
                P.op("vector", lambda e: e.tensor_tensor(
                    out=d_f[:], in0=d_f[:], in1=bias_t["b2_bc"][:], op=ALU.add),
                    after=[act, dve], inc=[dve])
                P.op("vector", lambda e, tt=tt, ao=None: e.tensor_tensor(
                    out=d_f[:], in0=d_f[:], in1=aor_f[tt % 2][:], op=ALU.add),
                    after=[dve, ld], inc=[dve])
                emit_ln(d_f[:], o_f[:], D, 1e-12)
                P.op("vector", lambda e: e.tensor_tensor(
                    out=o_f[:], in0=o_f[:], in1=bias_t["g2_bc"][:], op=ALU.mult),
                    after=[dve], inc=[dve])
                P.op("vector", lambda e: e.tensor_tensor(
                    out=o_f[:], in0=o_f[:], in1=bias_t["be2_bc"][:], op=ALU.add),
                    after=[dve], inc=[dve])
                xt = xt_f[tt % 2]
                P.op("sync", lambda e, b=b, tt=tt, xt=xt: e.dma_start(
                    out=xt[:], in_=xres[b, tt * 128:(tt + 1) * 128, :]),
                    after=[dve], inc=[ld], dma=True)
                P.op("vector", lambda e, xt=xt: e.tensor_tensor(
                    out=xt[:], in0=xt[:], in1=o_f[:], op=ALU.add),
                    after=[ld, dve], inc=[dve])
                P.op("sync", lambda e, b=b, tt=tt, xt=xt: e.dma_start(
                    out=dst[b, tt * 128:(tt + 1) * 128, :], in_=xt[:]),
                    after=[dve], inc=[st], dma=True)
            P.join()
            bfb.close()
        P.join()
        b3.close()

    emit_attn_b(out if upto == 3 else xres)
    if upto == 3:
        P.join()
        P.emit()
        return nc, stack

    emit_ffn(xres, out, 1)
    P.join()
    P.emit()
    return nc, stack


def _host_layouts(inputs):
    f = lambda a: np.ascontiguousarray(a, dtype=np.float32)
    L = {}
    L["ident"] = np.eye(128, dtype=ml_dtypes.bfloat16)
    L["qWT"] = f(np.asarray(inputs["L0_attn_a_qW"]).T)
    L["kWT"] = f(np.asarray(inputs["L0_attn_a_kW"]).T)
    L["vWT"] = f(np.asarray(inputs["L0_attn_a_vW"]).T)
    L["oWT"] = f(np.asarray(inputs["L0_attn_a_oW"]).T)
    L["qbT"] = f(np.asarray(inputs["L0_attn_a_qb"]).reshape(8, 128).T)
    L["kbT"] = f(np.asarray(inputs["L0_attn_a_kb"]).reshape(8, 128).T)
    L["vb_bc"] = f(np.broadcast_to(np.asarray(inputs["L0_attn_a_vb"]), (128, D)))
    L["ob_bc"] = f(np.broadcast_to(np.asarray(inputs["L0_attn_a_ob"]), (128, D)))
    L["skT"] = f(np.asarray(inputs["L0_attn_a_static_keys"])
                 .reshape(8, 8, 128).transpose(2, 1, 0))
    L["wgT"] = f(np.asarray(inputs["L0_ffn_w_gate"]).T)
    L["wuT"] = f(np.asarray(inputs["L0_ffn_w_up"]).T)
    L["wdT"] = f(np.asarray(inputs["L0_ffn_w_down"]).T)
    L["bWq"] = f(np.asarray(inputs["L0_attn_b_Wq"]))
    L["bWk"] = f(np.asarray(inputs["L0_attn_b_Wk"]))
    L["bWv"] = f(np.asarray(inputs["L0_attn_b_Wv"]))
    L["bWo"] = f(np.asarray(inputs["L0_attn_b_Wo"]))
    L["bW1"] = f(np.asarray(inputs["L0_attn_b_W1"]))
    L["bW2"] = f(np.asarray(inputs["L0_attn_b_W2"]))
    for nm, key in (("bq_bc", "L0_attn_b_bq"), ("bk_bc", "L0_attn_b_bk"),
                    ("bv_bc", "L0_attn_b_bv"), ("bo_bc", "L0_attn_b_bo"),
                    ("b2_bc", "L0_attn_b_b2"), ("g1_bc", "L0_attn_b_g1"),
                    ("be1_bc", "L0_attn_b_be1"), ("g2_bc", "L0_attn_b_g2"),
                    ("be2_bc", "L0_attn_b_be2")):
        L[nm] = f(np.broadcast_to(np.asarray(inputs[key]), (128, D)))
    L["b1_bc"] = f(np.broadcast_to(np.asarray(inputs["L0_attn_b_b1"]), (128, FF)))
    return L


def kernel(**inputs):
    x = np.ascontiguousarray(np.asarray(inputs["L0_attn_a_x"], dtype=np.float32))
    L = _host_layouts(inputs)
    nc, stack = build_program(upto=4)
    in_maps = []
    for c in range(NCORES):
        m = dict(L)
        m["x"] = np.ascontiguousarray(x[c * BPC:(c + 1) * BPC])
        in_maps.append(m)
    res = run_bass_kernel_spmd(nc, in_maps, core_ids=list(range(NCORES)))
    stack.close()
    return np.concatenate([r["out"] for r in res.results], axis=0)


# version 05
# speedup vs baseline: 1.0386x; 1.0386x over previous
import numpy as np
import ml_dtypes
from contextlib import ExitStack

import concourse.bass as bass
import concourse.mybir as mybir
from concourse.bass_utils import run_bass_kernel_spmd

F32 = mybir.dt.float32
BF16 = mybir.dt.bfloat16
I8 = mybir.dt.int8
AF = mybir.ActivationFunctionType
ALU = mybir.AluOpType
AX = mybir.AxisListType

NCORES = 8
SELF_SKIP = {"scalar", "gpsimd", "sync"}
BPC = 2
S = 2048
D = 1024
H = 16
FF = 4096
NT = S // 128


class Chan:
    __slots__ = ("sem", "count", "is_dma")

    def __init__(self, nc, stack, name):
        self.sem = stack.enter_context(nc.semaphore(name))
        self.count = 0
        self.is_dma = False

    def tok(self):
        return (self.sem, self.count)


class Prog:
    ENGINES = ("tensor", "vector", "scalar", "gpsimd", "sync")

    def __init__(self, nc, stack):
        self.nc = nc
        self.stack = stack
        self.ops = []
        self.chans = {}

    def chan(self, name):
        if name not in self.chans:
            self.chans[name] = Chan(self.nc, self.stack, name)
        return self.chans[name]

    def op(self, engine, fn, after=(), inc=(), dma=False):
        waits = []
        for a in after:
            if isinstance(a, Chan):
                if a.count > 0:
                    waits.append((id(a.sem), a.sem, a.count))
            else:
                sem, val = a
                if val > 0:
                    waits.append((id(sem), sem, val))
        wd = {}
        for k, sem, val in waits:
            if k not in wd or wd[k][1] < val:
                wd[k] = (sem, val)
        incs = []
        toks = {}
        for c in inc:
            n = 16 if dma else 1
            if dma:
                c.is_dma = True
            c.count += n
            incs.append((c.sem, n, dma))
            toks[c] = c.tok()
        self.ops.append((engine, fn, list(wd.values()), incs))
        return toks

    def join(self):
        toks = [(c.sem, c.count) for c in self.chans.values() if c.count > 0]
        for e in self.ENGINES:
            self.ops.append((e, None, toks, []))

    def emit(self):
        nc = self.nc
        with nc.Block() as block:
            for eng_name in self.ENGINES:
                section = getattr(block, eng_name)

                def make(name):
                    def body(eng):
                        waited = {}
                        selfc = {}
                        for engine, fn, waits, incs in self.ops:
                            if engine != name:
                                continue
                            for sem, val in waits:
                                k = id(sem)
                                if name in SELF_SKIP and selfc.get(k, 0) >= val:
                                    continue
                                if waited.get(k, 0) >= val:
                                    continue
                                eng.wait_ge(sem, val)
                                waited[k] = val
                            inst = fn(eng) if fn is not None else None
                            if incs:
                                if inst is None:
                                    inst = eng.nop()
                                for sem, n, is_dma in incs:
                                    inst.then_inc(sem, n)
                                    if not is_dma:
                                        selfc[id(sem)] = selfc.get(id(sem), 0) + n
                    return body

                section(make(eng_name))


def build_program(upto=4, dbg=False):
    nc = bass.Bass("TRN2", target_bir_lowering=False)
    stack = ExitStack()
    P = Prog(nc, stack)
    ld, st = P.chan("ld"), P.chan("st")
    pe, dve, act, gp = P.chan("pe"), P.chan("dve"), P.chan("act"), P.chan("gp")

    dram = lambda n, sh, dt, k="ExternalInput": nc.dram_tensor(n, sh, dt, kind=k).ap()
    x_in = dram("x", [BPC, S, D], F32)
    out = dram("out", [BPC, S, D], F32, "ExternalOutput")
    ident = dram("ident", [128, 128], BF16)
    qWT = dram("qWT", [D, D], F32)
    kWT = dram("kWT", [D, D], F32)
    vWT = dram("vWT", [D, D], F32)
    oWT = dram("oWT", [D, D], F32)
    qbT = dram("qbT", [128, 8], F32)
    kbT = dram("kbT", [128, 8], F32)
    vb_bc = dram("vb_bc", [128, D], F32)
    ob_bc = dram("ob_bc", [128, D], F32)
    skT = dram("skT", [128, 8, 8], F32)
    wgT = dram("wgT", [D, FF], F32)
    wuT = dram("wuT", [D, FF], F32)
    wdT = dram("wdT", [FF, D], F32)
    bWq = dram("bWq", [D, D], F32)
    bWk = dram("bWk", [D, D], F32)
    bWv = dram("bWv", [D, D], F32)
    bWo = dram("bWo", [D, D], F32)
    bW1 = dram("bW1", [D, FF], F32)
    bW2 = dram("bW2", [FF, D], F32)
    bia = {nm: dram(nm, [128, FF if nm == "b1_bc" else D], F32)
           for nm in ("bq_bc", "bk_bc", "bv_bc", "bo_bc", "b1_bc", "b2_bc",
                      "g1_bc", "be1_bc", "g2_bc", "be2_bc")}

    xres = dram("xres", [BPC, S, D], F32, "Internal")
    ao_scr = dram("ao_scr", [BPC, S, D], F32, "ExternalOutput" if dbg else "Internal")
    if dbg:
        qn_dbg = dram("qn_dbg", [BPC, S, D], F32, "ExternalOutput")
        kv_dbg = dram("kv_dbg", [BPC, 128, 512], F32, "ExternalOutput")
        ctx_dbg = dram("ctx_dbg", [BPC, S, D], F32, "ExternalOutput")
    tq_scr = dram("tq_scr", [BPC, S, FF], BF16, "Internal")
    wq_scr = {"g": dram("wgq_scr", [D, FF], BF16, "Internal"),
              "u": dram("wuq_scr", [D, FF], BF16, "Internal"),
              "d": dram("wdq_scr", [FF, D], BF16, "Internal")}

    sb = lambda stk, n, sh, dt: stk.enter_context(nc.sbuf_tensor(n, sh, dt))
    pst = lambda stk, n, sh, dt=F32: stk.enter_context(nc.psum_tensor(n, sh, dt))

    id_t = sb(stack, "id_t", [128, 128], BF16)
    P.op("sync", lambda e: e.dma_start(out=id_t[:], in_=ident), inc=[ld], dma=True)
    stat = sb(stack, "stat", [128, 32], F32)
    w_scales = sb(stack, "w_scales", [128, 4], F32)
    stq = sb(stack, "stq", [128, 2 * NT], F32)
    junk = sb(stack, "junk", [128, 2 * D], BF16)
    eps5 = sb(stack, "eps5", [128, 1], F32)
    eps12 = sb(stack, "eps12", [128, 1], F32)

    P.op("vector", lambda e: e.memset(eps5[:], 1e-5), inc=[dve])
    P.op("vector", lambda e: e.memset(eps12[:], 1e-12), inc=[dve])

    big_ps = pst(stack, "big_ps", [128, 2048])
    o_ps = pst(stack, "o_ps", [128, 1024])
    tr_ps = pst(stack, "tr_ps", [128, 2048], BF16)

    def emit_transposes(src_bf, n_chunks, dst, dst_fn, extra_first=()):
        for m in range(n_chunks):
            g4 = m // 4
            bank = g4 % 2
            aft = [pe]
            if m % 4 == 0:
                aft = [pe, act, dve, gp] + list(extra_first)
            P.op("tensor", lambda e, m=m, bank=bank: e.transpose(
                tr_ps[:, bank * 1024 + (m % 4) * 128: bank * 1024 + (m % 4) * 128 + 128],
                src_bf[:, m * 128:(m + 1) * 128], id_t[:]), after=aft, inc=[pe])
            if m % 4 == 3:
                P.op("scalar", lambda e, g4=g4, bank=bank: e.copy(
                    dst_fn(g4),
                    tr_ps[:, bank * 1024: bank * 1024 + 512]
                    .rearrange("p (a b) -> p a b", a=4)),
                    after=[pe, act], inc=[act])

    def emit_ln(in_ap, out_ap, width, eps, dep=()):
        eps_ap = (eps5 if eps > 1e-9 else eps12)[:]
        inv = 1.0 / width
        P.op("vector", lambda e: e.tensor_scalar(
            out=junk[:, 0:width], in0=in_ap, scalar1=inv, scalar2=0.0, op0=ALU.mult,
            op1=ALU.add, accum_out=stat[:, 0:1]), after=(dve, act) + tuple(dep), inc=[dve])
        sq = float(np.sqrt(inv))
        P.op("scalar", lambda e: e.activation(
            out=junk[:, D:D + width], in_=in_ap, func=AF.Square, bias=0.0, scale=sq,
            accum_out=stat[:, 1:2]), after=(act, dve) + tuple(dep), inc=[act])
        P.op("vector", lambda e: e.scalar_tensor_tensor(
            out=stat[:, 3:4], in0=stat[:, 0:1], scalar=-1.0, in1=stat[:, 0:1],
            op0=ALU.mult, op1=ALU.mult), after=(dve, act), inc=[dve])
        P.op("vector", lambda e: e.tensor_tensor(
            out=stat[:, 1:2], in0=stat[:, 1:2], in1=stat[:, 3:4], op=ALU.add),
            after=(dve,), inc=[dve])
        P.op("scalar", lambda e: e.activation(
            out=stat[:, 2:3], in_=stat[:, 1:2], func=AF.Ln, bias=eps_ap, scale=1.0),
            after=(act, dve), inc=[act])
        P.op("scalar", lambda e: e.activation(
            out=stat[:, 2:3], in_=stat[:, 2:3], func=AF.Exp, bias=0.0, scale=-0.5),
            after=(act,), inc=[act])
        return P.op("vector", lambda e: e.tensor_scalar(
            out=out_ap, in0=in_ap, scalar1=stat[:, 0:1], scalar2=stat[:, 2:3],
            op0=ALU.subtract, op1=ALU.mult), after=(dve, act, pe, gp), inc=[dve])

    def conv_w(dst_b, src, wstage, rows, cols):
        for c in range(rows // 128):
            P.op("sync", lambda e, c=c: e.dma_start(
                out=wstage[:, 0:cols], in_=src[c * 128:(c + 1) * 128, :]),
                after=[dve], inc=[ld], dma=True)
            P.op("vector", lambda e, c=c: e.tensor_copy(dst_b[:, c, :], wstage[:, 0:cols]),
                 after=[ld, pe, act], inc=[dve])

    p0 = ExitStack()
    wstage0 = sb(p0, "wstage0", [128, FF], F32)
    qi8 = sb(p0, "qi8", [128, FF], I8)
    qbf = sb(p0, "qbf", [128, FF], BF16)
    ones_col = sb(p0, "ones_col", [128, 1], F32)
    ones_row = sb(p0, "ones_row", [1, 128], F32)
    P.op("vector", lambda e: e.memset(ones_col[:], 1.0), inc=[dve])
    P.op("vector", lambda e: e.memset(ones_row[:], 1.0), inc=[dve])

    def quant_weight(src, rows, cols, scr, scol):
        nch = rows // 128
        acc = stat[:, 8:9]
        red = stat[:, 9:10]
        P.op("vector", lambda e: e.memset(acc, 0.0), after=[dve], inc=[dve])
        for c in range(nch):
            P.op("sync", lambda e, c=c: e.dma_start(
                out=wstage0[:, 0:cols], in_=src[c * 128:(c + 1) * 128, :]),
                after=[dve], inc=[ld], dma=True)
            P.op("vector", lambda e: e.reduce_sum(out=red, in_=wstage0[:, 0:cols],
                                                  axis=AX.X, apply_absolute_value=True),
                 after=[ld, dve], inc=[dve])
            P.op("vector", lambda e: e.tensor_tensor(out=acc, in0=acc, in1=red, op=ALU.add),
                 after=[dve], inc=[dve])
        P.op("tensor", lambda e: e.matmul(big_ps[0:1, 0:1], acc, ones_col[:],
                                          start=True, stop=True),
             after=[dve, pe, act], inc=[pe])
        P.op("vector", lambda e: e.tensor_scalar(
            out=stat[0:1, 10:11], in0=big_ps[0:1, 0:1], scalar1=1.0 / (rows * cols),
            scalar2=1e-5, op0=ALU.mult, op1=ALU.max), after=[pe, dve], inc=[dve])
        P.op("tensor", lambda e: e.matmul(big_ps[:, 1:2], ones_row[:], stat[0:1, 10:11],
                                          start=True, stop=True),
             after=[dve, pe], inc=[pe])
        P.op("vector", lambda e, scol=scol: e.tensor_copy(w_scales[:, scol:scol + 1],
                                                          big_ps[:, 1:2]),
             after=[pe, dve], inc=[dve])
        P.op("vector", lambda e: e.reciprocal(out=stat[:, 11:12],
                                              in_=w_scales[:, scol:scol + 1]),
             after=[dve], inc=[dve])
        for c in range(nch):
            P.op("sync", lambda e, c=c: e.dma_start(
                out=wstage0[:, 0:cols], in_=src[c * 128:(c + 1) * 128, :]),
                after=[dve, st], inc=[ld], dma=True)
            P.op("vector", lambda e: e.tensor_scalar(
                out=qi8[:, 0:cols], in0=wstage0[:, 0:cols], scalar1=stat[:, 11:12],
                scalar2=None, op0=ALU.mult), after=[ld, dve, st], inc=[dve])
            P.op("vector", lambda e: e.tensor_copy(qbf[:, 0:cols], qi8[:, 0:cols]),
                 after=[dve], inc=[dve])
            P.op("vector", lambda e: e.tensor_scalar(
                out=qbf[:, 0:cols], in0=qbf[:, 0:cols], scalar1=1.0, scalar2=-1.0,
                op0=ALU.min, op1=ALU.max), after=[dve], inc=[dve])
            P.op("sync", lambda e, c=c: e.dma_start(
                out=scr[c * 128:(c + 1) * 128, :], in_=qbf[:, 0:cols]),
                after=[dve], inc=[st], dma=True)

    quant_weight(wgT, D, FF, wq_scr["g"], 0)
    quant_weight(wuT, D, FF, wq_scr["u"], 1)
    quant_weight(wdT, FF, D, wq_scr["d"], 2)
    P.join()
    p0.close()

    def emit_attn_a(dst):
        p1 = ExitStack()
        wstage = junk[:].bitcast(F32)
        w0_b = sb(p1, "w0_b", [128, 8, D], BF16)
        kWT_b = sb(p1, "kWT_b", [128, 8, D], BF16)
        vWT_b = sb(p1, "vWT_b", [128, 8, D], BF16)
        qbT_t = sb(p1, "qbT_t", [128, 8], F32)
        kbT_t = sb(p1, "kbT_t", [128, 8], F32)
        skT_t = sb(p1, "skT_t", [128, 8, 8], F32)
        vb_t = sb(p1, "vb_t", [128, D], BF16)
        ob_t = sb(p1, "ob_t", [128, D], F32)
        QT = sb(p1, "QT", [128, 8, S], BF16)
        KT = sb(p1, "KT", [128, 8, S], BF16)
        Vt = sb(p1, "Vt", [128, NT, D], BF16)
        xt_f = [sb(p1, f"xt_f{i}", [128, D], F32) for i in range(2)]
        xhat_b = sb(p1, "xhat_b", [128, D], BF16)
        PTb = sb(p1, "PTb", [128, NT, 256], BF16)
        xTst = PTb[:].rearrange("p a b -> p (a b)").rearrange("p (a b) -> p a b", a=8)
        e_sb = sb(p1, "e_sb", [128, S], F32)
        m_sb = sb(p1, "m_sb", [128, S], F32)
        e2_sb = m_sb
        p_sb = [sb(p1, f"p_sb{i}", [128, S], BF16) for i in range(2)]
        tb16 = sb(p1, "tb16", [128, 16], F32)
        ctxT = sb(p1, "ctxT", [128, 8, 256], BF16)
        o_f = e_sb

        conv_w(kWT_b, kWT, wstage, D, D)
        conv_w(vWT_b, vWT, wstage, D, D)
        P.op("sync", lambda e: e.dma_start(out=wstage[:, 0:D], in_=vb_bc),
             after=[dve], inc=[ld], dma=True)
        P.op("vector", lambda e: e.tensor_copy(vb_t[:], wstage[:, 0:D]),
             after=[ld], inc=[dve])
        P.op("sync", lambda e: e.dma_start(out=qbT_t[:], in_=qbT), inc=[ld], dma=True)
        P.op("sync", lambda e: e.dma_start(out=kbT_t[:], in_=kbT), inc=[ld], dma=True)
        P.op("sync", lambda e: e.dma_start(out=skT_t[:], in_=skT), inc=[ld], dma=True)
        P.op("sync", lambda e: e.dma_start(out=ob_t[:], in_=ob_bc), inc=[ld], dma=True)
        P.op("vector", lambda e: e.tensor_scalar(out=qbT_t[:], in0=qbT_t[:], scalar1=0.125,
                                                 scalar2=None, op0=ALU.mult),
             after=[ld, dve], inc=[dve])

        for b in range(BPC):
            conv_w(w0_b, qWT, wstage, D, D)
            for sst in range(4):
                for sub in range(4):
                    tt = sst * 4 + sub
                    xt = xt_f[tt % 2]
                    P.op("sync", lambda e, b=b, tt=tt, xt=xt: e.dma_start(
                        out=xt[:], in_=x_in[b, tt * 128:(tt + 1) * 128, :]),
                        after=[dve, act], inc=[ld], dma=True)
                    emit_ln(xt[:], xhat_b[:], D, 1e-5, dep=(ld,))
                    emit_transposes(
                        xhat_b, 8, None,
                        lambda g4, sub=sub: xTst[:, g4 * 4:(g4 + 1) * 4,
                                                 sub * 128:(sub + 1) * 128])
                for WTb, bias, scale, DST in ((w0_b, qbT_t, 0.125, QT),
                                              (kWT_b, kbT_t, 1.0, KT)):
                    for m in range(8):
                        ps = big_ps[:, (m % 2) * 512:(m % 2) * 512 + 512]
                        for k in range(8):
                            aft = [act, pe, dve] if k == 0 else [pe]
                            P.op("tensor", lambda e, m=m, k=k, WTb=WTb, ps=ps: e.matmul(
                                ps, WTb[:, k, m * 128:(m + 1) * 128], xTst[:, k, :],
                                start=(k == 0), stop=(k == 7)), after=aft, inc=[pe])
                        P.op("scalar", lambda e, m=m, ps=ps, DST=DST, bias=bias,
                             scale=scale, sst=sst: e.activation(
                                 out=DST[:, m, sst * 512:(sst + 1) * 512], in_=ps,
                                 func=AF.Identity, bias=bias[:, m:m + 1], scale=scale),
                             after=[pe, act], inc=[act])
                for sub in range(4):
                    chunk = sst * 4 + sub
                    for n in range(2):
                        ps = big_ps[:, 1024 + n * 512:1024 + n * 512 + 512]
                        for k in range(8):
                            aft = [dve, pe, act] if k == 0 else [pe]
                            P.op("tensor", lambda e, sub=sub, n=n, k=k, ps=ps: e.matmul(
                                ps, xTst[:, k, sub * 128:(sub + 1) * 128],
                                vWT_b[:, k, n * 512:(n + 1) * 512],
                                start=(k == 0), stop=(k == 7)), after=aft, inc=[pe])
                        P.op("vector", lambda e, chunk=chunk, n=n, ps=ps: e.tensor_tensor(
                            out=Vt[:, chunk, n * 512:(n + 1) * 512], in0=ps,
                            in1=vb_t[:, n * 512:(n + 1) * 512], op=ALU.add),
                            after=[pe, dve], inc=[dve])
            P.op("vector", lambda e: e.tensor_copy(
                KT[:, :, 0:8], skT_t[:]), after=[dve, act, pe], inc=[dve])

            conv_w(w0_b, oWT, wstage, D, D)
            oWT_b = w0_b
            pv_tok = None
            for g in range(8):
                for h in range(H):
                    hrow = (h % 2) * 64
                    hm = h // 2
                    for qt in range(2):
                        qg = g * 2 + qt
                        for n in range(4):
                            aft = [act, pe, dve] if n == 0 else [pe]
                            P.op("tensor", lambda e, hrow=hrow, hm=hm, qg=qg, n=n: e.matmul(
                                big_ps[:, n * 512:(n + 1) * 512],
                                QT[hrow:hrow + 64, hm, qg * 128:(qg + 1) * 128],
                                KT[hrow:hrow + 64, hm, n * 512:(n + 1) * 512],
                                start=True, stop=True), after=aft, inc=[pe])
                        P.op("scalar", lambda e: e.activation(
                            out=e_sb[:], in_=big_ps[:], func=AF.Exp, bias=0.0, scale=1.0),
                            after=[pe, act, dve, gp], inc=[act])
                        P.op("vector", lambda e: e.max(out=tb16[:, 0:8], in_=e_sb[:]),
                             after=[act, dve], inc=[dve])
                        P.op("vector", lambda e: e.match_replace(
                            out=e2_sb[:], in_to_replace=tb16[:, 0:8], in_values=e_sb[:],
                            imm_value=0.0), after=[dve, gp], inc=[dve])
                        P.op("vector", lambda e: e.max(out=tb16[:, 8:16], in_=e2_sb[:]),
                             after=[dve], inc=[dve])
                        P.op("vector", lambda e: e.reduce_sum(
                            out=stat[:, 4:5], in_=tb16[:], axis=AX.X),
                            after=[dve], inc=[dve])
                        P.op("vector", lambda e: e.reciprocal(out=stat[:, 5:6],
                                                              in_=stat[:, 4:5]),
                             after=[dve], inc=[dve])
                        P.op("vector", lambda e: e.tensor_scalar(
                            out=m_sb[:], in0=e_sb[:], scalar1=tb16[:, 15:16],
                            scalar2=stat[:, 5:6], op0=ALU.is_ge, op1=ALU.mult),
                            after=[dve, gp], inc=[dve])
                        pbuf = p_sb[qt % 2]
                        P.op("gpsimd", lambda e, pbuf=pbuf: e.tensor_tensor(
                            out=pbuf[:], in0=m_sb[:], in1=e_sb[:], op=ALU.mult),
                            after=[dve, gp, pe], inc=[gp])
                        extra = [pv_tok[pe]] if pv_tok is not None else []
                        emit_transposes(
                            pbuf, 16, None,
                            lambda g4, qt=qt: PTb[:, g4 * 4:(g4 + 1) * 4,
                                                  qt * 128:(qt + 1) * 128],
                            extra_first=extra)
                    for cc in range(NT):
                        aft = [act, pe] if cc == 0 else [pe]
                        P.op("tensor", lambda e, cc=cc, hrow=hrow, h=h: e.matmul(
                            o_ps[hrow:hrow + 64, 0:256], Vt[:, cc, h * 64:(h + 1) * 64],
                            PTb[:, cc, :], start=(cc == 0), stop=(cc == 15)),
                            after=aft, inc=[pe])
                    pv_tok = P.op("tensor", lambda e: e.nop(), after=[pe], inc=[pe])
                    if h % 2 == 1:
                        P.op("scalar", lambda e, hm=hm: e.copy(ctxT[:, hm, :],
                                                               o_ps[:, 0:256]),
                             after=[pe, act], inc=[act])
                for qt in range(2):
                    qg = g * 2 + qt
                    for n in range(2):
                        for k in range(8):
                            aft = [act, pe, dve] if k == 0 else [pe]
                            P.op("tensor", lambda e, n=n, k=k, qt=qt: e.matmul(
                                o_ps[:, n * 512:(n + 1) * 512],
                                ctxT[:, k, qt * 128:(qt + 1) * 128],
                                oWT_b[:, k, n * 512:(n + 1) * 512],
                                start=(k == 0), stop=(k == 7)), after=aft, inc=[pe])
                    xt = xt_f[qt % 2]
                    P.op("sync", lambda e, b=b, qg=qg, xt=xt: e.dma_start(
                        out=xt[:], in_=x_in[b, qg * 128:(qg + 1) * 128, :]),
                        after=[gp, dve], inc=[ld], dma=True)
                    P.op("gpsimd", lambda e, xt=xt: e.tensor_tensor(
                        out=xt[:], in0=xt[:], in1=ob_t[:], op=ALU.add),
                        after=[ld, gp], inc=[gp])
                    P.op("scalar", lambda e: e.copy(o_f[:, 0:D], o_ps[:]),
                         after=[pe, act, dve, gp], inc=[act])
                    P.op("vector", lambda e, xt=xt: e.tensor_tensor(
                        out=xt[:], in0=xt[:], in1=o_f[:, 0:D], op=ALU.add),
                        after=[act, gp, dve], inc=[dve])
                    P.op("sync", lambda e, b=b, qg=qg, xt=xt: e.dma_start(
                        out=dst[b, qg * 128:(qg + 1) * 128, :], in_=xt[:]),
                        after=[dve], inc=[st], dma=True)
        P.join()
        p1.close()

    emit_attn_a(out if upto == 1 else xres)
    if upto == 1:
        P.join()
        P.emit()
        return nc, stack

    def emit_ffn(src, dst, tag):
        f1 = ExitStack()
        wgq_b = sb(f1, f"wgq_b{tag}", [128, 8, FF], BF16)
        wuq_b = sb(f1, f"wuq_b{tag}", [128, 8, FF], BF16)
        xt_f = [sb(f1, f"fx{tag}{i}", [128, D], F32) for i in range(2)]
        xhat_f = sb(f1, f"xhat_f{tag}", [128, D], F32)
        xq_i8 = sb(f1, f"xq_i8{tag}", [128, D], I8)
        xq_bf = sb(f1, f"xq_bf{tag}", [128, D], BF16)
        xqT = sb(f1, f"xqT{tag}", [128, 8, 128], BF16)
        g_f = sb(f1, f"g_f{tag}", [128, FF], F32)
        u_f = sb(f1, f"u_f{tag}", [128, FF], F32)
        tq_i8 = sb(f1, f"tq_i8{tag}", [128, FF], I8)
        tq_bf = [sb(f1, f"tq_bf{tag}{i}", [128, FF], BF16) for i in range(2)]
        for c in range(8):
            P.op("sync", lambda e, c=c: e.dma_start(
                out=wgq_b[:, c, :], in_=wq_scr["g"][c * 128:(c + 1) * 128, :]),
                inc=[ld], dma=True)
            P.op("sync", lambda e, c=c: e.dma_start(
                out=wuq_b[:, c, :], in_=wq_scr["u"][c * 128:(c + 1) * 128, :]),
                inc=[ld], dma=True)
        ld_tok = ld.tok()
        for b in range(BPC):
            for tt in range(NT):
                xt = xt_f[tt % 2]
                P.op("sync", lambda e, b=b, tt=tt, xt=xt: e.dma_start(
                    out=xt[:], in_=src[b, tt * 128:(tt + 1) * 128, :]),
                    after=[dve, act], inc=[ld], dma=True)
                emit_ln(xt[:], xhat_f[:], D, 1e-5, dep=(ld,))
                P.op("vector", lambda e: e.reduce_max(
                    out=stat[:, 6:7], in_=xhat_f[:], axis=AX.X,
                    apply_absolute_value=True), after=[dve], inc=[dve])
                P.op("vector", lambda e: e.tensor_scalar(
                    out=stat[:, 6:7], in0=stat[:, 6:7], scalar1=1e-5, scalar2=None,
                    op0=ALU.max), after=[dve], inc=[dve])
                P.op("vector", lambda e: e.reciprocal(out=stat[:, 7:8], in_=stat[:, 6:7]),
                     after=[dve], inc=[dve])
                P.op("vector", lambda e: e.tensor_scalar(
                    out=stat[:, 7:8], in0=stat[:, 7:8], scalar1=127.0, scalar2=None,
                    op0=ALU.mult), after=[dve], inc=[dve])
                P.op("vector", lambda e: e.tensor_scalar(
                    out=xq_i8[:], in0=xhat_f[:], scalar1=stat[:, 7:8], scalar2=None,
                    op0=ALU.mult), after=[dve], inc=[dve])
                P.op("vector", lambda e: e.tensor_copy(xq_bf[:], xq_i8[:]),
                     after=[dve, pe], inc=[dve])
                P.op("vector", lambda e: e.tensor_scalar(
                    out=stat[:, 8:9], in0=stat[:, 6:7], scalar1=1.0 / 127.0, scalar2=None,
                    op0=ALU.mult), after=[dve], inc=[dve])
                P.op("vector", lambda e: e.tensor_tensor(
                    out=stat[:, 12:13], in0=stat[:, 8:9], in1=w_scales[:, 0:1],
                    op=ALU.mult), after=[dve, act], inc=[dve])
                P.op("vector", lambda e: e.tensor_tensor(
                    out=stat[:, 13:14], in0=stat[:, 8:9], in1=w_scales[:, 1:2],
                    op=ALU.mult), after=[dve], inc=[dve])
                emit_transposes(xq_bf, 8, None,
                                lambda g4: xqT[:, g4 * 4:(g4 + 1) * 4, :],
                                extra_first=[ld_tok])
                for which, wq_b, dstf, scol, fn in (
                        (0, wgq_b, g_f, 12, AF.Silu), (1, wuq_b, u_f, 13, AF.Copy)):
                    for n in range(8):
                        ps = big_ps[:, (n % 4) * 512:(n % 4) * 512 + 512]
                        for k in range(8):
                            aft = [act, pe] if k == 0 else [pe]
                            P.op("tensor", lambda e, n=n, k=k, wq_b=wq_b, ps=ps: e.matmul(
                                ps, xqT[:, k, :], wq_b[:, k, n * 512:(n + 1) * 512],
                                start=(k == 0), stop=(k == 7)), after=aft, inc=[pe])
                        P.op("scalar", lambda e, n=n, ps=ps, dstf=dstf, scol=scol, fn=fn:
                             e.activation(out=dstf[:, n * 512:(n + 1) * 512], in_=ps,
                                          func=fn, bias=0.0, scale=stat[:, scol:scol + 1]),
                             after=[pe, act, dve], inc=[act])
                P.op("vector", lambda e: e.tensor_tensor(out=g_f[:], in0=g_f[:], in1=u_f[:],
                                                         op=ALU.mult),
                     after=[act, dve], inc=[dve])
                P.op("vector", lambda e: e.reduce_max(
                    out=stat[:, 6:7], in_=g_f[:], axis=AX.X, apply_absolute_value=True),
                    after=[dve], inc=[dve])
                P.op("vector", lambda e: e.tensor_scalar(
                    out=stat[:, 6:7], in0=stat[:, 6:7], scalar1=1e-5, scalar2=None,
                    op0=ALU.max), after=[dve], inc=[dve])
                P.op("vector", lambda e, b=b, tt=tt: e.tensor_scalar(
                    out=stq[:, b * NT + tt:b * NT + tt + 1], in0=stat[:, 6:7],
                    scalar1=1.0 / 127.0, scalar2=None, op0=ALU.mult),
                    after=[dve], inc=[dve])
                P.op("vector", lambda e: e.reciprocal(out=stat[:, 7:8], in_=stat[:, 6:7]),
                     after=[dve], inc=[dve])
                P.op("vector", lambda e: e.tensor_scalar(
                    out=stat[:, 7:8], in0=stat[:, 7:8], scalar1=127.0, scalar2=None,
                    op0=ALU.mult), after=[dve], inc=[dve])
                P.op("vector", lambda e: e.tensor_scalar(
                    out=tq_i8[:], in0=g_f[:], scalar1=stat[:, 7:8], scalar2=None,
                    op0=ALU.mult), after=[dve], inc=[dve])
                tqb = tq_bf[tt % 2]
                P.op("vector", lambda e, tqb=tqb: e.tensor_copy(tqb[:], tq_i8[:]),
                     after=[dve, st], inc=[dve])
                P.op("sync", lambda e, b=b, tt=tt, tqb=tqb: e.dma_start(
                    out=tq_scr[b, tt * 128:(tt + 1) * 128, :], in_=tqb[:]),
                    after=[dve], inc=[st], dma=True)
        P.join()
        f1.close()
        f2 = ExitStack()
        wdq_b = sb(f2, f"wdq_b{tag}", [128, 32, D], BF16)
        tqi = [sb(f2, f"tqi{tag}{i}", [128, FF], BF16) for i in range(2)]
        tqT = sb(f2, f"tqT{tag}", [128, 32, 128], BF16)
        xt_f2 = [sb(f2, f"fx2{tag}{i}", [128, D], F32) for i in range(2)]
        d_f = sb(f2, f"d_f{tag}", [128, D], F32)
        for c in range(32):
            P.op("sync", lambda e, c=c: e.dma_start(
                out=wdq_b[:, c, :], in_=wq_scr["d"][c * 128:(c + 1) * 128, :]),
                inc=[ld], dma=True)
        ld_tok2 = ld.tok()
        for b in range(BPC):
            for tt in range(NT):
                tqb = tqi[tt % 2]
                P.op("sync", lambda e, b=b, tt=tt, tqb=tqb: e.dma_start(
                    out=tqb[:], in_=tq_scr[b, tt * 128:(tt + 1) * 128, :]),
                    after=[pe], inc=[ld], dma=True)
                emit_transposes(tqb, 32, None,
                                lambda g4: tqT[:, g4 * 4:(g4 + 1) * 4, :],
                                extra_first=[ld, ld_tok2])
                for n in range(2):
                    for k in range(32):
                        aft = [act, pe, dve] if k == 0 else [pe]
                        P.op("tensor", lambda e, n=n, k=k: e.matmul(
                            o_ps[:, n * 512:(n + 1) * 512], tqT[:, k, :],
                            wdq_b[:, k, n * 512:(n + 1) * 512],
                            start=(k == 0), stop=(k == 31)), after=aft, inc=[pe])
                P.op("vector", lambda e, b=b, tt=tt: e.tensor_tensor(
                    out=stat[:, 14:15], in0=stq[:, b * NT + tt:b * NT + tt + 1],
                    in1=w_scales[:, 2:3], op=ALU.mult), after=[dve, act], inc=[dve])
                P.op("scalar", lambda e: e.activation(
                    out=d_f[:], in_=o_ps[:], func=AF.Copy, bias=0.0,
                    scale=stat[:, 14:15]), after=[pe, act, dve], inc=[act])
                xt = xt_f2[tt % 2]
                P.op("sync", lambda e, b=b, tt=tt, xt=xt: e.dma_start(
                    out=xt[:], in_=src[b, tt * 128:(tt + 1) * 128, :]),
                    after=[dve], inc=[ld], dma=True)
                P.op("vector", lambda e, xt=xt: e.tensor_tensor(
                    out=xt[:], in0=xt[:], in1=d_f[:], op=ALU.add),
                    after=[ld, act, dve], inc=[dve])
                P.op("sync", lambda e, b=b, tt=tt, xt=xt: e.dma_start(
                    out=dst[b, tt * 128:(tt + 1) * 128, :], in_=xt[:]),
                    after=[dve], inc=[st], dma=True)
        P.join()
        f2.close()

    emit_ffn(xres, out if upto == 2 else xres, 0)
    if upto == 2:
        P.join()
        P.emit()
        return nc, stack

    def emit_attn_b(dst):
        b3 = ExitStack()
        bias_t = {}
        for nm in ("bq_bc", "bk_bc", "bv_bc", "bo_bc", "b2_bc", "g1_bc", "be1_bc",
                   "g2_bc", "be2_bc"):
            bias_t[nm] = sb(b3, nm + "_t", [128, D], F32)
            P.op("sync", lambda e, nm=nm: e.dma_start(out=bias_t[nm][:], in_=bia[nm]),
                 inc=[ld], dma=True)
        xt_f = [sb(b3, f"bx{i}", [128, D], F32) for i in range(2)]
        ao_keep = sb(b3, "ao_keep", [128, NT, D], BF16)
        xh_bf = sb(b3, "xh_bf", [128, D], BF16)
        xhT = sb(b3, "xhT", [128, 8, 128], BF16)
        o_f = sb(b3, "o_f3", [128, D], F32)
        ao_f = sb(b3, "ao_f", [128, D], F32)

        kv_ps = big_ps[:, 1536:2048]

        def emit_elunorm(src_f, dst_bf, tmp_f, eq_f, sq_f, s16):
            P.op("vector", lambda e, src_f=src_f: e.tensor_scalar(
                out=tmp_f[:], in0=src_f[:], scalar1=0.0, scalar2=None, op0=ALU.min),
                after=[dve, act], inc=[dve])
            P.op("scalar", lambda e: e.activation(out=eq_f[:], in_=tmp_f[:], func=AF.Exp,
                                                  bias=0.0, scale=1.0),
                 after=[act, dve], inc=[act])
            P.op("vector", lambda e, src_f=src_f: e.scalar_tensor_tensor(
                out=eq_f[:], in0=src_f[:], scalar=0.0, in1=eq_f[:], op0=ALU.max,
                op1=ALU.add), after=[dve, act], inc=[dve])
            P.op("vector", lambda e: e.tensor_scalar(
                out=eq_f[:], in0=eq_f[:], scalar1=1.0, scalar2=None, op0=ALU.subtract),
                after=[dve], inc=[dve])
            P.op("scalar", lambda e: e.activation(out=sq_f[:], in_=eq_f[:], func=AF.Square,
                                                  bias=0.0, scale=1.0),
                 after=[act, dve], inc=[act])
            P.op("vector", lambda e: e.tensor_reduce(
                out=s16[:], in_=sq_f[:].rearrange("p (h d) -> p h d", h=16),
                axis=AX.X, op=ALU.add), after=[dve, act], inc=[dve])
            P.op("scalar", lambda e: e.activation(out=s16[:], in_=s16[:], func=AF.Ln,
                                                  bias=0.0, scale=1.0),
                 after=[act, dve], inc=[act])
            P.op("scalar", lambda e: e.activation(out=s16[:], in_=s16[:], func=AF.Exp,
                                                  bias=0.0, scale=-0.5),
                 after=[act], inc=[act])
            return P.op("vector", lambda e, dst_bf=dst_bf: e.tensor_tensor(
                out=dst_bf[:].rearrange("p (h d) -> p h d", h=16),
                in0=eq_f[:].rearrange("p (h d) -> p h d", h=16),
                in1=s16[:].unsqueeze(2).to_broadcast([128, 16, 64]), op=ALU.mult),
                after=[dve, act, pe], inc=[dve])

        for b in range(BPC):
            b12 = ExitStack()
            wstage = sb(b12, f"wstage3{b}", [128, D], F32)
            w0b_b = sb(b12, f"w0b_b{b}", [128, 8, D], BF16)
            Wk_b = sb(b12, f"Wk_b{b}", [128, 8, D], BF16)
            Wv_b = sb(b12, f"Wv_b{b}", [128, 8, D], BF16)
            qn_keep = sb(b12, f"qn_keep{b}", [128, NT, D], BF16)
            q_f = sb(b12, f"q_f{b}", [128, D], F32)
            k_f = sb(b12, f"k_f{b}", [128, D], F32)
            v_bf = sb(b12, f"v_bf{b}", [128, D], BF16)
            xh_f = sb(b12, f"xh_f{b}", [128, D], F32)
            tmp_f = sb(b12, f"tmp_f{b}", [128, D], F32)
            eq_f = sb(b12, f"eq_f{b}", [128, D], F32)
            sq_f = sb(b12, f"sq_f{b}", [128, D], F32)
            s16 = sb(b12, f"s16{b}", [128, 16], F32)
            n_bf = sb(b12, f"n_bf{b}", [128, D], BF16)
            kv_bf = sb(b12, f"kv_bf{b}", [128, 512], BF16)
            ctxbT = sb(b12, f"ctxbT{b}", [128, 8, 128], BF16)
            elun = lambda sf, db: emit_elunorm(sf, db, tmp_f, eq_f, sq_f, s16)
            conv_w(Wk_b, bWk, wstage, D, D)
            conv_w(Wv_b, bWv, wstage, D, D)
            conv_w(w0b_b, bWq, wstage, D, D)
            for tt in range(NT):
                xt = xt_f[tt % 2]
                P.op("sync", lambda e, b=b, tt=tt, xt=xt: e.dma_start(
                    out=xt[:], in_=xres[b, tt * 128:(tt + 1) * 128, :]),
                    after=[dve, act], inc=[ld], dma=True)
                emit_ln(xt[:], xh_bf[:], D, 1e-5, dep=(ld,))
                emit_transposes(xh_bf, 8, None,
                                lambda g4: xhT[:, g4 * 4:(g4 + 1) * 4, :])
                for W_b, bias_nm, dstt in ((w0b_b, "bq_bc", q_f), (Wk_b, "bk_bc", k_f),
                                           (Wv_b, "bv_bc", v_bf)):
                    for n in range(2):
                        ps = big_ps[:, n * 512:n * 512 + 512]
                        for k in range(8):
                            aft = [act, pe, dve, gp] if k == 0 else [pe]
                            P.op("tensor", lambda e, n=n, k=k, W_b=W_b, ps=ps: e.matmul(
                                ps, xhT[:, k, :], W_b[:, k, n * 512:(n + 1) * 512],
                                start=(k == 0), stop=(k == 7)), after=aft, inc=[pe])
                    P.op("vector", lambda e, dstt=dstt, bias_nm=bias_nm: e.tensor_tensor(
                        out=dstt[:], in0=big_ps[:, 0:1024], in1=bias_t[bias_nm][:],
                        op=ALU.add), after=[pe, dve], inc=[dve])
                elun(q_f, n_bf)
                P.op("vector", lambda e, tt=tt: e.tensor_copy(qn_keep[:, tt, :], n_bf[:]),
                     after=[dve], inc=[dve])
                if dbg:
                    P.op("vector", lambda e: e.tensor_copy(tmp_f[:], n_bf[:]),
                         after=[dve], inc=[dve])
                    P.op("sync", lambda e, b=b, tt=tt: e.dma_start(
                        out=qn_dbg[b, tt * 128:(tt + 1) * 128, :], in_=tmp_f[:]),
                        after=[dve], inc=[st], dma=True)
                elun(k_f, n_bf)
                for h in range(H):
                    hrow = (h % 2) * 64
                    hm = h // 2
                    aft = [dve, pe] if h == 0 else [pe]
                    P.op("tensor", lambda e, h=h, hrow=hrow, hm=hm, tt=tt: e.matmul(
                        kv_ps[hrow:hrow + 64, hm * 64:(hm + 1) * 64],
                        n_bf[:, h * 64:(h + 1) * 64], v_bf[:, h * 64:(h + 1) * 64],
                        start=(tt == 0 and h < 2), stop=(tt == NT - 1)),
                        after=aft, inc=[pe])
            P.op("scalar", lambda e: e.activation(out=kv_bf[:], in_=kv_ps, func=AF.Copy,
                                                  bias=0.0, scale=0.125),
                 after=[pe, act], inc=[act])
            if dbg:
                P.op("vector", lambda e: e.tensor_copy(xh_f[:, 0:512], kv_bf[:]),
                     after=[act, dve], inc=[dve])
                P.op("sync", lambda e, b=b: e.dma_start(
                    out=kv_dbg[b], in_=xh_f[:, 0:512]), after=[dve], inc=[st], dma=True)
            conv_w(w0b_b, bWo, wstage, D, D)
            Wo_b = w0b_b
            for tt in range(NT):
                xt = xt_f[tt % 2]
                P.op("sync", lambda e, b=b, tt=tt, xt=xt: e.dma_start(
                    out=xt[:], in_=xres[b, tt * 128:(tt + 1) * 128, :]),
                    after=[dve, act], inc=[ld], dma=True)
                emit_ln(xt[:], xh_f[:], D, 1e-5, dep=(ld,))
                P.op("vector", lambda e, tt=tt: e.tensor_copy(xh_bf[:], qn_keep[:, tt, :]),
                     after=[dve, pe], inc=[dve])
                emit_transposes(xh_bf, 8, None,
                                lambda g4: xhT[:, g4 * 4:(g4 + 1) * 4, :])
                for h in range(H):
                    hrow = (h % 2) * 64
                    hm = h // 2
                    aft = [act, pe] if h == 0 else [pe]
                    P.op("tensor", lambda e, hrow=hrow, hm=hm, h=h: e.matmul(
                        big_ps[hrow:hrow + 64, hm * 128:hm * 128 + 128],
                        kv_bf[hrow:hrow + 64, hm * 64:(hm + 1) * 64],
                        xhT[hrow:hrow + 64, h // 2, :],
                        start=True, stop=True), after=aft, inc=[pe])
                P.op("scalar", lambda e: e.copy(
                    ctxbT[:].rearrange("p a b -> p (a b)"), big_ps[:, 0:1024]),
                    after=[pe, act], inc=[act])
                for n in range(2):
                    for k in range(8):
                        aft = [act, pe, dve] if k == 0 else [pe]
                        P.op("tensor", lambda e, n=n, k=k: e.matmul(
                            o_ps[:, n * 512:(n + 1) * 512], ctxbT[:, k, :],
                            Wo_b[:, k, n * 512:(n + 1) * 512],
                            start=(k == 0), stop=(k == 7)), after=aft, inc=[pe])
                P.op("scalar", lambda e: e.copy(o_f[:], o_ps[:]),
                     after=[pe, act, dve], inc=[act])
                if dbg:
                    P.op("sync", lambda e, b=b, tt=tt: e.dma_start(
                        out=ctx_dbg[b, tt * 128:(tt + 1) * 128, :], in_=o_f[:]),
                        after=[act, dve], inc=[st], dma=True)
                P.op("vector", lambda e: e.tensor_tensor(
                    out=o_f[:], in0=o_f[:], in1=bias_t["bo_bc"][:], op=ALU.add),
                    after=[act, dve, st], inc=[dve])
                P.op("vector", lambda e: e.tensor_tensor(
                    out=o_f[:], in0=o_f[:], in1=xh_f[:], op=ALU.add),
                    after=[dve], inc=[dve])
                emit_ln(o_f[:], ao_f[:], D, 1e-12)
                P.op("vector", lambda e: e.tensor_tensor(
                    out=ao_f[:], in0=ao_f[:], in1=bias_t["g1_bc"][:], op=ALU.mult),
                    after=[dve], inc=[dve])
                P.op("vector", lambda e: e.tensor_tensor(
                    out=ao_f[:], in0=ao_f[:], in1=bias_t["be1_bc"][:],
                    op=ALU.add), after=[dve, st], inc=[dve])
                P.op("sync", lambda e, b=b, tt=tt: e.dma_start(
                    out=ao_scr[b, tt * 128:(tt + 1) * 128, :], in_=ao_f[:]),
                    after=[dve], inc=[st], dma=True)
                P.op("vector", lambda e, tt=tt: e.tensor_copy(ao_keep[:, tt, :], ao_f[:]),
                     after=[dve], inc=[dve])
            P.join()
            b12.close()
            bfa = ExitStack()
            W1_b = sb(bfa, f"W1_b{b}", [128, 8, FF], BF16)
            b1_t = sb(bfa, f"b1_t{b}", [128, FF], BF16)
            wstage4 = sb(bfa, f"wstage4{b}", [128, 2048], F32)
            hg_bf = [sb(bfa, f"hg_bf{b}{i}", [128, FF], BF16) for i in range(2)]
            aoT = sb(bfa, f"aoT{b}", [128, 8, 128], BF16)
            hpre = sb(bfa, f"hpre{b}", [128, 2048], F32)
            for half in range(2):
                P.op("sync", lambda e, half=half: e.dma_start(
                    out=wstage4[:], in_=bia["b1_bc"][:, half * 2048:(half + 1) * 2048]),
                    after=[dve], inc=[ld], dma=True)
                P.op("vector", lambda e, half=half: e.tensor_copy(
                    b1_t[:, half * 2048:(half + 1) * 2048], wstage4[:]),
                    after=[ld], inc=[dve])
            for c in range(8):
                for half in range(2):
                    P.op("sync", lambda e, c=c, half=half: e.dma_start(
                        out=wstage4[:],
                        in_=bW1[c * 128:(c + 1) * 128, half * 2048:(half + 1) * 2048]),
                        after=[dve], inc=[ld], dma=True)
                    P.op("vector", lambda e, c=c, half=half: e.tensor_copy(
                        W1_b[:, c, half * 2048:(half + 1) * 2048], wstage4[:]),
                        after=[ld, pe, act], inc=[dve])
            for tt in range(NT):
                P.op("vector", lambda e, tt=tt: e.tensor_copy(xh_bf[:], ao_keep[:, tt, :]),
                     after=[dve, pe], inc=[dve])
                emit_transposes(xh_bf, 8, None,
                                lambda g4: aoT[:, g4 * 4:(g4 + 1) * 4, :])
                hg = hg_bf[tt % 2]
                for n in range(8):
                    ps = big_ps[:, (n % 4) * 512:(n % 4) * 512 + 512]
                    for k in range(8):
                        aft = [act, pe, dve] if k == 0 else [pe]
                        P.op("tensor", lambda e, n=n, k=k, ps=ps: e.matmul(
                            ps, aoT[:, k, :], W1_b[:, k, n * 512:(n + 1) * 512],
                            start=(k == 0), stop=(k == 7)), after=aft, inc=[pe])
                    hp = hpre[:, (n % 4) * 512:(n % 4) * 512 + 512]
                    P.op("vector", lambda e, n=n, ps=ps, hp=hp: e.tensor_tensor(
                        out=hp, in0=ps,
                        in1=b1_t[:, n * 512:(n + 1) * 512], op=ALU.add),
                        after=[pe, dve, act], inc=[dve])
                    P.op("scalar", lambda e, n=n, hg=hg, hp=hp: e.activation(
                        out=hg[:, n * 512:(n + 1) * 512],
                        in_=hp, func=AF.Gelu,
                        bias=0.0, scale=1.0), after=[dve, act, st], inc=[act])
                P.op("sync", lambda e, b=b, tt=tt, hg=hg: e.dma_start(
                    out=tq_scr[b, tt * 128:(tt + 1) * 128, :], in_=hg[:]),
                    after=[act], inc=[st], dma=True)
            P.join()
            bfa.close()
            bfb = ExitStack()
            W2_b = sb(bfb, f"W2_b{b}", [128, 32, D], BF16)
            wstage5 = sb(bfb, f"wstage5{b}", [128, D], F32)
            hgi = [sb(bfb, f"hgi{b}{i}", [128, FF], BF16) for i in range(2)]
            hgT = sb(bfb, f"hgT{b}", [128, 32, 128], BF16)
            d_f = sb(bfb, f"d_f3{b}", [128, D], F32)
            aor_f = [sb(bfb, f"aor_f{b}{i}", [128, D], F32) for i in range(2)]
            conv_w(W2_b, bW2, wstage5, FF, D)
            for tt in range(NT):
                hg = hgi[tt % 2]
                P.op("sync", lambda e, b=b, tt=tt, hg=hg: e.dma_start(
                    out=hg[:], in_=tq_scr[b, tt * 128:(tt + 1) * 128, :]),
                    after=[pe], inc=[ld], dma=True)
                P.op("sync", lambda e, b=b, tt=tt: e.dma_start(
                    out=aor_f[tt % 2][:], in_=ao_scr[b, tt * 128:(tt + 1) * 128, :]),
                    after=[dve], inc=[ld], dma=True)
                emit_transposes(hg, 32, None,
                                lambda g4: hgT[:, g4 * 4:(g4 + 1) * 4, :],
                                extra_first=[ld])
                for n in range(2):
                    for k in range(32):
                        aft = [act, pe, dve] if k == 0 else [pe]
                        P.op("tensor", lambda e, n=n, k=k: e.matmul(
                            o_ps[:, n * 512:(n + 1) * 512], hgT[:, k, :],
                            W2_b[:, k, n * 512:(n + 1) * 512],
                            start=(k == 0), stop=(k == 31)), after=aft, inc=[pe])
                P.op("scalar", lambda e: e.copy(d_f[:], o_ps[:]),
                     after=[pe, act, dve], inc=[act])
                P.op("vector", lambda e: e.tensor_tensor(
                    out=d_f[:], in0=d_f[:], in1=bias_t["b2_bc"][:], op=ALU.add),
                    after=[act, dve], inc=[dve])
                P.op("vector", lambda e, tt=tt, ao=None: e.tensor_tensor(
                    out=d_f[:], in0=d_f[:], in1=aor_f[tt % 2][:], op=ALU.add),
                    after=[dve, ld], inc=[dve])
                emit_ln(d_f[:], o_f[:], D, 1e-12)
                P.op("vector", lambda e: e.tensor_tensor(
                    out=o_f[:], in0=o_f[:], in1=bias_t["g2_bc"][:], op=ALU.mult),
                    after=[dve], inc=[dve])
                P.op("vector", lambda e: e.tensor_tensor(
                    out=o_f[:], in0=o_f[:], in1=bias_t["be2_bc"][:], op=ALU.add),
                    after=[dve], inc=[dve])
                xt = xt_f[tt % 2]
                P.op("sync", lambda e, b=b, tt=tt, xt=xt: e.dma_start(
                    out=xt[:], in_=xres[b, tt * 128:(tt + 1) * 128, :]),
                    after=[dve], inc=[ld], dma=True)
                P.op("vector", lambda e, xt=xt: e.tensor_tensor(
                    out=xt[:], in0=xt[:], in1=o_f[:], op=ALU.add),
                    after=[ld, dve], inc=[dve])
                P.op("sync", lambda e, b=b, tt=tt, xt=xt: e.dma_start(
                    out=dst[b, tt * 128:(tt + 1) * 128, :], in_=xt[:]),
                    after=[dve], inc=[st], dma=True)
            P.join()
            bfb.close()
        P.join()
        b3.close()

    emit_attn_b(out if upto == 3 else xres)
    if upto == 3:
        P.join()
        P.emit()
        return nc, stack

    emit_ffn(xres, out, 1)
    P.join()
    P.emit()
    return nc, stack


def _host_layouts(inputs):
    f = lambda a: np.ascontiguousarray(a, dtype=np.float32)
    L = {}
    L["ident"] = np.eye(128, dtype=ml_dtypes.bfloat16)
    L["qWT"] = f(np.asarray(inputs["L0_attn_a_qW"]).T)
    L["kWT"] = f(np.asarray(inputs["L0_attn_a_kW"]).T)
    L["vWT"] = f(np.asarray(inputs["L0_attn_a_vW"]).T)
    L["oWT"] = f(np.asarray(inputs["L0_attn_a_oW"]).T)
    L["qbT"] = f(np.asarray(inputs["L0_attn_a_qb"]).reshape(8, 128).T)
    L["kbT"] = f(np.asarray(inputs["L0_attn_a_kb"]).reshape(8, 128).T)
    L["vb_bc"] = f(np.broadcast_to(np.asarray(inputs["L0_attn_a_vb"]), (128, D)))
    L["ob_bc"] = f(np.broadcast_to(np.asarray(inputs["L0_attn_a_ob"]), (128, D)))
    L["skT"] = f(np.asarray(inputs["L0_attn_a_static_keys"])
                 .reshape(8, 8, 128).transpose(2, 1, 0))
    L["wgT"] = f(np.asarray(inputs["L0_ffn_w_gate"]).T)
    L["wuT"] = f(np.asarray(inputs["L0_ffn_w_up"]).T)
    L["wdT"] = f(np.asarray(inputs["L0_ffn_w_down"]).T)
    L["bWq"] = f(np.asarray(inputs["L0_attn_b_Wq"]))
    L["bWk"] = f(np.asarray(inputs["L0_attn_b_Wk"]))
    L["bWv"] = f(np.asarray(inputs["L0_attn_b_Wv"]))
    L["bWo"] = f(np.asarray(inputs["L0_attn_b_Wo"]))
    L["bW1"] = f(np.asarray(inputs["L0_attn_b_W1"]))
    L["bW2"] = f(np.asarray(inputs["L0_attn_b_W2"]))
    for nm, key in (("bq_bc", "L0_attn_b_bq"), ("bk_bc", "L0_attn_b_bk"),
                    ("bv_bc", "L0_attn_b_bv"), ("bo_bc", "L0_attn_b_bo"),
                    ("b2_bc", "L0_attn_b_b2"), ("g1_bc", "L0_attn_b_g1"),
                    ("be1_bc", "L0_attn_b_be1"), ("g2_bc", "L0_attn_b_g2"),
                    ("be2_bc", "L0_attn_b_be2")):
        L[nm] = f(np.broadcast_to(np.asarray(inputs[key]), (128, D)))
    L["b1_bc"] = f(np.broadcast_to(np.asarray(inputs["L0_attn_b_b1"]), (128, FF)))
    return L


def kernel(**inputs):
    x = np.ascontiguousarray(np.asarray(inputs["L0_attn_a_x"], dtype=np.float32))
    L = _host_layouts(inputs)
    nc, stack = build_program(upto=4)
    in_maps = []
    for c in range(NCORES):
        m = dict(L)
        m["x"] = np.ascontiguousarray(x[c * BPC:(c + 1) * BPC])
        in_maps.append(m)
    res = run_bass_kernel_spmd(nc, in_maps, core_ids=list(range(NCORES)))
    stack.close()
    return np.concatenate([r["out"] for r in res.results], axis=0)


# version 08
# speedup vs baseline: 1.2670x; 1.2199x over previous
import numpy as np
import ml_dtypes
from contextlib import ExitStack

import concourse.bass as bass
import concourse.mybir as mybir
from concourse.bass_utils import run_bass_kernel_spmd

F32 = mybir.dt.float32
BF16 = mybir.dt.bfloat16
I8 = mybir.dt.int8
AF = mybir.ActivationFunctionType
ALU = mybir.AluOpType
AX = mybir.AxisListType

NCORES = 8
BPC = 2
S = 2048
D = 1024
H = 16
FF = 4096
NT = S // 128


class Chan:
    __slots__ = ("sem", "count")

    def __init__(self, nc, stack, name):
        self.sem = stack.enter_context(nc.semaphore(name))
        self.count = 0

    def tok(self):
        return (self.sem, self.count)


class Prog:
    ENGINES = ("tensor", "vector", "scalar", "gpsimd", "sync")

    def __init__(self, nc, stack):
        self.nc = nc
        self.stack = stack
        self.ops = []
        self.chans = {}

    def chan(self, name):
        if name not in self.chans:
            self.chans[name] = Chan(self.nc, self.stack, name)
        return self.chans[name]

    def op(self, engine, fn, after=(), inc=(), dma=False):
        waits = []
        for a in after:
            if a is None:
                continue
            if isinstance(a, Chan):
                if a.count > 0:
                    waits.append((a.sem, a.count))
            else:
                sem, val = a
                if val > 0:
                    waits.append((sem, val))
        wd = {}
        for sem, val in waits:
            k = id(sem)
            if k not in wd or wd[k][1] < val:
                wd[k] = (sem, val)
        incs = []
        toks = {}
        for c in inc:
            n = 16 if dma else 1
            c.count += n
            incs.append((c.sem, n))
            toks[c] = c.tok()
        self.ops.append((engine, fn, list(wd.values()), incs))
        return toks

    def join(self):
        toks = [(c.sem, c.count) for c in self.chans.values() if c.count > 0]
        for e in self.ENGINES:
            self.ops.append((e, None, toks, []))

    def emit(self):
        nc = self.nc
        with nc.Block() as block:
            for eng_name in self.ENGINES:
                section = getattr(block, eng_name)

                def make(name):
                    def body(eng):
                        for engine, fn, waits, incs in self.ops:
                            if engine != name:
                                continue
                            for sem, val in waits:
                                eng.wait_ge(sem, val)
                            inst = fn(eng) if fn is not None else None
                            if incs:
                                if inst is None:
                                    inst = eng.nop()
                                for sem, n in incs:
                                    inst.then_inc(sem, n)
                    return body

                section(make(eng_name))


def build_program(upto=4, dbg=False):
    nc = bass.Bass("TRN2", target_bir_lowering=False)
    stack = ExitStack()
    P = Prog(nc, stack)
    ld, st = P.chan("ld"), P.chan("st")
    pe, dve, act, gp = P.chan("pe"), P.chan("dve"), P.chan("act"), P.chan("gp")
    dA = [P.chan("dA0"), P.chan("dA1")]
    dB = [P.chan("dB0"), P.chan("dB1")]
    dW = P.chan("dW")
    sA = [P.chan("sA0"), P.chan("sA1")]

    dram = lambda n, sh, dt, k="ExternalInput": nc.dram_tensor(n, sh, dt, kind=k).ap()
    x_in = dram("x", [BPC, S, D], F32)
    out = dram("out", [BPC, S, D], F32, "ExternalOutput")
    ident = dram("ident", [128, 128], BF16)
    qWT = dram("qWT", [D, D], F32)
    kWT = dram("kWT", [D, D], F32)
    vWT = dram("vWT", [D, D], F32)
    oWT = dram("oWT", [D, D], F32)
    qbT = dram("qbT", [128, 8], F32)
    kbT = dram("kbT", [128, 8], F32)
    vb_bc = dram("vb_bc", [128, D], F32)
    ob_bc = dram("ob_bc", [128, D], F32)
    skT = dram("skT", [128, 8, 8], F32)
    wgT = dram("wgT", [D, FF], F32)
    wuT = dram("wuT", [D, FF], F32)
    wdT = dram("wdT", [FF, D], F32)
    bWq = dram("bWq", [D, D], F32)
    bWk = dram("bWk", [D, D], F32)
    bWv = dram("bWv", [D, D], F32)
    bWo = dram("bWo", [D, D], F32)
    bW1 = dram("bW1", [D, FF], F32)
    bW2 = dram("bW2", [FF, D], F32)
    bia = {nm: dram(nm, [128, FF if nm == "b1_bc" else D], F32)
           for nm in ("bq_bc", "bk_bc", "bv_bc", "bo_bc", "b1_bc", "b2_bc",
                      "g1_bc", "be1_bc", "g2_bc", "be2_bc")}

    xres = dram("xres", [BPC, S, D], F32, "Internal")
    ao_scr = dram("ao_scr", [BPC, S, D], F32, "ExternalOutput" if dbg else "Internal")
    if dbg:
        qn_dbg = dram("qn_dbg", [BPC, S, D], F32, "ExternalOutput")
        kv_dbg = dram("kv_dbg", [BPC, 128, 512], F32, "ExternalOutput")
        ctx_dbg = dram("ctx_dbg", [BPC, S, D], F32, "ExternalOutput")
    tq_scr = dram("tq_scr", [BPC, S, FF], BF16,
                  "ExternalOutput" if dbg == "ffn" else "Internal")
    if dbg == "ffn":
        dq_x = dram("dq_x", [BPC, S, D], F32, "ExternalOutput")
        dq_xq = dram("dq_xq", [BPC, S, D], BF16, "ExternalOutput")
        dq_gu = dram("dq_gu", [BPC, S, FF], F32, "ExternalOutput")
    wq_scr = {"g": dram("wgq_scr", [D, FF], BF16, "Internal"),
              "u": dram("wuq_scr", [D, FF], BF16, "Internal"),
              "d": dram("wdq_scr", [FF, D], BF16, "Internal")}

    sb = lambda stk, n, sh, dt: stk.enter_context(nc.sbuf_tensor(n, sh, dt))
    pst = lambda stk, n, sh, dt=F32: stk.enter_context(nc.psum_tensor(n, sh, dt))

    id_t = sb(stack, "id_t", [128, 128], BF16)
    P.op("sync", lambda e: e.dma_start(out=id_t[:], in_=ident), inc=[ld], dma=True)
    stat = sb(stack, "stat", [128, 32], F32)
    w_scales = sb(stack, "w_scales", [128, 4], F32)
    stq = sb(stack, "stq", [128, 2 * NT], F32)
    junk = sb(stack, "junk", [128, 2 * D], BF16)
    eps5 = sb(stack, "eps5", [128, 1], F32)
    eps12 = sb(stack, "eps12", [128, 1], F32)

    P.op("vector", lambda e: e.memset(eps5[:], 1e-5), inc=[dve])
    P.op("vector", lambda e: e.memset(eps12[:], 1e-12), inc=[dve])

    big_ps = pst(stack, "big_ps", [128, 2048])
    o_ps = pst(stack, "o_ps", [128, 1024])
    tr_ps = pst(stack, "tr_ps", [128, 2048], BF16)

    def emit_transposes(src_bf, n_chunks, dst, dst_fn, extra_first=()):
        for m in range(n_chunks):
            g4 = m // 4
            bank = g4 % 2
            aft = [pe]
            if m % 4 == 0:
                aft = [pe, act, dve, gp] + list(extra_first)
            P.op("tensor", lambda e, m=m, bank=bank: e.transpose(
                tr_ps[:, bank * 1024 + (m % 4) * 128: bank * 1024 + (m % 4) * 128 + 128],
                src_bf[:, m * 128:(m + 1) * 128], id_t[:]), after=aft, inc=[pe])
            if m % 4 == 3:
                P.op("scalar", lambda e, g4=g4, bank=bank: e.copy(
                    dst_fn(g4),
                    tr_ps[:, bank * 1024: bank * 1024 + 512]
                    .rearrange("p (a b) -> p a b", a=4)),
                    after=[pe, act], inc=[act])

    def emit_ln(in_ap, out_ap, width, eps, dep=()):
        eps_ap = (eps5 if eps > 1e-9 else eps12)[:]
        inv = 1.0 / width
        P.op("vector", lambda e: e.tensor_scalar(
            out=junk[:, 0:width], in0=in_ap, scalar1=inv, scalar2=0.0, op0=ALU.mult,
            op1=ALU.add, accum_out=stat[:, 0:1]), after=(dve, act) + tuple(dep), inc=[dve])
        sq = float(np.sqrt(inv))
        P.op("scalar", lambda e: e.activation(
            out=junk[:, D:D + width], in_=in_ap, func=AF.Square, bias=0.0, scale=sq,
            accum_out=stat[:, 1:2]), after=(act, dve) + tuple(dep), inc=[act])
        P.op("vector", lambda e: e.scalar_tensor_tensor(
            out=stat[:, 3:4], in0=stat[:, 0:1], scalar=-1.0, in1=stat[:, 0:1],
            op0=ALU.mult, op1=ALU.mult), after=(dve, act), inc=[dve])
        P.op("vector", lambda e: e.tensor_tensor(
            out=stat[:, 1:2], in0=stat[:, 1:2], in1=stat[:, 3:4], op=ALU.add),
            after=(dve,), inc=[dve])
        P.op("scalar", lambda e: e.activation(
            out=stat[:, 2:3], in_=stat[:, 1:2], func=AF.Ln, bias=eps_ap, scale=1.0),
            after=(act, dve), inc=[act])
        P.op("scalar", lambda e: e.activation(
            out=stat[:, 2:3], in_=stat[:, 2:3], func=AF.Exp, bias=0.0, scale=-0.5),
            after=(act,), inc=[act])
        return P.op("vector", lambda e: e.tensor_scalar(
            out=out_ap, in0=in_ap, scalar1=stat[:, 0:1], scalar2=stat[:, 2:3],
            op0=ALU.subtract, op1=ALU.mult), after=(dve, act, pe, gp), inc=[dve])

    def ln_t(in_ap, out_ap, width, eps_ap, lnA, prev, dep_dve=(), dep_act=(),
             dep_out=()):
        inv = 1.0 / width
        sq = float(np.sqrt(inv))
        P.op("vector", lambda e: e.tensor_scalar(
            out=junk[:, 0:width], in0=in_ap, scalar1=inv, scalar2=0.0, op0=ALU.mult,
            op1=ALU.add, accum_out=lnA[:, 0:1]), after=tuple(dep_dve), inc=[dve])
        t_sq = P.op("scalar", lambda e: e.activation(
            out=junk[:, D:D + width], in_=in_ap, func=AF.Square, bias=0.0, scale=sq,
            accum_out=lnA[:, 1:2]),
            after=tuple(dep_act) + (prev.get("tt"),), inc=[act])[act]
        P.op("vector", lambda e: e.scalar_tensor_tensor(
            out=lnA[:, 3:4], in0=lnA[:, 0:1], scalar=-1.0, in1=lnA[:, 0:1],
            op0=ALU.mult, op1=ALU.mult), after=[dve], inc=[dve])
        t_tt = P.op("vector", lambda e: e.tensor_tensor(
            out=lnA[:, 1:2], in0=lnA[:, 1:2], in1=lnA[:, 3:4], op=ALU.add),
            after=(t_sq, dve), inc=[dve])[dve]
        P.op("scalar", lambda e: e.activation(
            out=lnA[:, 2:3], in_=lnA[:, 1:2], func=AF.Ln, bias=eps_ap, scale=1.0),
            after=(t_tt, prev.get("fin")), inc=[act])
        t_exp = P.op("scalar", lambda e: e.activation(
            out=lnA[:, 2:3], in_=lnA[:, 2:3], func=AF.Exp, bias=0.0, scale=-0.5),
            after=[act], inc=[act])[act]
        t_fin = P.op("vector", lambda e: e.tensor_scalar(
            out=out_ap, in0=in_ap, scalar1=lnA[:, 0:1], scalar2=lnA[:, 2:3],
            op0=ALU.subtract, op1=ALU.mult),
            after=(t_exp, dve) + tuple(dep_out), inc=[dve])[dve]
        prev["tt"] = t_tt
        prev["fin"] = t_fin
        return {"fin": t_fin, "sq": t_sq}

    def conv_w(dst_b, src, wstage, rows, cols):
        for c in range(rows // 128):
            P.op("sync", lambda e, c=c: e.dma_start(
                out=wstage[:, 0:cols], in_=src[c * 128:(c + 1) * 128, :]),
                after=[dve], inc=[ld], dma=True)
            P.op("vector", lambda e, c=c: e.tensor_copy(dst_b[:, c, :], wstage[:, 0:cols]),
                 after=[ld, pe, act], inc=[dve])

    p0 = ExitStack()
    wstage0 = sb(p0, "wstage0", [128, FF], F32)
    qi8 = sb(p0, "qi8", [128, FF], I8)
    qbf = sb(p0, "qbf", [128, FF], BF16)
    ones_col = sb(p0, "ones_col", [128, 1], F32)
    ones_row = sb(p0, "ones_row", [1, 128], F32)
    P.op("vector", lambda e: e.memset(ones_col[:], 1.0), inc=[dve])
    P.op("vector", lambda e: e.memset(ones_row[:], 1.0), inc=[dve])

    def quant_weight(src, rows, cols, scr, scol):
        nch = rows // 128
        acc = stat[:, 8:9]
        red = stat[:, 9:10]
        P.op("vector", lambda e: e.memset(acc, 0.0), after=[dve], inc=[dve])
        for c in range(nch):
            P.op("sync", lambda e, c=c: e.dma_start(
                out=wstage0[:, 0:cols], in_=src[c * 128:(c + 1) * 128, :]),
                after=[dve], inc=[ld], dma=True)
            P.op("vector", lambda e: e.reduce_sum(out=red, in_=wstage0[:, 0:cols],
                                                  axis=AX.X, apply_absolute_value=True),
                 after=[ld, dve], inc=[dve])
            P.op("vector", lambda e: e.tensor_tensor(out=acc, in0=acc, in1=red, op=ALU.add),
                 after=[dve], inc=[dve])
        P.op("tensor", lambda e: e.matmul(big_ps[0:1, 0:1], acc, ones_col[:],
                                          start=True, stop=True),
             after=[dve, pe, act], inc=[pe])
        P.op("vector", lambda e: e.tensor_scalar(
            out=stat[0:1, 10:11], in0=big_ps[0:1, 0:1], scalar1=1.0 / (rows * cols),
            scalar2=1e-5, op0=ALU.mult, op1=ALU.max), after=[pe, dve], inc=[dve])
        P.op("tensor", lambda e: e.matmul(big_ps[:, 1:2], ones_row[:], stat[0:1, 10:11],
                                          start=True, stop=True),
             after=[dve, pe], inc=[pe])
        P.op("vector", lambda e, scol=scol: e.tensor_copy(w_scales[:, scol:scol + 1],
                                                          big_ps[:, 1:2]),
             after=[pe, dve], inc=[dve])
        P.op("vector", lambda e: e.reciprocal(out=stat[:, 11:12],
                                              in_=w_scales[:, scol:scol + 1]),
             after=[dve], inc=[dve])
        for c in range(nch):
            P.op("sync", lambda e, c=c: e.dma_start(
                out=wstage0[:, 0:cols], in_=src[c * 128:(c + 1) * 128, :]),
                after=[dve, st], inc=[ld], dma=True)
            P.op("vector", lambda e: e.tensor_scalar(
                out=qi8[:, 0:cols], in0=wstage0[:, 0:cols], scalar1=stat[:, 11:12],
                scalar2=None, op0=ALU.mult), after=[ld, dve, st], inc=[dve])
            P.op("vector", lambda e: e.tensor_copy(qbf[:, 0:cols], qi8[:, 0:cols]),
                 after=[dve], inc=[dve])
            P.op("vector", lambda e: e.tensor_scalar(
                out=qbf[:, 0:cols], in0=qbf[:, 0:cols], scalar1=1.0, scalar2=-1.0,
                op0=ALU.min, op1=ALU.max), after=[dve], inc=[dve])
            P.op("sync", lambda e, c=c: e.dma_start(
                out=scr[c * 128:(c + 1) * 128, :], in_=qbf[:, 0:cols]),
                after=[dve], inc=[st], dma=True)

    quant_weight(wgT, D, FF, wq_scr["g"], 0)
    quant_weight(wuT, D, FF, wq_scr["u"], 1)
    quant_weight(wdT, FF, D, wq_scr["d"], 2)
    P.join()
    p0.close()

    def emit_attn_a(dst):
        p1 = ExitStack()
        wstage = junk[:].bitcast(F32)
        w0_b = sb(p1, "w0_b", [128, 8, D], BF16)
        kWT_b = sb(p1, "kWT_b", [128, 8, D], BF16)
        vWT_b = sb(p1, "vWT_b", [128, 8, D], BF16)
        qbT_t = sb(p1, "qbT_t", [128, 8], F32)
        kbT_t = sb(p1, "kbT_t", [128, 8], F32)
        skT_t = sb(p1, "skT_t", [128, 8, 8], F32)
        vb_t = sb(p1, "vb_t", [128, D], BF16)
        ob_t = sb(p1, "ob_t", [128, D], F32)
        QT = sb(p1, "QT", [128, 8, S], BF16)
        KT = sb(p1, "KT", [128, 8, S], BF16)
        Vt = sb(p1, "Vt", [128, NT, D], BF16)
        xt_f = [sb(p1, f"xt_f{i}", [128, D], F32) for i in range(2)]
        xhat_b = sb(p1, "xhat_b", [128, D], BF16)
        PTb = sb(p1, "PTb", [128, NT, 256], BF16)
        xTst = PTb[:].rearrange("p a b -> p (a b)").rearrange("p (a b) -> p a b", a=8)
        e_sb = sb(p1, "e_sb", [128, S], F32)
        m_sb = sb(p1, "m_sb", [128, S], F32)
        e2_sb = m_sb
        p_sb = [sb(p1, f"p_sb{i}", [128, S], BF16) for i in range(2)]
        tb16 = sb(p1, "tb16", [128, 16], F32)
        ctxT = sb(p1, "ctxT", [128, 8, 256], BF16)
        o_f = e_sb

        conv_w(kWT_b, kWT, wstage, D, D)
        conv_w(vWT_b, vWT, wstage, D, D)
        P.op("sync", lambda e: e.dma_start(out=wstage[:, 0:D], in_=vb_bc),
             after=[dve], inc=[ld], dma=True)
        P.op("vector", lambda e: e.tensor_copy(vb_t[:], wstage[:, 0:D]),
             after=[ld], inc=[dve])
        P.op("sync", lambda e: e.dma_start(out=qbT_t[:], in_=qbT), inc=[ld], dma=True)
        P.op("sync", lambda e: e.dma_start(out=kbT_t[:], in_=kbT), inc=[ld], dma=True)
        P.op("sync", lambda e: e.dma_start(out=skT_t[:], in_=skT), inc=[ld], dma=True)
        P.op("sync", lambda e: e.dma_start(out=ob_t[:], in_=ob_bc), inc=[ld], dma=True)
        P.op("vector", lambda e: e.tensor_scalar(out=qbT_t[:], in0=qbT_t[:], scalar1=0.125,
                                                 scalar2=None, op0=ALU.mult),
             after=[ld, dve], inc=[dve])

        for b in range(BPC):
            conv_w(w0_b, qWT, wstage, D, D)
            for sst in range(4):
                for sub in range(4):
                    tt = sst * 4 + sub
                    xt = xt_f[tt % 2]
                    P.op("sync", lambda e, b=b, tt=tt, xt=xt: e.dma_start(
                        out=xt[:], in_=x_in[b, tt * 128:(tt + 1) * 128, :]),
                        after=[dve, act], inc=[ld], dma=True)
                    emit_ln(xt[:], xhat_b[:], D, 1e-5, dep=(ld,))
                    emit_transposes(
                        xhat_b, 8, None,
                        lambda g4, sub=sub: xTst[:, g4 * 4:(g4 + 1) * 4,
                                                 sub * 128:(sub + 1) * 128])
                for WTb, bias, scale, DST in ((w0_b, qbT_t, 0.125, QT),
                                              (kWT_b, kbT_t, 1.0, KT)):
                    for m in range(8):
                        ps = big_ps[:, (m % 2) * 512:(m % 2) * 512 + 512]
                        for k in range(8):
                            aft = [act, pe, dve] if k == 0 else [pe]
                            P.op("tensor", lambda e, m=m, k=k, WTb=WTb, ps=ps: e.matmul(
                                ps, WTb[:, k, m * 128:(m + 1) * 128], xTst[:, k, :],
                                start=(k == 0), stop=(k == 7)), after=aft, inc=[pe])
                        P.op("scalar", lambda e, m=m, ps=ps, DST=DST, bias=bias,
                             scale=scale, sst=sst: e.activation(
                                 out=DST[:, m, sst * 512:(sst + 1) * 512], in_=ps,
                                 func=AF.Identity, bias=bias[:, m:m + 1], scale=scale),
                             after=[pe, act], inc=[act])
                for sub in range(4):
                    chunk = sst * 4 + sub
                    for n in range(2):
                        ps = big_ps[:, 1024 + n * 512:1024 + n * 512 + 512]
                        for k in range(8):
                            aft = [dve, pe, act] if k == 0 else [pe]
                            P.op("tensor", lambda e, sub=sub, n=n, k=k, ps=ps: e.matmul(
                                ps, xTst[:, k, sub * 128:(sub + 1) * 128],
                                vWT_b[:, k, n * 512:(n + 1) * 512],
                                start=(k == 0), stop=(k == 7)), after=aft, inc=[pe])
                        P.op("vector", lambda e, chunk=chunk, n=n, ps=ps: e.tensor_tensor(
                            out=Vt[:, chunk, n * 512:(n + 1) * 512], in0=ps,
                            in1=vb_t[:, n * 512:(n + 1) * 512], op=ALU.add),
                            after=[pe, dve], inc=[dve])
            P.op("vector", lambda e: e.tensor_copy(
                KT[:, :, 0:8], skT_t[:]), after=[dve, act, pe], inc=[dve])

            conv_w(w0_b, oWT, wstage, D, D)
            oWT_b = w0_b
            pv_tok = None
            for g in range(8):
                for h in range(H):
                    hrow = (h % 2) * 64
                    hm = h // 2
                    for qt in range(2):
                        qg = g * 2 + qt
                        for n in range(4):
                            aft = [act, pe, dve] if n == 0 else [pe]
                            P.op("tensor", lambda e, hrow=hrow, hm=hm, qg=qg, n=n: e.matmul(
                                big_ps[:, n * 512:(n + 1) * 512],
                                QT[hrow:hrow + 64, hm, qg * 128:(qg + 1) * 128],
                                KT[hrow:hrow + 64, hm, n * 512:(n + 1) * 512],
                                start=True, stop=True), after=aft, inc=[pe])
                        P.op("scalar", lambda e: e.activation(
                            out=e_sb[:], in_=big_ps[:], func=AF.Exp, bias=0.0, scale=1.0),
                            after=[pe, act, dve, gp], inc=[act])
                        P.op("vector", lambda e: e.max(out=tb16[:, 0:8], in_=e_sb[:]),
                             after=[act, dve], inc=[dve])
                        P.op("vector", lambda e: e.match_replace(
                            out=e2_sb[:], in_to_replace=tb16[:, 0:8], in_values=e_sb[:],
                            imm_value=0.0), after=[dve, gp], inc=[dve])
                        P.op("vector", lambda e: e.max(out=tb16[:, 8:16], in_=e2_sb[:]),
                             after=[dve], inc=[dve])
                        P.op("vector", lambda e: e.reduce_sum(
                            out=stat[:, 4:5], in_=tb16[:], axis=AX.X),
                            after=[dve], inc=[dve])
                        P.op("vector", lambda e: e.reciprocal(out=stat[:, 5:6],
                                                              in_=stat[:, 4:5]),
                             after=[dve], inc=[dve])
                        P.op("vector", lambda e: e.tensor_scalar(
                            out=m_sb[:], in0=e_sb[:], scalar1=tb16[:, 15:16],
                            scalar2=stat[:, 5:6], op0=ALU.is_ge, op1=ALU.mult),
                            after=[dve, gp], inc=[dve])
                        pbuf = p_sb[qt % 2]
                        P.op("gpsimd", lambda e, pbuf=pbuf: e.tensor_tensor(
                            out=pbuf[:], in0=m_sb[:], in1=e_sb[:], op=ALU.mult),
                            after=[dve, gp, pe], inc=[gp])
                        extra = [pv_tok[pe]] if pv_tok is not None else []
                        emit_transposes(
                            pbuf, 16, None,
                            lambda g4, qt=qt: PTb[:, g4 * 4:(g4 + 1) * 4,
                                                  qt * 128:(qt + 1) * 128],
                            extra_first=extra)
                    for cc in range(NT):
                        aft = [act, pe] if cc == 0 else [pe]
                        P.op("tensor", lambda e, cc=cc, hrow=hrow, h=h: e.matmul(
                            o_ps[hrow:hrow + 64, 0:256], Vt[:, cc, h * 64:(h + 1) * 64],
                            PTb[:, cc, :], start=(cc == 0), stop=(cc == 15)),
                            after=aft, inc=[pe])
                    pv_tok = P.op("tensor", lambda e: e.nop(), after=[pe], inc=[pe])
                    if h % 2 == 1:
                        P.op("scalar", lambda e, hm=hm: e.copy(ctxT[:, hm, :],
                                                               o_ps[:, 0:256]),
                             after=[pe, act], inc=[act])
                for qt in range(2):
                    qg = g * 2 + qt
                    for n in range(2):
                        for k in range(8):
                            aft = [act, pe, dve] if k == 0 else [pe]
                            P.op("tensor", lambda e, n=n, k=k, qt=qt: e.matmul(
                                o_ps[:, n * 512:(n + 1) * 512],
                                ctxT[:, k, qt * 128:(qt + 1) * 128],
                                oWT_b[:, k, n * 512:(n + 1) * 512],
                                start=(k == 0), stop=(k == 7)), after=aft, inc=[pe])
                    xt = xt_f[qt % 2]
                    P.op("sync", lambda e, b=b, qg=qg, xt=xt: e.dma_start(
                        out=xt[:], in_=x_in[b, qg * 128:(qg + 1) * 128, :]),
                        after=[gp, dve], inc=[ld], dma=True)
                    P.op("gpsimd", lambda e, xt=xt: e.tensor_tensor(
                        out=xt[:], in0=xt[:], in1=ob_t[:], op=ALU.add),
                        after=[ld, gp], inc=[gp])
                    P.op("scalar", lambda e: e.copy(o_f[:, 0:D], o_ps[:]),
                         after=[pe, act, dve, gp], inc=[act])
                    P.op("vector", lambda e, xt=xt: e.tensor_tensor(
                        out=xt[:], in0=xt[:], in1=o_f[:, 0:D], op=ALU.add),
                        after=[act, gp, dve], inc=[dve])
                    P.op("sync", lambda e, b=b, qg=qg, xt=xt: e.dma_start(
                        out=dst[b, qg * 128:(qg + 1) * 128, :], in_=xt[:]),
                        after=[dve], inc=[st], dma=True)
        P.join()
        p1.close()

    emit_attn_a(out if upto == 1 else xres)
    if upto == 1:
        P.join()
        P.emit()
        return nc, stack

    def emit_ffn(src, dst, tag):
        TIL = BPC * NT
        f1 = ExitStack()
        wgq_b = sb(f1, f"wgq_b{tag}", [128, 8, FF], BF16)
        wuq_b = sb(f1, f"wuq_b{tag}", [128, 8, FF], BF16)
        xt_f = [sb(f1, f"fx{tag}{i}", [128, D], F32) for i in range(2)]
        xhat_f = sb(f1, f"xhat_f{tag}", [128, D], F32)
        xq_i8 = sb(f1, f"xq_i8{tag}", [128, D], I8)
        xq_bf = [sb(f1, f"xq_bf{tag}{i}", [128, D], BF16) for i in range(2)]
        xqT = sb(f1, f"xqT{tag}", [128, 8, 128], BF16)
        g_f = sb(f1, f"g_f{tag}", [128, FF], F32)
        u_f = sb(f1, f"u_f{tag}", [128, FF], F32)
        tq_i8 = sb(f1, f"tq_i8{tag}", [128, FF], I8)
        tq_bf = [sb(f1, f"tq_bf{tag}{i}", [128, FF], BF16) for i in range(2)]
        qst = sb(f1, f"qst{tag}", [128, 16], F32)
        lnA = sb(f1, f"lnA{tag}", [128, 2, 4], F32)
        for c in range(8):
            P.op("sync", lambda e, c=c: e.dma_start(
                out=wgq_b[:, c, :], in_=wq_scr["g"][c * 128:(c + 1) * 128, :]),
                inc=[dW], dma=True)
            wtok_all = P.op("sync", lambda e, c=c: e.dma_start(
                out=wuq_b[:, c, :], in_=wq_scr["u"][c * 128:(c + 1) * 128, :]),
                inc=[dW], dma=True)[dW]
        xt_free = [(), ()]
        ld_tok = [None, None]
        bank_tok = [None, None, None, None]
        trb_tok = [None, None]
        tq_out_tok = [None, None]
        gu_tok = None
        gf_done = None
        quant_tok = [None, None]
        sc_tok = [None, None]
        transp_tok = [None, None]
        up7_tok = [None, None]
        ln_prev = [{}, {}]

        def stageA(t):
            b, tt = divmod(t, NT)
            p = t % 2
            c0 = p * 8
            ld_tok[p] = P.op("sync", lambda e, b=b, tt=tt, p=p: e.dma_start(
                out=xt_f[p][:], in_=src[b, tt * 128:(tt + 1) * 128, :]),
                after=xt_free[p], inc=[dA[p]], dma=True)[dA[p]]
            r = ln_t(xt_f[p][:], xhat_f[:], D, eps5[:], lnA[:, p, :], ln_prev[p],
                     dep_dve=(ld_tok[p],), dep_act=(ld_tok[p],))
            xt_free[p] = (r["fin"], r["sq"])
            if dbg == "ffn" and tag == 0:
                P.op("sync", lambda e, b=b, tt=tt: e.dma_start(
                    out=dq_x[b, tt * 128:(tt + 1) * 128, :], in_=xhat_f[:]),
                    after=(r["fin"],), inc=[st], dma=True)
            P.op("vector", lambda e, c0=c0: e.reduce_max(
                out=qst[:, c0:c0 + 1], in_=xhat_f[:], axis=AX.X,
                apply_absolute_value=True), after=[dve], inc=[dve])
            P.op("vector", lambda e, c0=c0: e.tensor_scalar(
                out=qst[:, c0:c0 + 1], in0=qst[:, c0:c0 + 1], scalar1=1e-5,
                scalar2=None, op0=ALU.max), after=[dve], inc=[dve])
            P.op("vector", lambda e, c0=c0: e.reciprocal(
                out=qst[:, c0 + 1:c0 + 2], in_=qst[:, c0:c0 + 1]),
                after=[dve], inc=[dve])
            P.op("vector", lambda e, c0=c0: e.tensor_scalar(
                out=qst[:, c0 + 1:c0 + 2], in0=qst[:, c0 + 1:c0 + 2], scalar1=127.0,
                scalar2=None, op0=ALU.mult), after=[dve], inc=[dve])
            P.op("vector", lambda e, c0=c0: e.tensor_scalar(
                out=xq_i8[:], in0=xhat_f[:], scalar1=qst[:, c0 + 1:c0 + 2],
                scalar2=None, op0=ALU.mult), after=[dve], inc=[dve])
            quant_tok[p] = P.op("vector", lambda e, p=p: e.tensor_copy(
                xq_bf[p][:], xq_i8[:]), after=(transp_tok[p], dve), inc=[dve])[dve]
            if dbg == "ffn" and tag == 0:
                P.op("sync", lambda e, b=b, tt=tt, p=p: e.dma_start(
                    out=dq_xq[b, tt * 128:(tt + 1) * 128, :], in_=xq_bf[p][:]),
                    after=(quant_tok[p],), inc=[st], dma=True)
            P.op("vector", lambda e, c0=c0: e.tensor_scalar(
                out=qst[:, c0 + 4:c0 + 5], in0=qst[:, c0:c0 + 1],
                scalar1=1.0 / 127.0, scalar2=None, op0=ALU.mult),
                after=[dve], inc=[dve])
            P.op("vector", lambda e, c0=c0: e.tensor_tensor(
                out=qst[:, c0 + 2:c0 + 3], in0=qst[:, c0 + 4:c0 + 5],
                in1=w_scales[:, 0:1], op=ALU.mult),
                after=(up7_tok[p], dve), inc=[dve])
            sc_tok[p] = P.op("vector", lambda e, c0=c0: e.tensor_tensor(
                out=qst[:, c0 + 3:c0 + 4], in0=qst[:, c0 + 4:c0 + 5],
                in1=w_scales[:, 1:2], op=ALU.mult), after=[dve], inc=[dve])[dve]

        def stageB(t):
            nonlocal transp_tok
            p = t % 2
            c0 = p * 8
            trstop = [None, None]
            ev = [None, None]
            for m in range(8):
                bank = m // 4
                aft = []
                if m == 0:
                    aft.append(quant_tok[p])
                if m % 4 == 0:
                    aft.append(trb_tok[bank])
                tok = P.op("tensor", lambda e, m=m, bank=bank, p=p: e.transpose(
                    tr_ps[:, bank * 1024 + (m % 4) * 128:
                          bank * 1024 + (m % 4) * 128 + 128],
                    xq_bf[p][:, m * 128:(m + 1) * 128], id_t[:]),
                    after=aft, inc=[pe])[pe]
                if m % 4 == 3:
                    trstop[bank] = tok
            transp_tok[p] = tok
            for bank in range(2):
                ev[bank] = trb_tok[bank] = P.op("scalar", lambda e, bank=bank: e.copy(
                    xqT[:, bank * 4:(bank + 1) * 4, :],
                    tr_ps[:, bank * 1024: bank * 1024 + 512]
                    .rearrange("p (a b) -> p a b", a=4)),
                    after=(trstop[bank],), inc=[act])[act]
            for which, wq_b, dstf, scol, fn in (
                    (0, wgq_b, g_f, 2, AF.Silu), (1, wuq_b, u_f, 3, AF.Copy)):
                for n in range(8):
                    bk = (which * 8 + n) % 4
                    ps = big_ps[:, bk * 512:bk * 512 + 512]
                    stop = None
                    for k in range(8):
                        aft = []
                        if k == 0:
                            aft.append(bank_tok[bk])
                            if which == 0 and n == 0:
                                aft.append(ev[0])
                        if k == 4 and which == 0 and n == 0:
                            aft.append(ev[1])
                        if t == 0 and which == 0 and n == 0 and k == 0:
                            aft.append(wtok_all)
                        stop = P.op("tensor", lambda e, n=n, k=k, wq_b=wq_b, ps=ps:
                                    e.matmul(ps, xqT[:, k, :],
                                             wq_b[:, k, n * 512:(n + 1) * 512],
                                             start=(k == 0), stop=(k == 7)),
                                    after=aft, inc=[pe])[pe]
                    aft = [stop, sc_tok[p]]
                    if which == 0 and n == 0:
                        aft.append(gf_done)
                    bank_tok[bk] = P.op("scalar", lambda e, n=n, ps=ps, dstf=dstf,
                                        scol=scol, fn=fn, c0=c0: e.activation(
                        out=dstf[:, n * 512:(n + 1) * 512], in_=ps, func=fn, bias=0.0,
                        scale=qst[:, c0 + scol:c0 + scol + 1]),
                        after=aft, inc=[act])[act]
                    if which == 1 and n == 7:
                        up7_tok[p] = bank_tok[bk]

        def stageC(t):
            nonlocal gu_tok, gf_done
            b, tt = divmod(t, NT)
            p = t % 2
            c0 = p * 8
            gu_tok = P.op("vector", lambda e: e.tensor_tensor(
                out=g_f[:], in0=g_f[:], in1=u_f[:], op=ALU.mult),
                after=(up7_tok[p],), inc=[dve])[dve]
            if dbg == "ffn" and tag == 0:
                P.op("sync", lambda e, b=b, tt=tt: e.dma_start(
                    out=dq_gu[b, tt * 128:(tt + 1) * 128, :], in_=g_f[:]),
                    after=(gu_tok,), inc=[st], dma=True)
            P.op("vector", lambda e, c0=c0: e.reduce_max(
                out=qst[:, c0 + 5:c0 + 6], in_=g_f[:], axis=AX.X,
                apply_absolute_value=True), after=[dve], inc=[dve])
            P.op("vector", lambda e, c0=c0: e.tensor_scalar(
                out=qst[:, c0 + 5:c0 + 6], in0=qst[:, c0 + 5:c0 + 6], scalar1=1e-5,
                scalar2=None, op0=ALU.max), after=[dve], inc=[dve])
            P.op("vector", lambda e, b=b, tt=tt, c0=c0: e.tensor_scalar(
                out=stq[:, b * NT + tt:b * NT + tt + 1], in0=qst[:, c0 + 5:c0 + 6],
                scalar1=1.0 / 127.0, scalar2=None, op0=ALU.mult),
                after=[dve], inc=[dve])
            P.op("vector", lambda e, c0=c0: e.reciprocal(
                out=qst[:, c0 + 6:c0 + 7], in_=qst[:, c0 + 5:c0 + 6]),
                after=[dve], inc=[dve])
            P.op("vector", lambda e, c0=c0: e.tensor_scalar(
                out=qst[:, c0 + 6:c0 + 7], in0=qst[:, c0 + 6:c0 + 7], scalar1=127.0,
                scalar2=None, op0=ALU.mult), after=[dve], inc=[dve])
            gf_done = P.op("vector", lambda e, c0=c0: e.tensor_scalar(
                out=tq_i8[:], in0=g_f[:], scalar1=qst[:, c0 + 6:c0 + 7],
                scalar2=None, op0=ALU.mult), after=[dve], inc=[dve])[dve]
            cast = P.op("vector", lambda e, p=p: e.tensor_copy(
                tq_bf[p][:], tq_i8[:]), after=(tq_out_tok[p], dve), inc=[dve])[dve]
            tq_out_tok[p] = P.op("sync", lambda e, b=b, tt=tt, p=p: e.dma_start(
                out=tq_scr[b, tt * 128:(tt + 1) * 128, :], in_=tq_bf[p][:]),
                after=(cast,), inc=[sA[p]], dma=True)[sA[p]]

        for i in range(TIL + 1):
            if i < TIL:
                stageA(i)
            if i > 0:
                stageB(i - 1)
                stageC(i - 1)
        P.join()
        f1.close()
        f2 = ExitStack()
        wdq_b = sb(f2, f"wdq_b{tag}", [128, 32, D], BF16)
        tqi = [sb(f2, f"tqi{tag}{i}", [128, FF], BF16) for i in range(2)]
        tqT = sb(f2, f"tqT{tag}", [128, 32, 128], BF16)
        xt2 = [sb(f2, f"fx2{tag}{i}", [128, D], F32) for i in range(2)]
        d_f = sb(f2, f"d_f{tag}", [128, D], F32)
        s2st = sb(f2, f"s2st{tag}", [128, 2], F32)
        for c in range(32):
            wtok2_all = P.op("sync", lambda e, c=c: e.dma_start(
                out=wdq_b[:, c, :], in_=wq_scr["d"][c * 128:(c + 1) * 128, :]),
                inc=[dW], dma=True)[dW]
        ldq_tok = [None, None]
        ldx2_tok = [None, None]
        transp2_tok = [None, None]
        trb2_tok = [None, None]
        st14_tok = [None, None]
        st14_guard = [None, None]
        o_scale_tok = None
        add_prev = None
        add_tok = [None, None]
        out_tok = [None, None]

        def stageA2(t):
            b, tt = divmod(t, NT)
            p = t % 2
            ldq_tok[p] = P.op("sync", lambda e, b=b, tt=tt, p=p: e.dma_start(
                out=tqi[p][:], in_=tq_scr[b, tt * 128:(tt + 1) * 128, :]),
                after=(transp2_tok[p],), inc=[dA[p]], dma=True)[dA[p]]
            ldx2_tok[p] = P.op("sync", lambda e, b=b, tt=tt, p=p: e.dma_start(
                out=xt2[p][:], in_=src[b, tt * 128:(tt + 1) * 128, :]),
                after=(out_tok[p],), inc=[dB[p]], dma=True)[dB[p]]
            st14_tok[p] = P.op("vector", lambda e, b=b, tt=tt, p=p: e.tensor_tensor(
                out=s2st[:, p:p + 1], in0=stq[:, b * NT + tt:b * NT + tt + 1],
                in1=w_scales[:, 2:3], op=ALU.mult),
                after=(st14_guard[p],), inc=[dve])[dve]

        def stageB2(t):
            nonlocal o_scale_tok, add_prev
            b, tt = divmod(t, NT)
            p = t % 2
            ev2 = [None] * 8
            tok = None
            for m in range(32):
                g4 = m // 4
                bank = g4 % 2
                aft = []
                if m == 0:
                    aft.append(ldq_tok[p])
                if m % 4 == 0:
                    aft.append(trb2_tok[bank])
                tok = P.op("tensor", lambda e, m=m, bank=bank, p=p: e.transpose(
                    tr_ps[:, bank * 1024 + (m % 4) * 128:
                          bank * 1024 + (m % 4) * 128 + 128],
                    tqi[p][:, m * 128:(m + 1) * 128], id_t[:]),
                    after=aft, inc=[pe])[pe]
                if m % 4 == 3:
                    ev2[g4] = trb2_tok[bank] = P.op(
                        "scalar", lambda e, g4=g4, bank=bank: e.copy(
                            tqT[:, g4 * 4:(g4 + 1) * 4, :],
                            tr_ps[:, bank * 1024: bank * 1024 + 512]
                            .rearrange("p (a b) -> p a b", a=4)),
                        after=(tok,), inc=[act])[act]
            transp2_tok[p] = tok
            stop = None
            for n in range(2):
                ps = o_ps[:, n * 512:(n + 1) * 512]
                for k in range(32):
                    aft = []
                    if k == 0 and n == 0:
                        aft.append(o_scale_tok)
                    if n == 0 and k % 4 == 0:
                        aft.append(ev2[k // 4])
                    if t == 0 and n == 0 and k == 0:
                        aft.append(wtok2_all)
                    stop = P.op("tensor", lambda e, n=n, k=k, ps=ps: e.matmul(
                        ps, tqT[:, k, :], wdq_b[:, k, n * 512:(n + 1) * 512],
                        start=(k == 0), stop=(k == 31)), after=aft, inc=[pe])[pe]
            o_scale_tok = P.op("scalar", lambda e, p=p: e.activation(
                out=d_f[:], in_=o_ps[:], func=AF.Copy, bias=0.0,
                scale=s2st[:, p:p + 1]),
                after=(stop, st14_tok[p], add_prev), inc=[act])[act]
            st14_guard[p] = o_scale_tok
            add_prev = add_tok[p] = P.op("vector", lambda e, p=p: e.tensor_tensor(
                out=xt2[p][:], in0=xt2[p][:], in1=d_f[:], op=ALU.add),
                after=(o_scale_tok, ldx2_tok[p]), inc=[dve])[dve]
            out_tok[p] = P.op("sync", lambda e, b=b, tt=tt, p=p: e.dma_start(
                out=dst[b, tt * 128:(tt + 1) * 128, :], in_=xt2[p][:]),
                after=(add_tok[p],), inc=[sA[p]], dma=True)[sA[p]]

        for i in range(TIL + 1):
            if i < TIL:
                stageA2(i)
            if i > 0:
                stageB2(i - 1)
        P.join()
        f2.close()

    emit_ffn(xres, out if upto == 2 else xres, 0)
    if upto == 2:
        P.join()
        P.emit()
        return nc, stack

    def emit_attn_b(dst):
        b3 = ExitStack()
        bias_t = {}
        for nm in ("bq_bc", "bk_bc", "bv_bc", "bo_bc", "b2_bc", "g1_bc", "be1_bc",
                   "g2_bc", "be2_bc"):
            bias_t[nm] = sb(b3, nm + "_t", [128, D], F32)
            P.op("sync", lambda e, nm=nm: e.dma_start(out=bias_t[nm][:], in_=bia[nm]),
                 inc=[ld], dma=True)
        xt_f = [sb(b3, f"bx{i}", [128, D], F32) for i in range(2)]
        ao_keep = sb(b3, "ao_keep", [128, NT, D], BF16)
        xh_bf = sb(b3, "xh_bf", [128, D], BF16)
        xhT = sb(b3, "xhT", [128, 8, 128], BF16)
        o_f = sb(b3, "o_f3", [128, D], F32)
        ao_f = sb(b3, "ao_f", [128, D], F32)

        kv_ps = big_ps[:, 1536:2048]

        def emit_elunorm(src_f, dst_bf, tmp_f, eq_f, sq_f, s16):
            P.op("vector", lambda e, src_f=src_f: e.tensor_scalar(
                out=tmp_f[:], in0=src_f[:], scalar1=0.0, scalar2=None, op0=ALU.min),
                after=[dve, act], inc=[dve])
            P.op("scalar", lambda e: e.activation(out=eq_f[:], in_=tmp_f[:], func=AF.Exp,
                                                  bias=0.0, scale=1.0),
                 after=[act, dve], inc=[act])
            P.op("vector", lambda e, src_f=src_f: e.scalar_tensor_tensor(
                out=eq_f[:], in0=src_f[:], scalar=0.0, in1=eq_f[:], op0=ALU.max,
                op1=ALU.add), after=[dve, act], inc=[dve])
            P.op("vector", lambda e: e.tensor_scalar(
                out=eq_f[:], in0=eq_f[:], scalar1=1.0, scalar2=None, op0=ALU.subtract),
                after=[dve], inc=[dve])
            P.op("scalar", lambda e: e.activation(out=sq_f[:], in_=eq_f[:], func=AF.Square,
                                                  bias=0.0, scale=1.0),
                 after=[act, dve], inc=[act])
            P.op("vector", lambda e: e.tensor_reduce(
                out=s16[:], in_=sq_f[:].rearrange("p (h d) -> p h d", h=16),
                axis=AX.X, op=ALU.add), after=[dve, act], inc=[dve])
            P.op("scalar", lambda e: e.activation(out=s16[:], in_=s16[:], func=AF.Ln,
                                                  bias=0.0, scale=1.0),
                 after=[act, dve], inc=[act])
            P.op("scalar", lambda e: e.activation(out=s16[:], in_=s16[:], func=AF.Exp,
                                                  bias=0.0, scale=-0.5),
                 after=[act], inc=[act])
            return P.op("vector", lambda e, dst_bf=dst_bf: e.tensor_tensor(
                out=dst_bf[:].rearrange("p (h d) -> p h d", h=16),
                in0=eq_f[:].rearrange("p (h d) -> p h d", h=16),
                in1=s16[:].unsqueeze(2).to_broadcast([128, 16, 64]), op=ALU.mult),
                after=[dve, act, pe], inc=[dve])

        for b in range(BPC):
            b12 = ExitStack()
            wstage = sb(b12, f"wstage3{b}", [128, D], F32)
            w0b_b = sb(b12, f"w0b_b{b}", [128, 8, D], BF16)
            Wk_b = sb(b12, f"Wk_b{b}", [128, 8, D], BF16)
            Wv_b = sb(b12, f"Wv_b{b}", [128, 8, D], BF16)
            qn_keep = sb(b12, f"qn_keep{b}", [128, NT, D], BF16)
            q_f = sb(b12, f"q_f{b}", [128, D], F32)
            k_f = sb(b12, f"k_f{b}", [128, D], F32)
            v_bf = sb(b12, f"v_bf{b}", [128, D], BF16)
            xh_f = sb(b12, f"xh_f{b}", [128, D], F32)
            tmp_f = sb(b12, f"tmp_f{b}", [128, D], F32)
            eq_f = sb(b12, f"eq_f{b}", [128, D], F32)
            sq_f = sb(b12, f"sq_f{b}", [128, D], F32)
            s16 = sb(b12, f"s16{b}", [128, 16], F32)
            n_bf = sb(b12, f"n_bf{b}", [128, D], BF16)
            kv_bf = sb(b12, f"kv_bf{b}", [128, 512], BF16)
            ctxbT = sb(b12, f"ctxbT{b}", [128, 8, 128], BF16)
            elun = lambda sf, db: emit_elunorm(sf, db, tmp_f, eq_f, sq_f, s16)
            conv_w(Wk_b, bWk, wstage, D, D)
            conv_w(Wv_b, bWv, wstage, D, D)
            conv_w(w0b_b, bWq, wstage, D, D)
            for tt in range(NT):
                xt = xt_f[tt % 2]
                P.op("sync", lambda e, b=b, tt=tt, xt=xt: e.dma_start(
                    out=xt[:], in_=xres[b, tt * 128:(tt + 1) * 128, :]),
                    after=[dve, act], inc=[ld], dma=True)
                emit_ln(xt[:], xh_bf[:], D, 1e-5, dep=(ld,))
                emit_transposes(xh_bf, 8, None,
                                lambda g4: xhT[:, g4 * 4:(g4 + 1) * 4, :])
                for W_b, bias_nm, dstt in ((w0b_b, "bq_bc", q_f), (Wk_b, "bk_bc", k_f),
                                           (Wv_b, "bv_bc", v_bf)):
                    for n in range(2):
                        ps = big_ps[:, n * 512:n * 512 + 512]
                        for k in range(8):
                            aft = [act, pe, dve, gp] if k == 0 else [pe]
                            P.op("tensor", lambda e, n=n, k=k, W_b=W_b, ps=ps: e.matmul(
                                ps, xhT[:, k, :], W_b[:, k, n * 512:(n + 1) * 512],
                                start=(k == 0), stop=(k == 7)), after=aft, inc=[pe])
                    P.op("vector", lambda e, dstt=dstt, bias_nm=bias_nm: e.tensor_tensor(
                        out=dstt[:], in0=big_ps[:, 0:1024], in1=bias_t[bias_nm][:],
                        op=ALU.add), after=[pe, dve], inc=[dve])
                elun(q_f, n_bf)
                P.op("vector", lambda e, tt=tt: e.tensor_copy(qn_keep[:, tt, :], n_bf[:]),
                     after=[dve], inc=[dve])
                if dbg:
                    P.op("vector", lambda e: e.tensor_copy(tmp_f[:], n_bf[:]),
                         after=[dve], inc=[dve])
                    P.op("sync", lambda e, b=b, tt=tt: e.dma_start(
                        out=qn_dbg[b, tt * 128:(tt + 1) * 128, :], in_=tmp_f[:]),
                        after=[dve], inc=[st], dma=True)
                elun(k_f, n_bf)
                for h in range(H):
                    hrow = (h % 2) * 64
                    hm = h // 2
                    aft = [dve, pe] if h == 0 else [pe]
                    P.op("tensor", lambda e, h=h, hrow=hrow, hm=hm, tt=tt: e.matmul(
                        kv_ps[hrow:hrow + 64, hm * 64:(hm + 1) * 64],
                        n_bf[:, h * 64:(h + 1) * 64], v_bf[:, h * 64:(h + 1) * 64],
                        start=(tt == 0 and h < 2), stop=(tt == NT - 1)),
                        after=aft, inc=[pe])
            P.op("scalar", lambda e: e.activation(out=kv_bf[:], in_=kv_ps, func=AF.Copy,
                                                  bias=0.0, scale=0.125),
                 after=[pe, act], inc=[act])
            if dbg:
                P.op("vector", lambda e: e.tensor_copy(xh_f[:, 0:512], kv_bf[:]),
                     after=[act, dve], inc=[dve])
                P.op("sync", lambda e, b=b: e.dma_start(
                    out=kv_dbg[b], in_=xh_f[:, 0:512]), after=[dve], inc=[st], dma=True)
            conv_w(w0b_b, bWo, wstage, D, D)
            Wo_b = w0b_b
            for tt in range(NT):
                xt = xt_f[tt % 2]
                P.op("sync", lambda e, b=b, tt=tt, xt=xt: e.dma_start(
                    out=xt[:], in_=xres[b, tt * 128:(tt + 1) * 128, :]),
                    after=[dve, act], inc=[ld], dma=True)
                emit_ln(xt[:], xh_f[:], D, 1e-5, dep=(ld,))
                P.op("vector", lambda e, tt=tt: e.tensor_copy(xh_bf[:], qn_keep[:, tt, :]),
                     after=[dve, pe], inc=[dve])
                emit_transposes(xh_bf, 8, None,
                                lambda g4: xhT[:, g4 * 4:(g4 + 1) * 4, :])
                for h in range(H):
                    hrow = (h % 2) * 64
                    hm = h // 2
                    aft = [act, pe] if h == 0 else [pe]
                    P.op("tensor", lambda e, hrow=hrow, hm=hm, h=h: e.matmul(
                        big_ps[hrow:hrow + 64, hm * 128:hm * 128 + 128],
                        kv_bf[hrow:hrow + 64, hm * 64:(hm + 1) * 64],
                        xhT[hrow:hrow + 64, h // 2, :],
                        start=True, stop=True), after=aft, inc=[pe])
                P.op("scalar", lambda e: e.copy(
                    ctxbT[:].rearrange("p a b -> p (a b)"), big_ps[:, 0:1024]),
                    after=[pe, act], inc=[act])
                for n in range(2):
                    for k in range(8):
                        aft = [act, pe, dve] if k == 0 else [pe]
                        P.op("tensor", lambda e, n=n, k=k: e.matmul(
                            o_ps[:, n * 512:(n + 1) * 512], ctxbT[:, k, :],
                            Wo_b[:, k, n * 512:(n + 1) * 512],
                            start=(k == 0), stop=(k == 7)), after=aft, inc=[pe])
                P.op("scalar", lambda e: e.copy(o_f[:], o_ps[:]),
                     after=[pe, act, dve], inc=[act])
                if dbg:
                    P.op("sync", lambda e, b=b, tt=tt: e.dma_start(
                        out=ctx_dbg[b, tt * 128:(tt + 1) * 128, :], in_=o_f[:]),
                        after=[act, dve], inc=[st], dma=True)
                P.op("vector", lambda e: e.tensor_tensor(
                    out=o_f[:], in0=o_f[:], in1=bias_t["bo_bc"][:], op=ALU.add),
                    after=[act, dve, st], inc=[dve])
                P.op("vector", lambda e: e.tensor_tensor(
                    out=o_f[:], in0=o_f[:], in1=xh_f[:], op=ALU.add),
                    after=[dve], inc=[dve])
                emit_ln(o_f[:], ao_f[:], D, 1e-12)
                P.op("vector", lambda e: e.tensor_tensor(
                    out=ao_f[:], in0=ao_f[:], in1=bias_t["g1_bc"][:], op=ALU.mult),
                    after=[dve], inc=[dve])
                P.op("vector", lambda e: e.tensor_tensor(
                    out=ao_f[:], in0=ao_f[:], in1=bias_t["be1_bc"][:],
                    op=ALU.add), after=[dve, st], inc=[dve])
                P.op("sync", lambda e, b=b, tt=tt: e.dma_start(
                    out=ao_scr[b, tt * 128:(tt + 1) * 128, :], in_=ao_f[:]),
                    after=[dve], inc=[st], dma=True)
                P.op("vector", lambda e, tt=tt: e.tensor_copy(ao_keep[:, tt, :], ao_f[:]),
                     after=[dve], inc=[dve])
            P.join()
            b12.close()
            bfa = ExitStack()
            W1_b = sb(bfa, f"W1_b{b}", [128, 8, FF], BF16)
            b1_t = sb(bfa, f"b1_t{b}", [128, FF], BF16)
            wstage4 = sb(bfa, f"wstage4{b}", [128, 2048], F32)
            hg_bf = [sb(bfa, f"hg_bf{b}{i}", [128, FF], BF16) for i in range(2)]
            aoT = sb(bfa, f"aoT{b}", [128, 8, 128], BF16)
            hpre = sb(bfa, f"hpre{b}", [128, 2048], F32)
            for half in range(2):
                P.op("sync", lambda e, half=half: e.dma_start(
                    out=wstage4[:], in_=bia["b1_bc"][:, half * 2048:(half + 1) * 2048]),
                    after=[dve], inc=[ld], dma=True)
                P.op("vector", lambda e, half=half: e.tensor_copy(
                    b1_t[:, half * 2048:(half + 1) * 2048], wstage4[:]),
                    after=[ld], inc=[dve])
            for c in range(8):
                for half in range(2):
                    P.op("sync", lambda e, c=c, half=half: e.dma_start(
                        out=wstage4[:],
                        in_=bW1[c * 128:(c + 1) * 128, half * 2048:(half + 1) * 2048]),
                        after=[dve], inc=[ld], dma=True)
                    P.op("vector", lambda e, c=c, half=half: e.tensor_copy(
                        W1_b[:, c, half * 2048:(half + 1) * 2048], wstage4[:]),
                        after=[ld, pe, act], inc=[dve])
            for tt in range(NT):
                P.op("vector", lambda e, tt=tt: e.tensor_copy(xh_bf[:], ao_keep[:, tt, :]),
                     after=[dve, pe], inc=[dve])
                emit_transposes(xh_bf, 8, None,
                                lambda g4: aoT[:, g4 * 4:(g4 + 1) * 4, :])
                hg = hg_bf[tt % 2]
                for n in range(8):
                    ps = big_ps[:, (n % 4) * 512:(n % 4) * 512 + 512]
                    for k in range(8):
                        aft = [act, pe, dve] if k == 0 else [pe]
                        P.op("tensor", lambda e, n=n, k=k, ps=ps: e.matmul(
                            ps, aoT[:, k, :], W1_b[:, k, n * 512:(n + 1) * 512],
                            start=(k == 0), stop=(k == 7)), after=aft, inc=[pe])
                    hp = hpre[:, (n % 4) * 512:(n % 4) * 512 + 512]
                    P.op("vector", lambda e, n=n, ps=ps, hp=hp: e.tensor_tensor(
                        out=hp, in0=ps,
                        in1=b1_t[:, n * 512:(n + 1) * 512], op=ALU.add),
                        after=[pe, dve, act], inc=[dve])
                    P.op("scalar", lambda e, n=n, hg=hg, hp=hp: e.activation(
                        out=hg[:, n * 512:(n + 1) * 512],
                        in_=hp, func=AF.Gelu,
                        bias=0.0, scale=1.0), after=[dve, act, st], inc=[act])
                P.op("sync", lambda e, b=b, tt=tt, hg=hg: e.dma_start(
                    out=tq_scr[b, tt * 128:(tt + 1) * 128, :], in_=hg[:]),
                    after=[act], inc=[st], dma=True)
            P.join()
            bfa.close()
            bfb = ExitStack()
            W2_b = sb(bfb, f"W2_b{b}", [128, 32, D], BF16)
            wstage5 = sb(bfb, f"wstage5{b}", [128, D], F32)
            hgi = [sb(bfb, f"hgi{b}{i}", [128, FF], BF16) for i in range(2)]
            hgT = sb(bfb, f"hgT{b}", [128, 32, 128], BF16)
            d_f = sb(bfb, f"d_f3{b}", [128, D], F32)
            aor_f = [sb(bfb, f"aor_f{b}{i}", [128, D], F32) for i in range(2)]
            conv_w(W2_b, bW2, wstage5, FF, D)
            for tt in range(NT):
                hg = hgi[tt % 2]
                P.op("sync", lambda e, b=b, tt=tt, hg=hg: e.dma_start(
                    out=hg[:], in_=tq_scr[b, tt * 128:(tt + 1) * 128, :]),
                    after=[pe], inc=[ld], dma=True)
                P.op("sync", lambda e, b=b, tt=tt: e.dma_start(
                    out=aor_f[tt % 2][:], in_=ao_scr[b, tt * 128:(tt + 1) * 128, :]),
                    after=[dve], inc=[ld], dma=True)
                emit_transposes(hg, 32, None,
                                lambda g4: hgT[:, g4 * 4:(g4 + 1) * 4, :],
                                extra_first=[ld])
                for n in range(2):
                    for k in range(32):
                        aft = [act, pe, dve] if k == 0 else [pe]
                        P.op("tensor", lambda e, n=n, k=k: e.matmul(
                            o_ps[:, n * 512:(n + 1) * 512], hgT[:, k, :],
                            W2_b[:, k, n * 512:(n + 1) * 512],
                            start=(k == 0), stop=(k == 31)), after=aft, inc=[pe])
                P.op("scalar", lambda e: e.copy(d_f[:], o_ps[:]),
                     after=[pe, act, dve], inc=[act])
                P.op("vector", lambda e: e.tensor_tensor(
                    out=d_f[:], in0=d_f[:], in1=bias_t["b2_bc"][:], op=ALU.add),
                    after=[act, dve], inc=[dve])
                P.op("vector", lambda e, tt=tt, ao=None: e.tensor_tensor(
                    out=d_f[:], in0=d_f[:], in1=aor_f[tt % 2][:], op=ALU.add),
                    after=[dve, ld], inc=[dve])
                emit_ln(d_f[:], o_f[:], D, 1e-12)
                P.op("vector", lambda e: e.tensor_tensor(
                    out=o_f[:], in0=o_f[:], in1=bias_t["g2_bc"][:], op=ALU.mult),
                    after=[dve], inc=[dve])
                P.op("vector", lambda e: e.tensor_tensor(
                    out=o_f[:], in0=o_f[:], in1=bias_t["be2_bc"][:], op=ALU.add),
                    after=[dve], inc=[dve])
                xt = xt_f[tt % 2]
                P.op("sync", lambda e, b=b, tt=tt, xt=xt: e.dma_start(
                    out=xt[:], in_=xres[b, tt * 128:(tt + 1) * 128, :]),
                    after=[dve], inc=[ld], dma=True)
                P.op("vector", lambda e, xt=xt: e.tensor_tensor(
                    out=xt[:], in0=xt[:], in1=o_f[:], op=ALU.add),
                    after=[ld, dve], inc=[dve])
                P.op("sync", lambda e, b=b, tt=tt, xt=xt: e.dma_start(
                    out=dst[b, tt * 128:(tt + 1) * 128, :], in_=xt[:]),
                    after=[dve], inc=[st], dma=True)
            P.join()
            bfb.close()
        P.join()
        b3.close()

    emit_attn_b(out if upto == 3 else xres)
    if upto == 3:
        P.join()
        P.emit()
        return nc, stack

    emit_ffn(xres, out, 1)
    P.join()
    P.emit()
    return nc, stack


def _host_layouts(inputs):
    f = lambda a: np.ascontiguousarray(a, dtype=np.float32)
    L = {}
    L["ident"] = np.eye(128, dtype=ml_dtypes.bfloat16)
    L["qWT"] = f(np.asarray(inputs["L0_attn_a_qW"]).T)
    L["kWT"] = f(np.asarray(inputs["L0_attn_a_kW"]).T)
    L["vWT"] = f(np.asarray(inputs["L0_attn_a_vW"]).T)
    L["oWT"] = f(np.asarray(inputs["L0_attn_a_oW"]).T)
    L["qbT"] = f(np.asarray(inputs["L0_attn_a_qb"]).reshape(8, 128).T)
    L["kbT"] = f(np.asarray(inputs["L0_attn_a_kb"]).reshape(8, 128).T)
    L["vb_bc"] = f(np.broadcast_to(np.asarray(inputs["L0_attn_a_vb"]), (128, D)))
    L["ob_bc"] = f(np.broadcast_to(np.asarray(inputs["L0_attn_a_ob"]), (128, D)))
    L["skT"] = f(np.asarray(inputs["L0_attn_a_static_keys"])
                 .reshape(8, 8, 128).transpose(2, 1, 0))
    L["wgT"] = f(np.asarray(inputs["L0_ffn_w_gate"]).T)
    L["wuT"] = f(np.asarray(inputs["L0_ffn_w_up"]).T)
    L["wdT"] = f(np.asarray(inputs["L0_ffn_w_down"]).T)
    L["bWq"] = f(np.asarray(inputs["L0_attn_b_Wq"]))
    L["bWk"] = f(np.asarray(inputs["L0_attn_b_Wk"]))
    L["bWv"] = f(np.asarray(inputs["L0_attn_b_Wv"]))
    L["bWo"] = f(np.asarray(inputs["L0_attn_b_Wo"]))
    L["bW1"] = f(np.asarray(inputs["L0_attn_b_W1"]))
    L["bW2"] = f(np.asarray(inputs["L0_attn_b_W2"]))
    for nm, key in (("bq_bc", "L0_attn_b_bq"), ("bk_bc", "L0_attn_b_bk"),
                    ("bv_bc", "L0_attn_b_bv"), ("bo_bc", "L0_attn_b_bo"),
                    ("b2_bc", "L0_attn_b_b2"), ("g1_bc", "L0_attn_b_g1"),
                    ("be1_bc", "L0_attn_b_be1"), ("g2_bc", "L0_attn_b_g2"),
                    ("be2_bc", "L0_attn_b_be2")):
        L[nm] = f(np.broadcast_to(np.asarray(inputs[key]), (128, D)))
    L["b1_bc"] = f(np.broadcast_to(np.asarray(inputs["L0_attn_b_b1"]), (128, FF)))
    return L


def kernel(**inputs):
    x = np.ascontiguousarray(np.asarray(inputs["L0_attn_a_x"], dtype=np.float32))
    L = _host_layouts(inputs)
    nc, stack = build_program(upto=4)
    in_maps = []
    for c in range(NCORES):
        m = dict(L)
        m["x"] = np.ascontiguousarray(x[c * BPC:(c + 1) * BPC])
        in_maps.append(m)
    res = run_bass_kernel_spmd(nc, in_maps, core_ids=list(range(NCORES)))
    stack.close()
    return np.concatenate([r["out"] for r in res.results], axis=0)


# version 10
# speedup vs baseline: 1.9079x; 1.5058x over previous
import numpy as np
import ml_dtypes
from contextlib import ExitStack

import concourse.bass as bass
import concourse.mybir as mybir
from concourse.bass_utils import run_bass_kernel_spmd

F32 = mybir.dt.float32
BF16 = mybir.dt.bfloat16
I8 = mybir.dt.int8
AF = mybir.ActivationFunctionType
ALU = mybir.AluOpType
AX = mybir.AxisListType

NCORES = 8
BPC = 2
S = 2048
D = 1024
H = 16
FF = 4096
NT = S // 128


class Chan:
    __slots__ = ("sem", "count")

    def __init__(self, nc, stack, name):
        self.sem = stack.enter_context(nc.semaphore(name))
        self.count = 0

    def tok(self):
        return (self.sem, self.count)


class Prog:
    ENGINES = ("tensor", "vector", "scalar", "gpsimd", "sync")

    def __init__(self, nc, stack):
        self.nc = nc
        self.stack = stack
        self.ops = []
        self.chans = {}

    def chan(self, name):
        if name not in self.chans:
            self.chans[name] = Chan(self.nc, self.stack, name)
        return self.chans[name]

    def op(self, engine, fn, after=(), inc=(), dma=False):
        waits = []
        for a in after:
            if a is None:
                continue
            if isinstance(a, Chan):
                if a.count > 0:
                    waits.append((a.sem, a.count))
            else:
                sem, val = a
                if val > 0:
                    waits.append((sem, val))
        wd = {}
        for sem, val in waits:
            k = id(sem)
            if k not in wd or wd[k][1] < val:
                wd[k] = (sem, val)
        incs = []
        toks = {}
        for c in inc:
            n = 16 if dma else 1
            c.count += n
            incs.append((c.sem, n))
            toks[c] = c.tok()
        self.ops.append((engine, fn, list(wd.values()), incs))
        return toks

    def join(self):
        toks = [(c.sem, c.count) for c in self.chans.values() if c.count > 0]
        for e in self.ENGINES:
            self.ops.append((e, None, toks, []))

    def emit(self):
        nc = self.nc
        with nc.Block() as block:
            for eng_name in self.ENGINES:
                section = getattr(block, eng_name)

                def make(name):
                    def body(eng):
                        for engine, fn, waits, incs in self.ops:
                            if engine != name:
                                continue
                            for sem, val in waits:
                                eng.wait_ge(sem, val)
                            inst = fn(eng) if fn is not None else None
                            if incs:
                                if inst is None:
                                    inst = eng.nop()
                                for sem, n in incs:
                                    inst.then_inc(sem, n)
                    return body

                section(make(eng_name))


def build_program(upto=4, dbg=False):
    nc = bass.Bass("TRN2", target_bir_lowering=False)
    stack = ExitStack()
    P = Prog(nc, stack)
    ld, st = P.chan("ld"), P.chan("st")
    pe, dve, act, gp = P.chan("pe"), P.chan("dve"), P.chan("act"), P.chan("gp")
    dA = [P.chan("dA0"), P.chan("dA1")]
    dB = [P.chan("dB0"), P.chan("dB1")]
    dW = P.chan("dW")
    sA = [P.chan("sA0"), P.chan("sA1")]

    dram = lambda n, sh, dt, k="ExternalInput": nc.dram_tensor(n, sh, dt, kind=k).ap()
    x_in = dram("x", [BPC, S, D], F32)
    out = dram("out", [BPC, S, D], F32, "ExternalOutput")
    ident = dram("ident", [128, 128], BF16)
    qWT = dram("qWT", [D, D], F32)
    kWT = dram("kWT", [D, D], F32)
    vWT = dram("vWT", [D, D], F32)
    oWT = dram("oWT", [D, D], F32)
    qbT = dram("qbT", [128, 8], F32)
    kbT = dram("kbT", [128, 8], F32)
    vb_bc = dram("vb_bc", [128, D], F32)
    ob_bc = dram("ob_bc", [128, D], F32)
    skT = dram("skT", [128, 8, 8], F32)
    wgT = dram("wgT", [D, FF], F32)
    wuT = dram("wuT", [D, FF], F32)
    wdT = dram("wdT", [FF, D], F32)
    bWq = dram("bWq", [D, D], F32)
    bWk = dram("bWk", [D, D], F32)
    bWv = dram("bWv", [D, D], F32)
    bWo = dram("bWo", [D, D], F32)
    bW1 = dram("bW1", [D, FF], F32)
    bW2 = dram("bW2", [FF, D], F32)
    bia = {nm: dram(nm, [128, FF if nm == "b1_bc" else D], F32)
           for nm in ("bq_bc", "bk_bc", "bv_bc", "bo_bc", "b1_bc", "b2_bc",
                      "g1_bc", "be1_bc", "g2_bc", "be2_bc")}

    xres = dram("xres", [BPC, S, D], F32, "Internal")
    ao_scr = dram("ao_scr", [BPC, S, D], F32, "ExternalOutput" if dbg else "Internal")
    if dbg:
        qn_dbg = dram("qn_dbg", [BPC, S, D], F32, "ExternalOutput")
        kv_dbg = dram("kv_dbg", [BPC, 128, 512], F32, "ExternalOutput")
        ctx_dbg = dram("ctx_dbg", [BPC, S, D], F32, "ExternalOutput")
    tq_scr = dram("tq_scr", [BPC, S, FF], BF16,
                  "ExternalOutput" if dbg == "ffn" else "Internal")
    if dbg == "ffn":
        dq_x = dram("dq_x", [BPC, S, D], F32, "ExternalOutput")
        dq_xq = dram("dq_xq", [BPC, S, D], BF16, "ExternalOutput")
        dq_gu = dram("dq_gu", [BPC, S, FF], F32, "ExternalOutput")
    wq_scr = {"g": dram("wgq_scr", [D, FF], BF16, "Internal"),
              "u": dram("wuq_scr", [D, FF], BF16, "Internal"),
              "d": dram("wdq_scr", [FF, D], BF16, "Internal")}

    sb = lambda stk, n, sh, dt: stk.enter_context(nc.sbuf_tensor(n, sh, dt))
    pst = lambda stk, n, sh, dt=F32: stk.enter_context(nc.psum_tensor(n, sh, dt))

    id_t = sb(stack, "id_t", [128, 128], BF16)
    P.op("sync", lambda e: e.dma_start(out=id_t[:], in_=ident), inc=[ld], dma=True)
    stat = sb(stack, "stat", [128, 32], F32)
    w_scales = sb(stack, "w_scales", [128, 4], F32)
    stq = sb(stack, "stq", [128, 2 * NT], F32)
    junk = sb(stack, "junk", [128, 2 * D], BF16)
    eps5 = sb(stack, "eps5", [128, 1], F32)
    eps12 = sb(stack, "eps12", [128, 1], F32)

    P.op("vector", lambda e: e.memset(eps5[:], 1e-5), inc=[dve])
    P.op("vector", lambda e: e.memset(eps12[:], 1e-12), inc=[dve])

    big_ps = pst(stack, "big_ps", [128, 2048])
    o_ps = pst(stack, "o_ps", [128, 1024])
    tr_ps = pst(stack, "tr_ps", [128, 2048], BF16)

    def emit_transposes(src_bf, n_chunks, dst, dst_fn, extra_first=()):
        for m in range(n_chunks):
            g4 = m // 4
            bank = g4 % 2
            aft = [pe]
            if m % 4 == 0:
                aft = [pe, act, dve, gp] + list(extra_first)
            P.op("tensor", lambda e, m=m, bank=bank: e.transpose(
                tr_ps[:, bank * 1024 + (m % 4) * 128: bank * 1024 + (m % 4) * 128 + 128],
                src_bf[:, m * 128:(m + 1) * 128], id_t[:]), after=aft, inc=[pe])
            if m % 4 == 3:
                P.op("scalar", lambda e, g4=g4, bank=bank: e.copy(
                    dst_fn(g4),
                    tr_ps[:, bank * 1024: bank * 1024 + 512]
                    .rearrange("p (a b) -> p a b", a=4)),
                    after=[pe, act], inc=[act])

    def emit_ln(in_ap, out_ap, width, eps, dep=()):
        eps_ap = (eps5 if eps > 1e-9 else eps12)[:]
        inv = 1.0 / width
        P.op("vector", lambda e: e.tensor_scalar(
            out=junk[:, 0:width], in0=in_ap, scalar1=inv, scalar2=0.0, op0=ALU.mult,
            op1=ALU.add, accum_out=stat[:, 0:1]), after=(dve, act) + tuple(dep), inc=[dve])
        sq = float(np.sqrt(inv))
        P.op("scalar", lambda e: e.activation(
            out=junk[:, D:D + width], in_=in_ap, func=AF.Square, bias=0.0, scale=sq,
            accum_out=stat[:, 1:2]), after=(act, dve) + tuple(dep), inc=[act])
        P.op("vector", lambda e: e.scalar_tensor_tensor(
            out=stat[:, 3:4], in0=stat[:, 0:1], scalar=-1.0, in1=stat[:, 0:1],
            op0=ALU.mult, op1=ALU.mult), after=(dve, act), inc=[dve])
        P.op("vector", lambda e: e.tensor_tensor(
            out=stat[:, 1:2], in0=stat[:, 1:2], in1=stat[:, 3:4], op=ALU.add),
            after=(dve,), inc=[dve])
        P.op("scalar", lambda e: e.activation(
            out=stat[:, 2:3], in_=stat[:, 1:2], func=AF.Ln, bias=eps_ap, scale=1.0),
            after=(act, dve), inc=[act])
        P.op("scalar", lambda e: e.activation(
            out=stat[:, 2:3], in_=stat[:, 2:3], func=AF.Exp, bias=0.0, scale=-0.5),
            after=(act,), inc=[act])
        return P.op("vector", lambda e: e.tensor_scalar(
            out=out_ap, in0=in_ap, scalar1=stat[:, 0:1], scalar2=stat[:, 2:3],
            op0=ALU.subtract, op1=ALU.mult), after=(dve, act, pe, gp), inc=[dve])

    def ln_t(in_ap, out_ap, width, eps_ap, lnA, prev, dep_dve=(), dep_act=(),
             dep_out=()):
        inv = 1.0 / width
        sq = float(np.sqrt(inv))
        P.op("vector", lambda e: e.tensor_scalar(
            out=junk[:, 0:width], in0=in_ap, scalar1=inv, scalar2=0.0, op0=ALU.mult,
            op1=ALU.add, accum_out=lnA[:, 0:1]), after=tuple(dep_dve), inc=[dve])
        t_sq = P.op("scalar", lambda e: e.activation(
            out=junk[:, D:D + width], in_=in_ap, func=AF.Square, bias=0.0, scale=sq,
            accum_out=lnA[:, 1:2]),
            after=tuple(dep_act) + (prev.get("tt"),), inc=[act])[act]
        P.op("vector", lambda e: e.scalar_tensor_tensor(
            out=lnA[:, 3:4], in0=lnA[:, 0:1], scalar=-1.0, in1=lnA[:, 0:1],
            op0=ALU.mult, op1=ALU.mult), after=[dve], inc=[dve])
        t_tt = P.op("vector", lambda e: e.tensor_tensor(
            out=lnA[:, 1:2], in0=lnA[:, 1:2], in1=lnA[:, 3:4], op=ALU.add),
            after=(t_sq, dve), inc=[dve])[dve]
        P.op("scalar", lambda e: e.activation(
            out=lnA[:, 2:3], in_=lnA[:, 1:2], func=AF.Ln, bias=eps_ap, scale=1.0),
            after=(t_tt, prev.get("fin")), inc=[act])
        t_exp = P.op("scalar", lambda e: e.activation(
            out=lnA[:, 2:3], in_=lnA[:, 2:3], func=AF.Exp, bias=0.0, scale=-0.5),
            after=[act], inc=[act])[act]
        t_fin = P.op("vector", lambda e: e.tensor_scalar(
            out=out_ap, in0=in_ap, scalar1=lnA[:, 0:1], scalar2=lnA[:, 2:3],
            op0=ALU.subtract, op1=ALU.mult),
            after=(t_exp, dve) + tuple(dep_out), inc=[dve])[dve]
        prev["tt"] = t_tt
        prev["fin"] = t_fin
        return {"fin": t_fin, "sq": t_sq}

    def conv_w(dst_b, src, wstage, rows, cols):
        for c in range(rows // 128):
            P.op("sync", lambda e, c=c: e.dma_start(
                out=wstage[:, 0:cols], in_=src[c * 128:(c + 1) * 128, :]),
                after=[dve], inc=[ld], dma=True)
            P.op("vector", lambda e, c=c: e.tensor_copy(dst_b[:, c, :], wstage[:, 0:cols]),
                 after=[ld, pe, act], inc=[dve])

    p0 = ExitStack()
    wstage0 = sb(p0, "wstage0", [128, FF], F32)
    qi8 = sb(p0, "qi8", [128, FF], I8)
    qbf = sb(p0, "qbf", [128, FF], BF16)
    ones_col = sb(p0, "ones_col", [128, 1], F32)
    ones_row = sb(p0, "ones_row", [1, 128], F32)
    P.op("vector", lambda e: e.memset(ones_col[:], 1.0), inc=[dve])
    P.op("vector", lambda e: e.memset(ones_row[:], 1.0), inc=[dve])

    def quant_weight(src, rows, cols, scr, scol):
        nch = rows // 128
        acc = stat[:, 8:9]
        red = stat[:, 9:10]
        P.op("vector", lambda e: e.memset(acc, 0.0), after=[dve], inc=[dve])
        for c in range(nch):
            P.op("sync", lambda e, c=c: e.dma_start(
                out=wstage0[:, 0:cols], in_=src[c * 128:(c + 1) * 128, :]),
                after=[dve], inc=[ld], dma=True)
            P.op("vector", lambda e: e.reduce_sum(out=red, in_=wstage0[:, 0:cols],
                                                  axis=AX.X, apply_absolute_value=True),
                 after=[ld, dve], inc=[dve])
            P.op("vector", lambda e: e.tensor_tensor(out=acc, in0=acc, in1=red, op=ALU.add),
                 after=[dve], inc=[dve])
        P.op("tensor", lambda e: e.matmul(big_ps[0:1, 0:1], acc, ones_col[:],
                                          start=True, stop=True),
             after=[dve, pe, act], inc=[pe])
        P.op("vector", lambda e: e.tensor_scalar(
            out=stat[0:1, 10:11], in0=big_ps[0:1, 0:1], scalar1=1.0 / (rows * cols),
            scalar2=1e-5, op0=ALU.mult, op1=ALU.max), after=[pe, dve], inc=[dve])
        P.op("tensor", lambda e: e.matmul(big_ps[:, 1:2], ones_row[:], stat[0:1, 10:11],
                                          start=True, stop=True),
             after=[dve, pe], inc=[pe])
        P.op("vector", lambda e, scol=scol: e.tensor_copy(w_scales[:, scol:scol + 1],
                                                          big_ps[:, 1:2]),
             after=[pe, dve], inc=[dve])
        P.op("vector", lambda e: e.reciprocal(out=stat[:, 11:12],
                                              in_=w_scales[:, scol:scol + 1]),
             after=[dve], inc=[dve])
        for c in range(nch):
            P.op("sync", lambda e, c=c: e.dma_start(
                out=wstage0[:, 0:cols], in_=src[c * 128:(c + 1) * 128, :]),
                after=[dve, st], inc=[ld], dma=True)
            P.op("vector", lambda e: e.tensor_scalar(
                out=qi8[:, 0:cols], in0=wstage0[:, 0:cols], scalar1=stat[:, 11:12],
                scalar2=None, op0=ALU.mult), after=[ld, dve, st], inc=[dve])
            P.op("vector", lambda e: e.tensor_copy(qbf[:, 0:cols], qi8[:, 0:cols]),
                 after=[dve], inc=[dve])
            P.op("vector", lambda e: e.tensor_scalar(
                out=qbf[:, 0:cols], in0=qbf[:, 0:cols], scalar1=1.0, scalar2=-1.0,
                op0=ALU.min, op1=ALU.max), after=[dve], inc=[dve])
            P.op("sync", lambda e, c=c: e.dma_start(
                out=scr[c * 128:(c + 1) * 128, :], in_=qbf[:, 0:cols]),
                after=[dve], inc=[st], dma=True)

    quant_weight(wgT, D, FF, wq_scr["g"], 0)
    quant_weight(wuT, D, FF, wq_scr["u"], 1)
    quant_weight(wdT, FF, D, wq_scr["d"], 2)
    P.join()
    p0.close()

    def emit_attn_a(dst):
        p1 = ExitStack()
        wstage = junk[:].bitcast(F32)
        w0_b = sb(p1, "w0_b", [128, 8, D], BF16)
        qbT_t = sb(p1, "qbT_t", [128, 8], F32)
        kbT_t = sb(p1, "kbT_t", [128, 8], F32)
        skT_t = sb(p1, "skT_t", [128, 8, 8], F32)
        vb_t = sb(p1, "vb_t", [128, D], BF16)
        ob_t = sb(p1, "ob_t", [128, D], F32)
        QT = sb(p1, "QT", [128, 8, S], BF16)
        KT = sb(p1, "KT", [128, 8, S], BF16)
        Vt = sb(p1, "Vt", [128, NT, D], BF16)
        xt_f = [sb(p1, f"xt_f{i}", [128, D], F32) for i in range(2)]
        xhat_b = sb(p1, "xhat_b", [128, D], BF16)
        PTb = sb(p1, "PTb", [128, NT, 256], BF16)
        xTst = PTb[:].rearrange("p a b -> p (a b)").rearrange("p (a b) -> p a b", a=8)
        un = sb(p1, "un", [128, 16384], BF16)
        unf = un[:].bitcast(F32)
        e_sbs = [unf[:, 0:2048], unf[:, 2048:4096]]
        e2f = unf[:, 4096:6144]
        pbufs = [un[:, 12288:14336], un[:, 14336:16384]]
        tb16s = sb(p1, "tb16s", [128, 2, 16], F32)
        ast = sb(p1, "ast", [128, 8], F32)
        ctxT = sb(p1, "ctxT", [128, 8, 256], BF16)
        kWT_b = un[:, 0:8192].rearrange("p (a b) -> p a b", a=8)
        vWT_b = un[:, 8192:16384].rearrange("p (a b) -> p a b", a=8)

        P.op("sync", lambda e: e.dma_start(out=wstage[:, 0:D], in_=vb_bc),
             after=[dve], inc=[ld], dma=True)
        P.op("vector", lambda e: e.tensor_copy(vb_t[:], wstage[:, 0:D]),
             after=[ld], inc=[dve])
        P.op("sync", lambda e: e.dma_start(out=qbT_t[:], in_=qbT), inc=[ld], dma=True)
        P.op("sync", lambda e: e.dma_start(out=kbT_t[:], in_=kbT), inc=[ld], dma=True)
        P.op("sync", lambda e: e.dma_start(out=skT_t[:], in_=skT), inc=[ld], dma=True)
        P.op("sync", lambda e: e.dma_start(out=ob_t[:], in_=ob_bc), inc=[ld], dma=True)
        P.op("vector", lambda e: e.tensor_scalar(out=qbT_t[:], in0=qbT_t[:], scalar1=0.125,
                                                 scalar2=None, op0=ALU.mult),
             after=[ld, dve], inc=[dve])

        for b in range(BPC):
            conv_w(kWT_b, kWT, wstage, D, D)
            conv_w(vWT_b, vWT, wstage, D, D)
            conv_w(w0_b, qWT, wstage, D, D)
            for sst in range(4):
                for sub in range(4):
                    tt = sst * 4 + sub
                    xt = xt_f[tt % 2]
                    P.op("sync", lambda e, b=b, tt=tt, xt=xt: e.dma_start(
                        out=xt[:], in_=x_in[b, tt * 128:(tt + 1) * 128, :]),
                        after=[dve, act], inc=[ld], dma=True)
                    emit_ln(xt[:], xhat_b[:], D, 1e-5, dep=(ld,))
                    emit_transposes(
                        xhat_b, 8, None,
                        lambda g4, sub=sub: xTst[:, g4 * 4:(g4 + 1) * 4,
                                                 sub * 128:(sub + 1) * 128])
                for WTb, bias, scale, DST in ((w0_b, qbT_t, 0.125, QT),
                                              (kWT_b, kbT_t, 1.0, KT)):
                    for m in range(8):
                        ps = big_ps[:, (m % 2) * 512:(m % 2) * 512 + 512]
                        for k in range(8):
                            aft = [act, pe, dve] if k == 0 else [pe]
                            P.op("tensor", lambda e, m=m, k=k, WTb=WTb, ps=ps: e.matmul(
                                ps, WTb[:, k, m * 128:(m + 1) * 128], xTst[:, k, :],
                                start=(k == 0), stop=(k == 7)), after=aft, inc=[pe])
                        P.op("scalar", lambda e, m=m, ps=ps, DST=DST, bias=bias,
                             scale=scale, sst=sst: e.activation(
                                 out=DST[:, m, sst * 512:(sst + 1) * 512], in_=ps,
                                 func=AF.Identity, bias=bias[:, m:m + 1], scale=scale),
                             after=[pe, act], inc=[act])
                for sub in range(4):
                    chunk = sst * 4 + sub
                    for n in range(2):
                        ps = big_ps[:, 1024 + n * 512:1024 + n * 512 + 512]
                        for k in range(8):
                            aft = [dve, pe, act] if k == 0 else [pe]
                            P.op("tensor", lambda e, sub=sub, n=n, k=k, ps=ps: e.matmul(
                                ps, xTst[:, k, sub * 128:(sub + 1) * 128],
                                vWT_b[:, k, n * 512:(n + 1) * 512],
                                start=(k == 0), stop=(k == 7)), after=aft, inc=[pe])
                        P.op("vector", lambda e, chunk=chunk, n=n, ps=ps: e.tensor_tensor(
                            out=Vt[:, chunk, n * 512:(n + 1) * 512], in0=ps,
                            in1=vb_t[:, n * 512:(n + 1) * 512], op=ALU.add),
                            after=[pe, dve], inc=[dve])
            P.op("vector", lambda e: e.tensor_copy(
                KT[:, :, 0:8], skT_t[:]), after=[dve, act, pe], inc=[dve])

            conv_w(w0_b, oWT, wstage, D, D)
            oWT_b = w0_b
            P.join()
            exp_tok = [None, None]
            stt_tok = [None, None]
            scale_tok = [None, None]
            transp_tok2 = [None, None]
            trev_tok = [None, None]
            ev_last = [None, None]
            pv_tok = None
            ctx_tok = None
            oproj_tok = None
            xg_out = [None, None]
            xg_add = None
            sstops = {}

            def a2_scores(r):
                g, h, qt = r // 32, (r // 2) % 16, r % 2
                hrow, hm, qg = (h % 2) * 64, h // 2, g * 2 + qt
                stops = [None, None]
                for half in range(2):
                    for n2 in range(2):
                        n = half * 2 + n2
                        aft = [exp_tok[half]] if n2 == 0 else []
                        stops[half] = P.op("tensor", lambda e, hrow=hrow, hm=hm,
                                           qg=qg, n=n: e.matmul(
                            big_ps[:, n * 512:(n + 1) * 512],
                            QT[hrow:hrow + 64, hm, qg * 128:(qg + 1) * 128],
                            KT[hrow:hrow + 64, hm, n * 512:(n + 1) * 512],
                            start=True, stop=True), after=aft, inc=[pe])[pe]
                return stops

            def a2_mask(r, stops):
                pp = r % 2
                exp_tok[0] = P.op("scalar", lambda e, pp=pp: e.activation(
                    out=e_sbs[pp][:, 0:1024], in_=big_ps[:, 0:1024], func=AF.Exp,
                    bias=0.0, scale=1.0),
                    after=(stops[0], stt_tok[pp]), inc=[act])[act]
                exp_tok[1] = P.op("scalar", lambda e, pp=pp: e.activation(
                    out=e_sbs[pp][:, 1024:2048], in_=big_ps[:, 1024:2048],
                    func=AF.Exp, bias=0.0, scale=1.0),
                    after=(stops[1],), inc=[act])[act]
                P.op("vector", lambda e, pp=pp: e.max(
                    out=tb16s[:, pp, 0:8], in_=e_sbs[pp][:]),
                    after=(exp_tok[1], dve), inc=[dve])
                P.op("vector", lambda e, pp=pp: e.match_replace(
                    out=e2f[:], in_to_replace=tb16s[:, pp, 0:8],
                    in_values=e_sbs[pp][:], imm_value=0.0),
                    after=[dve], inc=[dve])
                P.op("vector", lambda e, pp=pp: e.max(
                    out=tb16s[:, pp, 8:16], in_=e2f[:]), after=[dve], inc=[dve])
                P.op("vector", lambda e, pp=pp: e.reduce_sum(
                    out=ast[:, pp * 4:pp * 4 + 1], in_=tb16s[:, pp, :], axis=AX.X),
                    after=[dve], inc=[dve])
                P.op("vector", lambda e, pp=pp: e.reciprocal(
                    out=ast[:, pp * 4 + 1:pp * 4 + 2], in_=ast[:, pp * 4:pp * 4 + 1]),
                    after=[dve], inc=[dve])
                stt_tok[pp] = P.op("vector", lambda e, pp=pp: e.scalar_tensor_tensor(
                    out=pbufs[pp][:], in0=e_sbs[pp][:],
                    scalar=tb16s[:, pp, 15:16], in1=e_sbs[pp][:],
                    op0=ALU.is_ge, op1=ALU.mult),
                    after=(transp_tok2[pp], dve), inc=[dve])[dve]
                scale_tok[pp] = P.op("vector", lambda e, pp=pp: e.tensor_scalar(
                    out=pbufs[pp][:], in0=pbufs[pp][:],
                    scalar1=ast[:, pp * 4 + 1:pp * 4 + 2], scalar2=None,
                    op0=ALU.mult), after=[dve], inc=[dve])[dve]

            def a2_oproj(g):
                nonlocal xg_add, oproj_tok
                for qt in range(2):
                    qg = g * 2 + qt
                    xg = xt_f[qt]
                    ldt = P.op("sync", lambda e, b=b, qg=qg, xg=xg: e.dma_start(
                        out=xg[:], in_=x_in[b, qg * 128:(qg + 1) * 128, :]),
                        after=(xg_out[qt],), inc=[dA[qt]], dma=True)[dA[qt]]
                    gpt = P.op("vector", lambda e, xg=xg: e.tensor_tensor(
                        out=xg[:], in0=xg[:], in1=ob_t[:], op=ALU.add),
                        after=(ldt,), inc=[dve])[dve]
                    for n in range(2):
                        stop = None
                        for k in range(8):
                            aft = []
                            if k == 0:
                                aft.append(xg_add)
                                if n == 0 and qt == 0:
                                    aft.append(ctx_tok)
                            stop = P.op("tensor", lambda e, n=n, k=k, qt=qt: e.matmul(
                                o_ps[:, 512:1024],
                                ctxT[:, k, qt * 128:(qt + 1) * 128],
                                oWT_b[:, k, n * 512:(n + 1) * 512],
                                start=(k == 0), stop=(k == 7)), after=aft, inc=[pe])[pe]
                        xg_add = P.op("vector", lambda e, n=n, xg=xg: e.tensor_tensor(
                            out=xg[:, n * 512:(n + 1) * 512],
                            in0=xg[:, n * 512:(n + 1) * 512],
                            in1=o_ps[:, 512:1024], op=ALU.add),
                            after=(stop, gpt), inc=[dve])[dve]
                    xg_out[qt] = P.op("sync", lambda e, b=b, qg=qg, xg=xg: e.dma_start(
                        out=dst[b, qg * 128:(qg + 1) * 128, :], in_=xg[:]),
                        after=(xg_add,), inc=[sA[qt]], dma=True)[sA[qt]]
                    oproj_tok = stop

            def a2_tail(r):
                nonlocal pv_tok, ctx_tok
                g, h, qt = r // 32, (r // 2) % 16, r % 2
                pp = r % 2
                trstop = [None, None]
                for c in range(16):
                    tb = c // 8
                    aft = []
                    if c == 0:
                        aft.append(scale_tok[pp])
                    if c % 8 == 0:
                        aft.append(trev_tok[tb])
                    tok = P.op("tensor", lambda e, c=c, tb=tb, pp=pp: e.transpose(
                        tr_ps[:, tb * 1024 + (c % 8) * 128:
                              tb * 1024 + (c % 8) * 128 + 128],
                        pbufs[pp][:, c * 128:(c + 1) * 128], id_t[:]),
                        after=aft, inc=[pe])[pe]
                    if c % 8 == 7:
                        trstop[tb] = tok
                transp_tok2[pp] = tok
                for tb in range(2):
                    aft = [trstop[tb]]
                    if qt == 0 and tb == 0:
                        aft.append(pv_tok)
                    trev_tok[tb] = P.op("scalar", lambda e, tb=tb, qt=qt: e.copy(
                        PTb[:, tb * 8:(tb + 1) * 8, qt * 128:(qt + 1) * 128],
                        tr_ps[:, tb * 1024: tb * 1024 + 1024]
                        .rearrange("p (a b) -> p a b", a=8)),
                        after=aft, inc=[act])[act]
                ev_last[qt] = trev_tok[1]
                if qt == 1:
                    hrow = (h % 2) * 64
                    pvstop = None
                    for cc in range(NT):
                        aft = []
                        if cc == 0:
                            aft = [ev_last[0], ev_last[1]]
                            if h % 2 == 0:
                                aft.append(ctx_tok)
                        pvstop = P.op("tensor", lambda e, cc=cc, hrow=hrow, h=h:
                                      e.matmul(
                            o_ps[hrow:hrow + 64, 0:256],
                            Vt[:, cc, h * 64:(h + 1) * 64], PTb[:, cc, :],
                            start=(cc == 0), stop=(cc == 15)), after=aft, inc=[pe])[pe]
                    pv_tok = pvstop
                    if h % 2 == 1:
                        hm = h // 2
                        aft = [pvstop]
                        if hm == 0:
                            aft.append(oproj_tok)
                        ctx_tok = P.op("scalar", lambda e, hm=hm: e.copy(
                            ctxT[:, hm, :], o_ps[:, 0:256]), after=aft, inc=[act])[act]
                    if h == 15:
                        a2_oproj(g)

            R = 8 * H * 2
            for r in range(R + 1):
                if r < R:
                    sstops[r] = a2_scores(r)
                    a2_mask(r, sstops[r])
                if r > 0:
                    a2_tail(r - 1)
            P.join()
        P.join()
        p1.close()

    emit_attn_a(out if upto == 1 else xres)
    if upto == 1:
        P.join()
        P.emit()
        return nc, stack

    def emit_ffn(src, dst, tag):
        TIL = BPC * NT
        f1 = ExitStack()
        wgq_b = sb(f1, f"wgq_b{tag}", [128, 8, FF], BF16)
        wuq_b = sb(f1, f"wuq_b{tag}", [128, 8, FF], BF16)
        xt_f = [sb(f1, f"fx{tag}{i}", [128, D], F32) for i in range(2)]
        xhat_f = sb(f1, f"xhat_f{tag}", [128, D], F32)
        xq_i8 = sb(f1, f"xq_i8{tag}", [128, D], I8)
        xq_bf = [sb(f1, f"xq_bf{tag}{i}", [128, D], BF16) for i in range(2)]
        xqT = sb(f1, f"xqT{tag}", [128, 8, 128], BF16)
        g_f = sb(f1, f"g_f{tag}", [128, FF], F32)
        u_f = sb(f1, f"u_f{tag}", [128, FF], F32)
        tq_i8 = sb(f1, f"tq_i8{tag}", [128, FF], I8)
        tq_bf = [sb(f1, f"tq_bf{tag}{i}", [128, FF], BF16) for i in range(2)]
        qst = sb(f1, f"qst{tag}", [128, 16], F32)
        lnA = sb(f1, f"lnA{tag}", [128, 2, 4], F32)
        for c in range(8):
            P.op("sync", lambda e, c=c: e.dma_start(
                out=wgq_b[:, c, :], in_=wq_scr["g"][c * 128:(c + 1) * 128, :]),
                inc=[dW], dma=True)
            wtok_all = P.op("sync", lambda e, c=c: e.dma_start(
                out=wuq_b[:, c, :], in_=wq_scr["u"][c * 128:(c + 1) * 128, :]),
                inc=[dW], dma=True)[dW]
        xt_free = [(), ()]
        ld_tok = [None, None]
        bank_tok = [None, None, None, None]
        trb_tok = [None, None]
        tq_out_tok = [None, None]
        gu_tok = None
        gf_done = None
        quant_tok = [None, None]
        sc_tok = [None, None]
        transp_tok = [None, None]
        up7_tok = [None, None]
        ln_prev = [{}, {}]

        def stageA(t):
            b, tt = divmod(t, NT)
            p = t % 2
            c0 = p * 8
            ld_tok[p] = P.op("sync", lambda e, b=b, tt=tt, p=p: e.dma_start(
                out=xt_f[p][:], in_=src[b, tt * 128:(tt + 1) * 128, :]),
                after=xt_free[p], inc=[dA[p]], dma=True)[dA[p]]
            r = ln_t(xt_f[p][:], xhat_f[:], D, eps5[:], lnA[:, p, :], ln_prev[p],
                     dep_dve=(ld_tok[p],), dep_act=(ld_tok[p],))
            xt_free[p] = (r["fin"], r["sq"])
            if dbg == "ffn" and tag == 0:
                P.op("sync", lambda e, b=b, tt=tt: e.dma_start(
                    out=dq_x[b, tt * 128:(tt + 1) * 128, :], in_=xhat_f[:]),
                    after=(r["fin"],), inc=[st], dma=True)
            P.op("vector", lambda e, c0=c0: e.reduce_max(
                out=qst[:, c0:c0 + 1], in_=xhat_f[:], axis=AX.X,
                apply_absolute_value=True), after=[dve], inc=[dve])
            P.op("vector", lambda e, c0=c0: e.tensor_scalar(
                out=qst[:, c0:c0 + 1], in0=qst[:, c0:c0 + 1], scalar1=1e-5,
                scalar2=None, op0=ALU.max), after=[dve], inc=[dve])
            P.op("vector", lambda e, c0=c0: e.reciprocal(
                out=qst[:, c0 + 1:c0 + 2], in_=qst[:, c0:c0 + 1]),
                after=[dve], inc=[dve])
            P.op("vector", lambda e, c0=c0: e.tensor_scalar(
                out=qst[:, c0 + 1:c0 + 2], in0=qst[:, c0 + 1:c0 + 2], scalar1=127.0,
                scalar2=None, op0=ALU.mult), after=[dve], inc=[dve])
            P.op("vector", lambda e, c0=c0: e.tensor_scalar(
                out=xq_i8[:], in0=xhat_f[:], scalar1=qst[:, c0 + 1:c0 + 2],
                scalar2=None, op0=ALU.mult), after=[dve], inc=[dve])
            quant_tok[p] = P.op("vector", lambda e, p=p: e.tensor_copy(
                xq_bf[p][:], xq_i8[:]), after=(transp_tok[p], dve), inc=[dve])[dve]
            if dbg == "ffn" and tag == 0:
                P.op("sync", lambda e, b=b, tt=tt, p=p: e.dma_start(
                    out=dq_xq[b, tt * 128:(tt + 1) * 128, :], in_=xq_bf[p][:]),
                    after=(quant_tok[p],), inc=[st], dma=True)
            P.op("vector", lambda e, c0=c0: e.tensor_scalar(
                out=qst[:, c0 + 4:c0 + 5], in0=qst[:, c0:c0 + 1],
                scalar1=1.0 / 127.0, scalar2=None, op0=ALU.mult),
                after=[dve], inc=[dve])
            P.op("vector", lambda e, c0=c0: e.tensor_tensor(
                out=qst[:, c0 + 2:c0 + 3], in0=qst[:, c0 + 4:c0 + 5],
                in1=w_scales[:, 0:1], op=ALU.mult),
                after=(up7_tok[p], dve), inc=[dve])
            sc_tok[p] = P.op("vector", lambda e, c0=c0: e.tensor_tensor(
                out=qst[:, c0 + 3:c0 + 4], in0=qst[:, c0 + 4:c0 + 5],
                in1=w_scales[:, 1:2], op=ALU.mult), after=[dve], inc=[dve])[dve]

        def stageB(t):
            nonlocal transp_tok
            p = t % 2
            c0 = p * 8
            trstop = [None, None]
            ev = [None, None]
            for m in range(8):
                bank = m // 4
                aft = []
                if m == 0:
                    aft.append(quant_tok[p])
                if m % 4 == 0:
                    aft.append(trb_tok[bank])
                tok = P.op("tensor", lambda e, m=m, bank=bank, p=p: e.transpose(
                    tr_ps[:, bank * 1024 + (m % 4) * 128:
                          bank * 1024 + (m % 4) * 128 + 128],
                    xq_bf[p][:, m * 128:(m + 1) * 128], id_t[:]),
                    after=aft, inc=[pe])[pe]
                if m % 4 == 3:
                    trstop[bank] = tok
            transp_tok[p] = tok
            for bank in range(2):
                ev[bank] = trb_tok[bank] = P.op("scalar", lambda e, bank=bank: e.copy(
                    xqT[:, bank * 4:(bank + 1) * 4, :],
                    tr_ps[:, bank * 1024: bank * 1024 + 512]
                    .rearrange("p (a b) -> p a b", a=4)),
                    after=(trstop[bank],), inc=[act])[act]
            for which, wq_b, dstf, scol, fn in (
                    (0, wgq_b, g_f, 2, AF.Silu), (1, wuq_b, u_f, 3, AF.Copy)):
                for n in range(8):
                    bk = (which * 8 + n) % 4
                    ps = big_ps[:, bk * 512:bk * 512 + 512]
                    stop = None
                    for k in range(8):
                        aft = []
                        if k == 0:
                            aft.append(bank_tok[bk])
                            if which == 0 and n == 0:
                                aft.append(ev[0])
                        if k == 4 and which == 0 and n == 0:
                            aft.append(ev[1])
                        if t == 0 and which == 0 and n == 0 and k == 0:
                            aft.append(wtok_all)
                        stop = P.op("tensor", lambda e, n=n, k=k, wq_b=wq_b, ps=ps:
                                    e.matmul(ps, xqT[:, k, :],
                                             wq_b[:, k, n * 512:(n + 1) * 512],
                                             start=(k == 0), stop=(k == 7)),
                                    after=aft, inc=[pe])[pe]
                    aft = [stop, sc_tok[p]]
                    if which == 0 and n == 0:
                        aft.append(gf_done)
                    bank_tok[bk] = P.op("scalar", lambda e, n=n, ps=ps, dstf=dstf,
                                        scol=scol, fn=fn, c0=c0: e.activation(
                        out=dstf[:, n * 512:(n + 1) * 512], in_=ps, func=fn, bias=0.0,
                        scale=qst[:, c0 + scol:c0 + scol + 1]),
                        after=aft, inc=[act])[act]
                    if which == 1 and n == 7:
                        up7_tok[p] = bank_tok[bk]

        def stageC(t):
            nonlocal gu_tok, gf_done
            b, tt = divmod(t, NT)
            p = t % 2
            c0 = p * 8
            gu_tok = P.op("vector", lambda e: e.tensor_tensor(
                out=g_f[:], in0=g_f[:], in1=u_f[:], op=ALU.mult),
                after=(up7_tok[p],), inc=[dve])[dve]
            if dbg == "ffn" and tag == 0:
                P.op("sync", lambda e, b=b, tt=tt: e.dma_start(
                    out=dq_gu[b, tt * 128:(tt + 1) * 128, :], in_=g_f[:]),
                    after=(gu_tok,), inc=[st], dma=True)
            P.op("vector", lambda e, c0=c0: e.reduce_max(
                out=qst[:, c0 + 5:c0 + 6], in_=g_f[:], axis=AX.X,
                apply_absolute_value=True), after=[dve], inc=[dve])
            P.op("vector", lambda e, c0=c0: e.tensor_scalar(
                out=qst[:, c0 + 5:c0 + 6], in0=qst[:, c0 + 5:c0 + 6], scalar1=1e-5,
                scalar2=None, op0=ALU.max), after=[dve], inc=[dve])
            P.op("vector", lambda e, b=b, tt=tt, c0=c0: e.tensor_scalar(
                out=stq[:, b * NT + tt:b * NT + tt + 1], in0=qst[:, c0 + 5:c0 + 6],
                scalar1=1.0 / 127.0, scalar2=None, op0=ALU.mult),
                after=[dve], inc=[dve])
            P.op("vector", lambda e, c0=c0: e.reciprocal(
                out=qst[:, c0 + 6:c0 + 7], in_=qst[:, c0 + 5:c0 + 6]),
                after=[dve], inc=[dve])
            P.op("vector", lambda e, c0=c0: e.tensor_scalar(
                out=qst[:, c0 + 6:c0 + 7], in0=qst[:, c0 + 6:c0 + 7], scalar1=127.0,
                scalar2=None, op0=ALU.mult), after=[dve], inc=[dve])
            gf_done = P.op("vector", lambda e, c0=c0: e.tensor_scalar(
                out=tq_i8[:], in0=g_f[:], scalar1=qst[:, c0 + 6:c0 + 7],
                scalar2=None, op0=ALU.mult), after=[dve], inc=[dve])[dve]
            cast = P.op("vector", lambda e, p=p: e.tensor_copy(
                tq_bf[p][:], tq_i8[:]), after=(tq_out_tok[p], dve), inc=[dve])[dve]
            tq_out_tok[p] = P.op("sync", lambda e, b=b, tt=tt, p=p: e.dma_start(
                out=tq_scr[b, tt * 128:(tt + 1) * 128, :], in_=tq_bf[p][:]),
                after=(cast,), inc=[sA[p]], dma=True)[sA[p]]

        for i in range(TIL + 1):
            if i < TIL:
                stageA(i)
            if i > 0:
                stageB(i - 1)
                stageC(i - 1)
        P.join()
        f1.close()
        f2 = ExitStack()
        wdq_b = sb(f2, f"wdq_b{tag}", [128, 32, D], BF16)
        tqi = [sb(f2, f"tqi{tag}{i}", [128, FF], BF16) for i in range(2)]
        tqT = sb(f2, f"tqT{tag}", [128, 32, 128], BF16)
        xt2 = [sb(f2, f"fx2{tag}{i}", [128, D], F32) for i in range(2)]
        d_f = sb(f2, f"d_f{tag}", [128, D], F32)
        s2st = sb(f2, f"s2st{tag}", [128, 2], F32)
        for c in range(32):
            wtok2_all = P.op("sync", lambda e, c=c: e.dma_start(
                out=wdq_b[:, c, :], in_=wq_scr["d"][c * 128:(c + 1) * 128, :]),
                inc=[dW], dma=True)[dW]
        ldq_tok = [None, None]
        ldx2_tok = [None, None]
        transp2_tok = [None, None]
        trb2_tok = [None, None]
        st14_tok = [None, None]
        st14_guard = [None, None]
        o_scale_tok = None
        add_prev = None
        add_tok = [None, None]
        out_tok = [None, None]

        def stageA2(t):
            b, tt = divmod(t, NT)
            p = t % 2
            ldq_tok[p] = P.op("sync", lambda e, b=b, tt=tt, p=p: e.dma_start(
                out=tqi[p][:], in_=tq_scr[b, tt * 128:(tt + 1) * 128, :]),
                after=(transp2_tok[p],), inc=[dA[p]], dma=True)[dA[p]]
            ldx2_tok[p] = P.op("sync", lambda e, b=b, tt=tt, p=p: e.dma_start(
                out=xt2[p][:], in_=src[b, tt * 128:(tt + 1) * 128, :]),
                after=(out_tok[p],), inc=[dB[p]], dma=True)[dB[p]]
            st14_tok[p] = P.op("vector", lambda e, b=b, tt=tt, p=p: e.tensor_tensor(
                out=s2st[:, p:p + 1], in0=stq[:, b * NT + tt:b * NT + tt + 1],
                in1=w_scales[:, 2:3], op=ALU.mult),
                after=(st14_guard[p],), inc=[dve])[dve]

        def stageB2(t):
            nonlocal o_scale_tok, add_prev
            b, tt = divmod(t, NT)
            p = t % 2
            ev2 = [None] * 8
            tok = None
            for m in range(32):
                g4 = m // 4
                bank = g4 % 2
                aft = []
                if m == 0:
                    aft.append(ldq_tok[p])
                if m % 4 == 0:
                    aft.append(trb2_tok[bank])
                tok = P.op("tensor", lambda e, m=m, bank=bank, p=p: e.transpose(
                    tr_ps[:, bank * 1024 + (m % 4) * 128:
                          bank * 1024 + (m % 4) * 128 + 128],
                    tqi[p][:, m * 128:(m + 1) * 128], id_t[:]),
                    after=aft, inc=[pe])[pe]
                if m % 4 == 3:
                    ev2[g4] = trb2_tok[bank] = P.op(
                        "scalar", lambda e, g4=g4, bank=bank: e.copy(
                            tqT[:, g4 * 4:(g4 + 1) * 4, :],
                            tr_ps[:, bank * 1024: bank * 1024 + 512]
                            .rearrange("p (a b) -> p a b", a=4)),
                        after=(tok,), inc=[act])[act]
            transp2_tok[p] = tok
            stop = None
            for n in range(2):
                ps = o_ps[:, n * 512:(n + 1) * 512]
                for k in range(32):
                    aft = []
                    if k == 0 and n == 0:
                        aft.append(o_scale_tok)
                    if n == 0 and k % 4 == 0:
                        aft.append(ev2[k // 4])
                    if t == 0 and n == 0 and k == 0:
                        aft.append(wtok2_all)
                    stop = P.op("tensor", lambda e, n=n, k=k, ps=ps: e.matmul(
                        ps, tqT[:, k, :], wdq_b[:, k, n * 512:(n + 1) * 512],
                        start=(k == 0), stop=(k == 31)), after=aft, inc=[pe])[pe]
            o_scale_tok = P.op("scalar", lambda e, p=p: e.activation(
                out=d_f[:], in_=o_ps[:], func=AF.Copy, bias=0.0,
                scale=s2st[:, p:p + 1]),
                after=(stop, st14_tok[p], add_prev), inc=[act])[act]
            st14_guard[p] = o_scale_tok
            add_prev = add_tok[p] = P.op("vector", lambda e, p=p: e.tensor_tensor(
                out=xt2[p][:], in0=xt2[p][:], in1=d_f[:], op=ALU.add),
                after=(o_scale_tok, ldx2_tok[p]), inc=[dve])[dve]
            out_tok[p] = P.op("sync", lambda e, b=b, tt=tt, p=p: e.dma_start(
                out=dst[b, tt * 128:(tt + 1) * 128, :], in_=xt2[p][:]),
                after=(add_tok[p],), inc=[sA[p]], dma=True)[sA[p]]

        for i in range(TIL + 1):
            if i < TIL:
                stageA2(i)
            if i > 0:
                stageB2(i - 1)
        P.join()
        f2.close()

    emit_ffn(xres, out if upto == 2 else xres, 0)
    if upto == 2:
        P.join()
        P.emit()
        return nc, stack

    def emit_attn_b(dst):
        b3 = ExitStack()
        bias_t = {}
        for nm in ("bq_bc", "bk_bc", "bv_bc", "bo_bc", "b2_bc", "g1_bc", "be1_bc",
                   "g2_bc", "be2_bc"):
            bias_t[nm] = sb(b3, nm + "_t", [128, D], F32)
            P.op("sync", lambda e, nm=nm: e.dma_start(out=bias_t[nm][:], in_=bia[nm]),
                 inc=[ld], dma=True)
        xt_f = [sb(b3, f"bx{i}", [128, D], F32) for i in range(2)]
        ao_keep = sb(b3, "ao_keep", [128, NT, D], BF16)
        xh_bf = sb(b3, "xh_bf", [128, D], BF16)
        xhT = sb(b3, "xhT", [128, 8, 128], BF16)
        o_f = sb(b3, "o_f3", [128, D], F32)
        ao_f = sb(b3, "ao_f", [128, D], F32)

        kv_ps = big_ps[:, 1536:2048]

        def emit_elunorm(src_f, dst_bf, tmp_f, eq_f, sq_f, s16):
            P.op("vector", lambda e, src_f=src_f: e.tensor_scalar(
                out=tmp_f[:], in0=src_f[:], scalar1=0.0, scalar2=None, op0=ALU.min),
                after=[dve, act], inc=[dve])
            P.op("scalar", lambda e: e.activation(out=eq_f[:], in_=tmp_f[:], func=AF.Exp,
                                                  bias=0.0, scale=1.0),
                 after=[act, dve], inc=[act])
            P.op("vector", lambda e, src_f=src_f: e.scalar_tensor_tensor(
                out=eq_f[:], in0=src_f[:], scalar=0.0, in1=eq_f[:], op0=ALU.max,
                op1=ALU.add), after=[dve, act], inc=[dve])
            P.op("vector", lambda e: e.tensor_scalar(
                out=eq_f[:], in0=eq_f[:], scalar1=1.0, scalar2=None, op0=ALU.subtract),
                after=[dve], inc=[dve])
            P.op("scalar", lambda e: e.activation(out=sq_f[:], in_=eq_f[:], func=AF.Square,
                                                  bias=0.0, scale=1.0),
                 after=[act, dve], inc=[act])
            P.op("vector", lambda e: e.tensor_reduce(
                out=s16[:], in_=sq_f[:].rearrange("p (h d) -> p h d", h=16),
                axis=AX.X, op=ALU.add), after=[dve, act], inc=[dve])
            P.op("scalar", lambda e: e.activation(out=s16[:], in_=s16[:], func=AF.Ln,
                                                  bias=0.0, scale=1.0),
                 after=[act, dve], inc=[act])
            P.op("scalar", lambda e: e.activation(out=s16[:], in_=s16[:], func=AF.Exp,
                                                  bias=0.0, scale=-0.5),
                 after=[act], inc=[act])
            return P.op("vector", lambda e, dst_bf=dst_bf: e.tensor_tensor(
                out=dst_bf[:].rearrange("p (h d) -> p h d", h=16),
                in0=eq_f[:].rearrange("p (h d) -> p h d", h=16),
                in1=s16[:].unsqueeze(2).to_broadcast([128, 16, 64]), op=ALU.mult),
                after=[dve, act, pe], inc=[dve])

        for b in range(BPC):
            b12 = ExitStack()
            wstage = sb(b12, f"wstage3{b}", [128, D], F32)
            w0b_b = sb(b12, f"w0b_b{b}", [128, 8, D], BF16)
            Wk_b = sb(b12, f"Wk_b{b}", [128, 8, D], BF16)
            Wv_b = sb(b12, f"Wv_b{b}", [128, 8, D], BF16)
            qn_keep = sb(b12, f"qn_keep{b}", [128, NT, D], BF16)
            q_f = sb(b12, f"q_f{b}", [128, D], F32)
            k_f = sb(b12, f"k_f{b}", [128, D], F32)
            v_bf = sb(b12, f"v_bf{b}", [128, D], BF16)
            xh_f = sb(b12, f"xh_f{b}", [128, D], F32)
            tmp_f = sb(b12, f"tmp_f{b}", [128, D], F32)
            eq_f = sb(b12, f"eq_f{b}", [128, D], F32)
            sq_f = sb(b12, f"sq_f{b}", [128, D], F32)
            s16 = sb(b12, f"s16{b}", [128, 16], F32)
            n_bf = sb(b12, f"n_bf{b}", [128, D], BF16)
            kv_bf = sb(b12, f"kv_bf{b}", [128, 512], BF16)
            ctxbT = sb(b12, f"ctxbT{b}", [128, 8, 128], BF16)
            elun = lambda sf, db: emit_elunorm(sf, db, tmp_f, eq_f, sq_f, s16)
            conv_w(Wk_b, bWk, wstage, D, D)
            conv_w(Wv_b, bWv, wstage, D, D)
            conv_w(w0b_b, bWq, wstage, D, D)
            for tt in range(NT):
                xt = xt_f[tt % 2]
                P.op("sync", lambda e, b=b, tt=tt, xt=xt: e.dma_start(
                    out=xt[:], in_=xres[b, tt * 128:(tt + 1) * 128, :]),
                    after=[dve, act], inc=[ld], dma=True)
                emit_ln(xt[:], xh_bf[:], D, 1e-5, dep=(ld,))
                emit_transposes(xh_bf, 8, None,
                                lambda g4: xhT[:, g4 * 4:(g4 + 1) * 4, :])
                for W_b, bias_nm, dstt in ((w0b_b, "bq_bc", q_f), (Wk_b, "bk_bc", k_f),
                                           (Wv_b, "bv_bc", v_bf)):
                    for n in range(2):
                        ps = big_ps[:, n * 512:n * 512 + 512]
                        for k in range(8):
                            aft = [act, pe, dve, gp] if k == 0 else [pe]
                            P.op("tensor", lambda e, n=n, k=k, W_b=W_b, ps=ps: e.matmul(
                                ps, xhT[:, k, :], W_b[:, k, n * 512:(n + 1) * 512],
                                start=(k == 0), stop=(k == 7)), after=aft, inc=[pe])
                    P.op("vector", lambda e, dstt=dstt, bias_nm=bias_nm: e.tensor_tensor(
                        out=dstt[:], in0=big_ps[:, 0:1024], in1=bias_t[bias_nm][:],
                        op=ALU.add), after=[pe, dve], inc=[dve])
                elun(q_f, n_bf)
                P.op("vector", lambda e, tt=tt: e.tensor_copy(qn_keep[:, tt, :], n_bf[:]),
                     after=[dve], inc=[dve])
                if dbg:
                    P.op("vector", lambda e: e.tensor_copy(tmp_f[:], n_bf[:]),
                         after=[dve], inc=[dve])
                    P.op("sync", lambda e, b=b, tt=tt: e.dma_start(
                        out=qn_dbg[b, tt * 128:(tt + 1) * 128, :], in_=tmp_f[:]),
                        after=[dve], inc=[st], dma=True)
                elun(k_f, n_bf)
                for h in range(H):
                    hrow = (h % 2) * 64
                    hm = h // 2
                    aft = [dve, pe] if h == 0 else [pe]
                    P.op("tensor", lambda e, h=h, hrow=hrow, hm=hm, tt=tt: e.matmul(
                        kv_ps[hrow:hrow + 64, hm * 64:(hm + 1) * 64],
                        n_bf[:, h * 64:(h + 1) * 64], v_bf[:, h * 64:(h + 1) * 64],
                        start=(tt == 0 and h < 2), stop=(tt == NT - 1)),
                        after=aft, inc=[pe])
            P.op("scalar", lambda e: e.activation(out=kv_bf[:], in_=kv_ps, func=AF.Copy,
                                                  bias=0.0, scale=0.125),
                 after=[pe, act], inc=[act])
            if dbg:
                P.op("vector", lambda e: e.tensor_copy(xh_f[:, 0:512], kv_bf[:]),
                     after=[act, dve], inc=[dve])
                P.op("sync", lambda e, b=b: e.dma_start(
                    out=kv_dbg[b], in_=xh_f[:, 0:512]), after=[dve], inc=[st], dma=True)
            conv_w(w0b_b, bWo, wstage, D, D)
            Wo_b = w0b_b
            for tt in range(NT):
                xt = xt_f[tt % 2]
                P.op("sync", lambda e, b=b, tt=tt, xt=xt: e.dma_start(
                    out=xt[:], in_=xres[b, tt * 128:(tt + 1) * 128, :]),
                    after=[dve, act], inc=[ld], dma=True)
                emit_ln(xt[:], xh_f[:], D, 1e-5, dep=(ld,))
                P.op("vector", lambda e, tt=tt: e.tensor_copy(xh_bf[:], qn_keep[:, tt, :]),
                     after=[dve, pe], inc=[dve])
                emit_transposes(xh_bf, 8, None,
                                lambda g4: xhT[:, g4 * 4:(g4 + 1) * 4, :])
                for h in range(H):
                    hrow = (h % 2) * 64
                    hm = h // 2
                    aft = [act, pe] if h == 0 else [pe]
                    P.op("tensor", lambda e, hrow=hrow, hm=hm, h=h: e.matmul(
                        big_ps[hrow:hrow + 64, hm * 128:hm * 128 + 128],
                        kv_bf[hrow:hrow + 64, hm * 64:(hm + 1) * 64],
                        xhT[hrow:hrow + 64, h // 2, :],
                        start=True, stop=True), after=aft, inc=[pe])
                P.op("scalar", lambda e: e.copy(
                    ctxbT[:].rearrange("p a b -> p (a b)"), big_ps[:, 0:1024]),
                    after=[pe, act], inc=[act])
                for n in range(2):
                    for k in range(8):
                        aft = [act, pe, dve] if k == 0 else [pe]
                        P.op("tensor", lambda e, n=n, k=k: e.matmul(
                            o_ps[:, n * 512:(n + 1) * 512], ctxbT[:, k, :],
                            Wo_b[:, k, n * 512:(n + 1) * 512],
                            start=(k == 0), stop=(k == 7)), after=aft, inc=[pe])
                P.op("scalar", lambda e: e.copy(o_f[:], o_ps[:]),
                     after=[pe, act, dve], inc=[act])
                if dbg:
                    P.op("sync", lambda e, b=b, tt=tt: e.dma_start(
                        out=ctx_dbg[b, tt * 128:(tt + 1) * 128, :], in_=o_f[:]),
                        after=[act, dve], inc=[st], dma=True)
                P.op("vector", lambda e: e.tensor_tensor(
                    out=o_f[:], in0=o_f[:], in1=bias_t["bo_bc"][:], op=ALU.add),
                    after=[act, dve, st], inc=[dve])
                P.op("vector", lambda e: e.tensor_tensor(
                    out=o_f[:], in0=o_f[:], in1=xh_f[:], op=ALU.add),
                    after=[dve], inc=[dve])
                emit_ln(o_f[:], ao_f[:], D, 1e-12)
                P.op("vector", lambda e: e.tensor_tensor(
                    out=ao_f[:], in0=ao_f[:], in1=bias_t["g1_bc"][:], op=ALU.mult),
                    after=[dve], inc=[dve])
                P.op("vector", lambda e: e.tensor_tensor(
                    out=ao_f[:], in0=ao_f[:], in1=bias_t["be1_bc"][:],
                    op=ALU.add), after=[dve, st], inc=[dve])
                P.op("sync", lambda e, b=b, tt=tt: e.dma_start(
                    out=ao_scr[b, tt * 128:(tt + 1) * 128, :], in_=ao_f[:]),
                    after=[dve], inc=[st], dma=True)
                P.op("vector", lambda e, tt=tt: e.tensor_copy(ao_keep[:, tt, :], ao_f[:]),
                     after=[dve], inc=[dve])
            P.join()
            b12.close()
            bfa = ExitStack()
            W1_b = sb(bfa, f"W1_b{b}", [128, 8, FF], BF16)
            b1_t = sb(bfa, f"b1_t{b}", [128, FF], BF16)
            wstage4 = sb(bfa, f"wstage4{b}", [128, 2048], F32)
            hg_bf = [sb(bfa, f"hg_bf{b}{i}", [128, FF], BF16) for i in range(2)]
            aoT = sb(bfa, f"aoT{b}", [128, 8, 128], BF16)
            hpre = sb(bfa, f"hpre{b}", [128, 2048], F32)
            for half in range(2):
                P.op("sync", lambda e, half=half: e.dma_start(
                    out=wstage4[:], in_=bia["b1_bc"][:, half * 2048:(half + 1) * 2048]),
                    after=[dve], inc=[ld], dma=True)
                P.op("vector", lambda e, half=half: e.tensor_copy(
                    b1_t[:, half * 2048:(half + 1) * 2048], wstage4[:]),
                    after=[ld], inc=[dve])
            for c in range(8):
                for half in range(2):
                    P.op("sync", lambda e, c=c, half=half: e.dma_start(
                        out=wstage4[:],
                        in_=bW1[c * 128:(c + 1) * 128, half * 2048:(half + 1) * 2048]),
                        after=[dve], inc=[ld], dma=True)
                    P.op("vector", lambda e, c=c, half=half: e.tensor_copy(
                        W1_b[:, c, half * 2048:(half + 1) * 2048], wstage4[:]),
                        after=[ld, pe, act], inc=[dve])
            for tt in range(NT):
                P.op("vector", lambda e, tt=tt: e.tensor_copy(xh_bf[:], ao_keep[:, tt, :]),
                     after=[dve, pe], inc=[dve])
                emit_transposes(xh_bf, 8, None,
                                lambda g4: aoT[:, g4 * 4:(g4 + 1) * 4, :])
                hg = hg_bf[tt % 2]
                for n in range(8):
                    ps = big_ps[:, (n % 4) * 512:(n % 4) * 512 + 512]
                    for k in range(8):
                        aft = [act, pe, dve] if k == 0 else [pe]
                        P.op("tensor", lambda e, n=n, k=k, ps=ps: e.matmul(
                            ps, aoT[:, k, :], W1_b[:, k, n * 512:(n + 1) * 512],
                            start=(k == 0), stop=(k == 7)), after=aft, inc=[pe])
                    hp = hpre[:, (n % 4) * 512:(n % 4) * 512 + 512]
                    P.op("vector", lambda e, n=n, ps=ps, hp=hp: e.tensor_tensor(
                        out=hp, in0=ps,
                        in1=b1_t[:, n * 512:(n + 1) * 512], op=ALU.add),
                        after=[pe, dve, act], inc=[dve])
                    P.op("scalar", lambda e, n=n, hg=hg, hp=hp: e.activation(
                        out=hg[:, n * 512:(n + 1) * 512],
                        in_=hp, func=AF.Gelu,
                        bias=0.0, scale=1.0), after=[dve, act, st], inc=[act])
                P.op("sync", lambda e, b=b, tt=tt, hg=hg: e.dma_start(
                    out=tq_scr[b, tt * 128:(tt + 1) * 128, :], in_=hg[:]),
                    after=[act], inc=[st], dma=True)
            P.join()
            bfa.close()
            bfb = ExitStack()
            W2_b = sb(bfb, f"W2_b{b}", [128, 32, D], BF16)
            wstage5 = sb(bfb, f"wstage5{b}", [128, D], F32)
            hgi = [sb(bfb, f"hgi{b}{i}", [128, FF], BF16) for i in range(2)]
            hgT = sb(bfb, f"hgT{b}", [128, 32, 128], BF16)
            d_f = sb(bfb, f"d_f3{b}", [128, D], F32)
            aor_f = [sb(bfb, f"aor_f{b}{i}", [128, D], F32) for i in range(2)]
            conv_w(W2_b, bW2, wstage5, FF, D)
            for tt in range(NT):
                hg = hgi[tt % 2]
                P.op("sync", lambda e, b=b, tt=tt, hg=hg: e.dma_start(
                    out=hg[:], in_=tq_scr[b, tt * 128:(tt + 1) * 128, :]),
                    after=[pe], inc=[ld], dma=True)
                P.op("sync", lambda e, b=b, tt=tt: e.dma_start(
                    out=aor_f[tt % 2][:], in_=ao_scr[b, tt * 128:(tt + 1) * 128, :]),
                    after=[dve], inc=[ld], dma=True)
                emit_transposes(hg, 32, None,
                                lambda g4: hgT[:, g4 * 4:(g4 + 1) * 4, :],
                                extra_first=[ld])
                for n in range(2):
                    for k in range(32):
                        aft = [act, pe, dve] if k == 0 else [pe]
                        P.op("tensor", lambda e, n=n, k=k: e.matmul(
                            o_ps[:, n * 512:(n + 1) * 512], hgT[:, k, :],
                            W2_b[:, k, n * 512:(n + 1) * 512],
                            start=(k == 0), stop=(k == 31)), after=aft, inc=[pe])
                P.op("scalar", lambda e: e.copy(d_f[:], o_ps[:]),
                     after=[pe, act, dve], inc=[act])
                P.op("vector", lambda e: e.tensor_tensor(
                    out=d_f[:], in0=d_f[:], in1=bias_t["b2_bc"][:], op=ALU.add),
                    after=[act, dve], inc=[dve])
                P.op("vector", lambda e, tt=tt, ao=None: e.tensor_tensor(
                    out=d_f[:], in0=d_f[:], in1=aor_f[tt % 2][:], op=ALU.add),
                    after=[dve, ld], inc=[dve])
                emit_ln(d_f[:], o_f[:], D, 1e-12)
                P.op("vector", lambda e: e.tensor_tensor(
                    out=o_f[:], in0=o_f[:], in1=bias_t["g2_bc"][:], op=ALU.mult),
                    after=[dve], inc=[dve])
                P.op("vector", lambda e: e.tensor_tensor(
                    out=o_f[:], in0=o_f[:], in1=bias_t["be2_bc"][:], op=ALU.add),
                    after=[dve], inc=[dve])
                xt = xt_f[tt % 2]
                P.op("sync", lambda e, b=b, tt=tt, xt=xt: e.dma_start(
                    out=xt[:], in_=xres[b, tt * 128:(tt + 1) * 128, :]),
                    after=[dve], inc=[ld], dma=True)
                P.op("vector", lambda e, xt=xt: e.tensor_tensor(
                    out=xt[:], in0=xt[:], in1=o_f[:], op=ALU.add),
                    after=[ld, dve], inc=[dve])
                P.op("sync", lambda e, b=b, tt=tt, xt=xt: e.dma_start(
                    out=dst[b, tt * 128:(tt + 1) * 128, :], in_=xt[:]),
                    after=[dve], inc=[st], dma=True)
            P.join()
            bfb.close()
        P.join()
        b3.close()

    emit_attn_b(out if upto == 3 else xres)
    if upto == 3:
        P.join()
        P.emit()
        return nc, stack

    emit_ffn(xres, out, 1)
    P.join()
    P.emit()
    return nc, stack


def _host_layouts(inputs):
    f = lambda a: np.ascontiguousarray(a, dtype=np.float32)
    L = {}
    L["ident"] = np.eye(128, dtype=ml_dtypes.bfloat16)
    L["qWT"] = f(np.asarray(inputs["L0_attn_a_qW"]).T)
    L["kWT"] = f(np.asarray(inputs["L0_attn_a_kW"]).T)
    L["vWT"] = f(np.asarray(inputs["L0_attn_a_vW"]).T)
    L["oWT"] = f(np.asarray(inputs["L0_attn_a_oW"]).T)
    L["qbT"] = f(np.asarray(inputs["L0_attn_a_qb"]).reshape(8, 128).T)
    L["kbT"] = f(np.asarray(inputs["L0_attn_a_kb"]).reshape(8, 128).T)
    L["vb_bc"] = f(np.broadcast_to(np.asarray(inputs["L0_attn_a_vb"]), (128, D)))
    L["ob_bc"] = f(np.broadcast_to(np.asarray(inputs["L0_attn_a_ob"]), (128, D)))
    L["skT"] = f(np.asarray(inputs["L0_attn_a_static_keys"])
                 .reshape(8, 8, 128).transpose(2, 1, 0))
    L["wgT"] = f(np.asarray(inputs["L0_ffn_w_gate"]).T)
    L["wuT"] = f(np.asarray(inputs["L0_ffn_w_up"]).T)
    L["wdT"] = f(np.asarray(inputs["L0_ffn_w_down"]).T)
    L["bWq"] = f(np.asarray(inputs["L0_attn_b_Wq"]))
    L["bWk"] = f(np.asarray(inputs["L0_attn_b_Wk"]))
    L["bWv"] = f(np.asarray(inputs["L0_attn_b_Wv"]))
    L["bWo"] = f(np.asarray(inputs["L0_attn_b_Wo"]))
    L["bW1"] = f(np.asarray(inputs["L0_attn_b_W1"]))
    L["bW2"] = f(np.asarray(inputs["L0_attn_b_W2"]))
    for nm, key in (("bq_bc", "L0_attn_b_bq"), ("bk_bc", "L0_attn_b_bk"),
                    ("bv_bc", "L0_attn_b_bv"), ("bo_bc", "L0_attn_b_bo"),
                    ("b2_bc", "L0_attn_b_b2"), ("g1_bc", "L0_attn_b_g1"),
                    ("be1_bc", "L0_attn_b_be1"), ("g2_bc", "L0_attn_b_g2"),
                    ("be2_bc", "L0_attn_b_be2")):
        L[nm] = f(np.broadcast_to(np.asarray(inputs[key]), (128, D)))
    L["b1_bc"] = f(np.broadcast_to(np.asarray(inputs["L0_attn_b_b1"]), (128, FF)))
    return L


def kernel(**inputs):
    x = np.ascontiguousarray(np.asarray(inputs["L0_attn_a_x"], dtype=np.float32))
    L = _host_layouts(inputs)
    nc, stack = build_program(upto=4)
    in_maps = []
    for c in range(NCORES):
        m = dict(L)
        m["x"] = np.ascontiguousarray(x[c * BPC:(c + 1) * BPC])
        in_maps.append(m)
    res = run_bass_kernel_spmd(nc, in_maps, core_ids=list(range(NCORES)))
    stack.close()
    return np.concatenate([r["out"] for r in res.results], axis=0)


# version 14
# speedup vs baseline: 2.3641x; 1.2392x over previous
import numpy as np
import ml_dtypes
from contextlib import ExitStack

import concourse.bass as bass
import concourse.mybir as mybir
from concourse.bass_utils import run_bass_kernel_spmd

F32 = mybir.dt.float32
BF16 = mybir.dt.bfloat16
I8 = mybir.dt.int8
AF = mybir.ActivationFunctionType
ALU = mybir.AluOpType
AX = mybir.AxisListType

NCORES = 8
BPC = 2
S = 2048
D = 1024
H = 16
FF = 4096
NT = S // 128


class Chan:
    __slots__ = ("sem", "count")

    def __init__(self, nc, stack, name):
        self.sem = stack.enter_context(nc.semaphore(name))
        self.count = 0

    def tok(self):
        return (self.sem, self.count)


class Prog:
    ENGINES = ("tensor", "vector", "scalar", "gpsimd", "sync")

    def __init__(self, nc, stack):
        self.nc = nc
        self.stack = stack
        self.ops = []
        self.chans = {}

    def chan(self, name):
        if name not in self.chans:
            self.chans[name] = Chan(self.nc, self.stack, name)
        return self.chans[name]

    def op(self, engine, fn, after=(), inc=(), dma=False):
        waits = []
        for a in after:
            if a is None:
                continue
            if isinstance(a, Chan):
                if a.count > 0:
                    waits.append((a.sem, a.count))
            else:
                sem, val = a
                if val > 0:
                    waits.append((sem, val))
        wd = {}
        for sem, val in waits:
            k = id(sem)
            if k not in wd or wd[k][1] < val:
                wd[k] = (sem, val)
        incs = []
        toks = {}
        for c in inc:
            n = 16 if dma else 1
            c.count += n
            incs.append((c.sem, n))
            toks[c] = c.tok()
        self.ops.append((engine, fn, list(wd.values()), incs))
        return toks

    def join(self):
        toks = [(c.sem, c.count) for c in self.chans.values() if c.count > 0]
        for e in self.ENGINES:
            self.ops.append((e, None, toks, []))

    def emit(self):
        nc = self.nc
        with nc.Block() as block:
            for eng_name in self.ENGINES:
                section = getattr(block, eng_name)

                def make(name):
                    def body(eng):
                        for engine, fn, waits, incs in self.ops:
                            if engine != name:
                                continue
                            for sem, val in waits:
                                eng.wait_ge(sem, val)
                            inst = fn(eng) if fn is not None else None
                            if incs:
                                if inst is None:
                                    inst = eng.nop()
                                for sem, n in incs:
                                    inst.then_inc(sem, n)
                    return body

                section(make(eng_name))


def build_program(upto=4, dbg=False):
    nc = bass.Bass("TRN2", target_bir_lowering=False)
    stack = ExitStack()
    P = Prog(nc, stack)
    ld, st = P.chan("ld"), P.chan("st")
    pe, dve, act, gp = P.chan("pe"), P.chan("dve"), P.chan("act"), P.chan("gp")
    dA = [P.chan("dA0"), P.chan("dA1")]
    dB = [P.chan("dB0"), P.chan("dB1")]
    dW = P.chan("dW")
    dC = [P.chan("dC0"), P.chan("dC1")]
    sA = [P.chan("sA0"), P.chan("sA1")]

    dram = lambda n, sh, dt, k="ExternalInput": nc.dram_tensor(n, sh, dt, kind=k).ap()
    x_in = dram("x", [BPC, S, D], F32)
    out = dram("out", [BPC, S, D], F32, "ExternalOutput")
    ident = dram("ident", [128, 128], BF16)
    qWT = dram("qWT", [D, D], F32)
    kWT = dram("kWT", [D, D], F32)
    vWT = dram("vWT", [D, D], F32)
    oWT = dram("oWT", [D, D], F32)
    qbT = dram("qbT", [128, 8], F32)
    kbT = dram("kbT", [128, 8], F32)
    vb_bc = dram("vb_bc", [128, D], F32)
    ob_bc = dram("ob_bc", [128, D], F32)
    skT = dram("skT", [128, 8, 8], F32)
    wgT = dram("wgT", [D, FF], F32)
    wuT = dram("wuT", [D, FF], F32)
    wdT = dram("wdT", [FF, D], F32)
    bWq = dram("bWq", [D, D], F32)
    bWk = dram("bWk", [D, D], F32)
    bWv = dram("bWv", [D, D], F32)
    bWo = dram("bWo", [D, D], F32)
    bW1 = dram("bW1", [D, FF], F32)
    bW2 = dram("bW2", [FF, D], F32)
    bia = {nm: dram(nm, [128, FF if nm == "b1_bc" else D], F32)
           for nm in ("bq_bc", "bk_bc", "bv_bc", "bo_bc", "b1_bc", "b2_bc",
                      "g1_bc", "be1_bc", "g2_bc", "be2_bc")}

    xres = dram("xres", [BPC, S, D], F32, "Internal")
    ao_scr = dram("ao_scr", [BPC, S, D], F32, "ExternalOutput" if dbg else "Internal")
    if dbg:
        qn_dbg = dram("qn_dbg", [BPC, S, D], F32, "ExternalOutput")
        kv_dbg = dram("kv_dbg", [BPC, 128, 512], F32, "ExternalOutput")
        ctx_dbg = dram("ctx_dbg", [BPC, S, D], F32, "ExternalOutput")
    tq_scr = dram("tq_scr", [BPC, S, FF], BF16,
                  "ExternalOutput" if dbg == "ffn" else "Internal")
    if dbg == "ffn":
        dq_x = dram("dq_x", [BPC, S, D], F32, "ExternalOutput")
        dq_xq = dram("dq_xq", [BPC, S, D], BF16, "ExternalOutput")
        dq_gu = dram("dq_gu", [BPC, S, FF], F32, "ExternalOutput")
    wq_scr = {"g": dram("wgq_scr", [D, FF], BF16, "Internal"),
              "u": dram("wuq_scr", [D, FF], BF16, "Internal"),
              "d": dram("wdq_scr", [FF, D], BF16, "Internal")}

    sb = lambda stk, n, sh, dt: stk.enter_context(nc.sbuf_tensor(n, sh, dt))
    pst = lambda stk, n, sh, dt=F32: stk.enter_context(nc.psum_tensor(n, sh, dt))

    id_t = sb(stack, "id_t", [128, 128], BF16)
    P.op("sync", lambda e: e.dma_start(out=id_t[:], in_=ident), inc=[ld], dma=True)
    stat = sb(stack, "stat", [128, 32], F32)
    w_scales = sb(stack, "w_scales", [128, 4], F32)
    stq = sb(stack, "stq", [128, 2 * NT], F32)
    junk = sb(stack, "junk", [128, 2 * D], BF16)
    eps5 = sb(stack, "eps5", [128, 1], F32)
    eps12 = sb(stack, "eps12", [128, 1], F32)

    P.op("vector", lambda e: e.memset(eps5[:], 1e-5), inc=[dve])
    P.op("vector", lambda e: e.memset(eps12[:], 1e-12), inc=[dve])

    big_ps = pst(stack, "big_ps", [128, 2048])
    o_ps = pst(stack, "o_ps", [128, 1024])
    tr_ps = pst(stack, "tr_ps", [128, 2048], BF16)

    def emit_transposes(src_bf, n_chunks, dst, dst_fn, extra_first=()):
        for m in range(n_chunks):
            g4 = m // 4
            bank = g4 % 2
            aft = [pe]
            if m % 4 == 0:
                aft = [pe, act, dve, gp] + list(extra_first)
            P.op("tensor", lambda e, m=m, bank=bank: e.transpose(
                tr_ps[:, bank * 1024 + (m % 4) * 128: bank * 1024 + (m % 4) * 128 + 128],
                src_bf[:, m * 128:(m + 1) * 128], id_t[:]), after=aft, inc=[pe])
            if m % 4 == 3:
                P.op("scalar", lambda e, g4=g4, bank=bank: e.copy(
                    dst_fn(g4),
                    tr_ps[:, bank * 1024: bank * 1024 + 512]
                    .rearrange("p (a b) -> p a b", a=4)),
                    after=[pe, act], inc=[act])

    def emit_ln(in_ap, out_ap, width, eps, dep=()):
        eps_ap = (eps5 if eps > 1e-9 else eps12)[:]
        inv = 1.0 / width
        P.op("vector", lambda e: e.tensor_scalar(
            out=junk[:, 0:width], in0=in_ap, scalar1=inv, scalar2=0.0, op0=ALU.mult,
            op1=ALU.add, accum_out=stat[:, 0:1]), after=(dve, act) + tuple(dep), inc=[dve])
        sq = float(np.sqrt(inv))
        P.op("scalar", lambda e: e.activation(
            out=junk[:, D:D + width], in_=in_ap, func=AF.Square, bias=0.0, scale=sq,
            accum_out=stat[:, 1:2]), after=(act, dve) + tuple(dep), inc=[act])
        P.op("vector", lambda e: e.scalar_tensor_tensor(
            out=stat[:, 3:4], in0=stat[:, 0:1], scalar=-1.0, in1=stat[:, 0:1],
            op0=ALU.mult, op1=ALU.mult), after=(dve, act), inc=[dve])
        P.op("vector", lambda e: e.tensor_tensor(
            out=stat[:, 1:2], in0=stat[:, 1:2], in1=stat[:, 3:4], op=ALU.add),
            after=(dve,), inc=[dve])
        P.op("scalar", lambda e: e.activation(
            out=stat[:, 2:3], in_=stat[:, 1:2], func=AF.Ln, bias=eps_ap, scale=1.0),
            after=(act, dve), inc=[act])
        P.op("scalar", lambda e: e.activation(
            out=stat[:, 2:3], in_=stat[:, 2:3], func=AF.Exp, bias=0.0, scale=-0.5),
            after=(act,), inc=[act])
        return P.op("vector", lambda e: e.tensor_scalar(
            out=out_ap, in0=in_ap, scalar1=stat[:, 0:1], scalar2=stat[:, 2:3],
            op0=ALU.subtract, op1=ALU.mult), after=(dve, act, pe, gp), inc=[dve])

    def ln_t(in_ap, out_ap, width, eps_ap, lnA, prev, dep_dve=(), dep_act=(),
             dep_out=()):
        inv = 1.0 / width
        sq = float(np.sqrt(inv))
        P.op("vector", lambda e: e.tensor_scalar(
            out=junk[:, 0:width], in0=in_ap, scalar1=inv, scalar2=0.0, op0=ALU.mult,
            op1=ALU.add, accum_out=lnA[:, 0:1]), after=tuple(dep_dve), inc=[dve])
        t_sq = P.op("scalar", lambda e: e.activation(
            out=junk[:, D:D + width], in_=in_ap, func=AF.Square, bias=0.0, scale=sq,
            accum_out=lnA[:, 1:2]),
            after=tuple(dep_act) + (prev.get("tt"),), inc=[act])[act]
        P.op("vector", lambda e: e.scalar_tensor_tensor(
            out=lnA[:, 3:4], in0=lnA[:, 0:1], scalar=-1.0, in1=lnA[:, 0:1],
            op0=ALU.mult, op1=ALU.mult), after=[dve], inc=[dve])
        t_tt = P.op("vector", lambda e: e.tensor_tensor(
            out=lnA[:, 1:2], in0=lnA[:, 1:2], in1=lnA[:, 3:4], op=ALU.add),
            after=(t_sq, dve), inc=[dve])[dve]
        P.op("scalar", lambda e: e.activation(
            out=lnA[:, 2:3], in_=lnA[:, 1:2], func=AF.Ln, bias=eps_ap, scale=1.0),
            after=(t_tt, prev.get("fin")), inc=[act])
        t_exp = P.op("scalar", lambda e: e.activation(
            out=lnA[:, 2:3], in_=lnA[:, 2:3], func=AF.Exp, bias=0.0, scale=-0.5),
            after=[act], inc=[act])[act]
        t_fin = P.op("vector", lambda e: e.tensor_scalar(
            out=out_ap, in0=in_ap, scalar1=lnA[:, 0:1], scalar2=lnA[:, 2:3],
            op0=ALU.subtract, op1=ALU.mult),
            after=(t_exp, dve) + tuple(dep_out), inc=[dve])[dve]
        prev["tt"] = t_tt
        prev["fin"] = t_fin
        return {"fin": t_fin, "sq": t_sq}

    def conv_w(dst_b, src, wstage, rows, cols):
        for c in range(rows // 128):
            P.op("sync", lambda e, c=c: e.dma_start(
                out=wstage[:, 0:cols], in_=src[c * 128:(c + 1) * 128, :]),
                after=[dve], inc=[ld], dma=True)
            P.op("vector", lambda e, c=c: e.tensor_copy(dst_b[:, c, :], wstage[:, 0:cols]),
                 after=[ld, pe, act], inc=[dve])

    p0 = ExitStack()
    wstage0 = [sb(p0, f"wstage0{i}", [128, FF], F32) for i in range(2)]
    qi8 = [sb(p0, f"qi8{i}", [128, FF], I8) for i in range(2)]
    qbf = [sb(p0, f"qbf{i}", [128, FF], BF16) for i in range(2)]
    pq_tok = [None, None]
    sq_tok = [None, None]
    ones_col = sb(p0, "ones_col", [128, 1], F32)
    ones_row = sb(p0, "ones_row", [1, 128], F32)
    P.op("vector", lambda e: e.memset(ones_col[:], 1.0), inc=[dve])
    P.op("vector", lambda e: e.memset(ones_row[:], 1.0), inc=[dve])

    def quant_weight(src, rows, cols, scr, scol):
        nch = rows // 128
        acc = stat[:, 8:9]
        red = stat[:, 9:10]
        P.op("vector", lambda e: e.memset(acc, 0.0), after=[dve], inc=[dve])
        for c in range(nch):
            p = c % 2
            lt = P.op("sync", lambda e, c=c, p=p: e.dma_start(
                out=wstage0[p][:, 0:cols], in_=src[c * 128:(c + 1) * 128, :]),
                after=(pq_tok[p],), inc=[dA[p]], dma=True)[dA[p]]
            pq_tok[p] = P.op("vector", lambda e, p=p: e.reduce_sum(
                out=red, in_=wstage0[p][:, 0:cols], axis=AX.X,
                apply_absolute_value=True), after=(lt, dve), inc=[dve])[dve]
            P.op("vector", lambda e: e.tensor_tensor(out=acc, in0=acc, in1=red, op=ALU.add),
                 after=[dve], inc=[dve])
        P.op("tensor", lambda e: e.matmul(big_ps[0:1, 0:1], acc, ones_col[:],
                                          start=True, stop=True),
             after=[dve, pe, act], inc=[pe])
        P.op("vector", lambda e: e.tensor_scalar(
            out=stat[0:1, 10:11], in0=big_ps[0:1, 0:1], scalar1=1.0 / (rows * cols),
            scalar2=1e-5, op0=ALU.mult, op1=ALU.max), after=[pe, dve], inc=[dve])
        P.op("tensor", lambda e: e.matmul(big_ps[:, 1:2], ones_row[:], stat[0:1, 10:11],
                                          start=True, stop=True),
             after=[dve, pe], inc=[pe])
        P.op("vector", lambda e, scol=scol: e.tensor_copy(w_scales[:, scol:scol + 1],
                                                          big_ps[:, 1:2]),
             after=[pe, dve], inc=[dve])
        P.op("vector", lambda e: e.reciprocal(out=stat[:, 11:12],
                                              in_=w_scales[:, scol:scol + 1]),
             after=[dve], inc=[dve])
        for c in range(nch):
            p = c % 2
            lt = P.op("sync", lambda e, c=c, p=p: e.dma_start(
                out=wstage0[p][:, 0:cols], in_=src[c * 128:(c + 1) * 128, :]),
                after=(pq_tok[p],), inc=[dA[p]], dma=True)[dA[p]]
            pq_tok[p] = P.op("vector", lambda e, p=p: e.tensor_scalar(
                out=qi8[p][:, 0:cols], in0=wstage0[p][:, 0:cols],
                scalar1=stat[:, 11:12], scalar2=None, op0=ALU.mult),
                after=(lt, dve), inc=[dve])[dve]
            P.op("vector", lambda e, p=p: e.tensor_copy(
                qbf[p][:, 0:cols], qi8[p][:, 0:cols]),
                after=(sq_tok[p], dve), inc=[dve])
            cl = P.op("vector", lambda e, p=p: e.tensor_scalar(
                out=qbf[p][:, 0:cols], in0=qbf[p][:, 0:cols], scalar1=1.0,
                scalar2=-1.0, op0=ALU.min, op1=ALU.max),
                after=[dve], inc=[dve])[dve]
            sq_tok[p] = P.op("sync", lambda e, c=c, p=p: e.dma_start(
                out=scr[c * 128:(c + 1) * 128, :], in_=qbf[p][:, 0:cols]),
                after=(cl,), inc=[sA[p]], dma=True)[sA[p]]

    quant_weight(wgT, D, FF, wq_scr["g"], 0)
    quant_weight(wuT, D, FF, wq_scr["u"], 1)
    quant_weight(wdT, FF, D, wq_scr["d"], 2)
    P.join()
    p0.close()

    def emit_attn_a(dst):
        p1 = ExitStack()
        wstage = junk[:].bitcast(F32)
        w0_b = sb(p1, "w0_b", [128, 8, D], BF16)
        qbT_t = sb(p1, "qbT_t", [128, 8], F32)
        kbT_t = sb(p1, "kbT_t", [128, 8], F32)
        skT_t = sb(p1, "skT_t", [128, 8, 8], F32)
        vb_t = sb(p1, "vb_t", [128, D], BF16)
        ob_t = sb(p1, "ob_t", [128, D], F32)
        QT = sb(p1, "QT", [128, 8, S], BF16)
        KT = sb(p1, "KT", [128, 8, S], BF16)
        Vt = sb(p1, "Vt", [128, NT, D], BF16)
        xt_f = [sb(p1, f"xt_f{i}", [128, D], F32) for i in range(2)]
        xhat_b = sb(p1, "xhat_b", [128, D], BF16)
        PTb = sb(p1, "PTb", [128, NT, 256], BF16)
        xTst = PTb[:].rearrange("p a b -> p (a b)").rearrange("p (a b) -> p a b", a=8)
        un = sb(p1, "un", [128, 16384], BF16)
        unf = un[:].bitcast(F32)
        e_sbs = [unf[:, 0:2048], unf[:, 2048:4096]]
        e2f = unf[:, 4096:6144]
        pbufs = [un[:, 12288:14336], un[:, 14336:16384]]
        tb16s = sb(p1, "tb16s", [128, 2, 16], F32)
        ast = sb(p1, "ast", [128, 8], F32)
        ctxT = sb(p1, "ctxT", [128, 8, 256], BF16)
        kWT_b = un[:, 0:8192].rearrange("p (a b) -> p a b", a=8)
        vWT_b = un[:, 8192:16384].rearrange("p (a b) -> p a b", a=8)

        P.op("sync", lambda e: e.dma_start(out=wstage[:, 0:D], in_=vb_bc),
             after=[dve], inc=[ld], dma=True)
        P.op("vector", lambda e: e.tensor_copy(vb_t[:], wstage[:, 0:D]),
             after=[ld], inc=[dve])
        P.op("sync", lambda e: e.dma_start(out=qbT_t[:], in_=qbT), inc=[ld], dma=True)
        P.op("sync", lambda e: e.dma_start(out=kbT_t[:], in_=kbT), inc=[ld], dma=True)
        P.op("sync", lambda e: e.dma_start(out=skT_t[:], in_=skT), inc=[ld], dma=True)
        P.op("sync", lambda e: e.dma_start(out=ob_t[:], in_=ob_bc), inc=[ld], dma=True)
        P.op("vector", lambda e: e.tensor_scalar(out=qbT_t[:], in0=qbT_t[:], scalar1=0.125,
                                                 scalar2=None, op0=ALU.mult),
             after=[ld, dve], inc=[dve])

        a1bank_tok = [None, None, None, None]
        for b in range(BPC):
            conv_w(kWT_b, kWT, wstage, D, D)
            conv_w(vWT_b, vWT, wstage, D, D)
            conv_w(w0_b, qWT, wstage, D, D)
            for sst in range(4):
                for sub in range(4):
                    tt = sst * 4 + sub
                    xt = xt_f[tt % 2]
                    P.op("sync", lambda e, b=b, tt=tt, xt=xt: e.dma_start(
                        out=xt[:], in_=x_in[b, tt * 128:(tt + 1) * 128, :]),
                        after=[dve, act], inc=[ld], dma=True)
                    emit_ln(xt[:], xhat_b[:], D, 1e-5, dep=(ld,))
                    emit_transposes(
                        xhat_b, 8, None,
                        lambda g4, sub=sub: xTst[:, g4 * 4:(g4 + 1) * 4,
                                                 sub * 128:(sub + 1) * 128])
                sstA = (act.sem, act.count)
                first = True
                for WTb, bias, scale, DST in ((w0_b, qbT_t, 0.125, QT),
                                              (kWT_b, kbT_t, 1.0, KT)):
                    for m in range(8):
                        bk = m % 2
                        ps = big_ps[:, bk * 512:bk * 512 + 512]
                        stop = None
                        for k in range(8):
                            aft = []
                            if k == 0:
                                aft.append(a1bank_tok[bk])
                                if first:
                                    aft.append(sstA)
                                    first = False
                            stop = P.op("tensor", lambda e, m=m, k=k, WTb=WTb, ps=ps:
                                        e.matmul(
                                ps, WTb[:, k, m * 128:(m + 1) * 128], xTst[:, k, :],
                                start=(k == 0), stop=(k == 7)), after=aft, inc=[pe])[pe]
                        a1bank_tok[bk] = P.op("scalar", lambda e, m=m, ps=ps, DST=DST,
                                              bias=bias, scale=scale, sst=sst:
                                              e.activation(
                                 out=DST[:, m, sst * 512:(sst + 1) * 512], in_=ps,
                                 func=AF.Identity, bias=bias[:, m:m + 1], scale=scale),
                             after=(stop,), inc=[act])[act]
                for sub in range(4):
                    chunk = sst * 4 + sub
                    for n in range(2):
                        bk = 2 + n
                        ps = big_ps[:, 1024 + n * 512:1024 + n * 512 + 512]
                        stop = None
                        for k in range(8):
                            aft = [a1bank_tok[bk]] if k == 0 else []
                            stop = P.op("tensor", lambda e, sub=sub, n=n, k=k, ps=ps:
                                        e.matmul(
                                ps, xTst[:, k, sub * 128:(sub + 1) * 128],
                                vWT_b[:, k, n * 512:(n + 1) * 512],
                                start=(k == 0), stop=(k == 7)), after=aft, inc=[pe])[pe]
                        a1bank_tok[bk] = P.op("vector", lambda e, chunk=chunk, n=n,
                                              ps=ps: e.tensor_tensor(
                            out=Vt[:, chunk, n * 512:(n + 1) * 512], in0=ps,
                            in1=vb_t[:, n * 512:(n + 1) * 512], op=ALU.add),
                            after=(stop, dve), inc=[dve])[dve]
            P.op("vector", lambda e: e.tensor_copy(
                KT[:, :, 0:8], skT_t[:]), after=[dve, act, pe], inc=[dve])

            conv_w(w0_b, oWT, wstage, D, D)
            oWT_b = w0_b
            P.join()
            exp_tok = [None, None]
            stt_tok = [None, None]
            scale_tok = [None, None]
            transp_tok2 = [None, None]
            trev_tok = [None, None]
            ev_last = [None, None]
            pv_tok = None
            ctx_tok = None
            oproj_tok = None
            xg_out = [None, None]
            xg_add = None
            sstops = {}

            def a2_scores(r):
                g, h, qt = r // 32, (r // 2) % 16, r % 2
                hrow, hm, qg = (h % 2) * 64, h // 2, g * 2 + qt
                stops = [None, None]
                for half in range(2):
                    for n2 in range(2):
                        n = half * 2 + n2
                        aft = [exp_tok[half]] if n2 == 0 else []
                        stops[half] = P.op("tensor", lambda e, hrow=hrow, hm=hm,
                                           qg=qg, n=n: e.matmul(
                            big_ps[:, n * 512:(n + 1) * 512],
                            QT[hrow:hrow + 64, hm, qg * 128:(qg + 1) * 128],
                            KT[hrow:hrow + 64, hm, n * 512:(n + 1) * 512],
                            start=True, stop=True), after=aft, inc=[pe])[pe]
                return stops

            def a2_mask(r, stops):
                pp = r % 2
                exp_tok[0] = P.op("scalar", lambda e, pp=pp: e.activation(
                    out=e_sbs[pp][:, 0:1024], in_=big_ps[:, 0:1024], func=AF.Exp,
                    bias=0.0, scale=1.0),
                    after=(stops[0], stt_tok[pp]), inc=[act])[act]
                exp_tok[1] = P.op("scalar", lambda e, pp=pp: e.activation(
                    out=e_sbs[pp][:, 1024:2048], in_=big_ps[:, 1024:2048],
                    func=AF.Exp, bias=0.0, scale=1.0),
                    after=(stops[1],), inc=[act])[act]
                P.op("vector", lambda e, pp=pp: e.max(
                    out=tb16s[:, pp, 0:8], in_=e_sbs[pp][:]),
                    after=(exp_tok[1], dve), inc=[dve])
                P.op("vector", lambda e, pp=pp: e.match_replace(
                    out=e2f[:], in_to_replace=tb16s[:, pp, 0:8],
                    in_values=e_sbs[pp][:], imm_value=0.0),
                    after=[dve], inc=[dve])
                P.op("vector", lambda e, pp=pp: e.max(
                    out=tb16s[:, pp, 8:16], in_=e2f[:]), after=[dve], inc=[dve])
                P.op("vector", lambda e, pp=pp: e.reduce_sum(
                    out=ast[:, pp * 4:pp * 4 + 1], in_=tb16s[:, pp, :], axis=AX.X),
                    after=[dve], inc=[dve])
                P.op("vector", lambda e, pp=pp: e.reciprocal(
                    out=ast[:, pp * 4 + 1:pp * 4 + 2], in_=ast[:, pp * 4:pp * 4 + 1]),
                    after=[dve], inc=[dve])
                stt_tok[pp] = P.op("vector", lambda e, pp=pp: e.scalar_tensor_tensor(
                    out=pbufs[pp][:], in0=e_sbs[pp][:],
                    scalar=tb16s[:, pp, 15:16], in1=e_sbs[pp][:],
                    op0=ALU.is_ge, op1=ALU.mult),
                    after=(transp_tok2[pp], dve), inc=[dve])[dve]
                scale_tok[pp] = P.op("vector", lambda e, pp=pp: e.tensor_scalar(
                    out=pbufs[pp][:], in0=pbufs[pp][:],
                    scalar1=ast[:, pp * 4 + 1:pp * 4 + 2], scalar2=None,
                    op0=ALU.mult), after=[dve], inc=[dve])[dve]

            def a2_oproj(g):
                nonlocal xg_add, oproj_tok
                for qt in range(2):
                    qg = g * 2 + qt
                    xg = xt_f[qt]
                    ldt = P.op("sync", lambda e, b=b, qg=qg, xg=xg: e.dma_start(
                        out=xg[:], in_=x_in[b, qg * 128:(qg + 1) * 128, :]),
                        after=(xg_out[qt],), inc=[dA[qt]], dma=True)[dA[qt]]
                    gpt = P.op("vector", lambda e, xg=xg: e.tensor_tensor(
                        out=xg[:], in0=xg[:], in1=ob_t[:], op=ALU.add),
                        after=(ldt,), inc=[dve])[dve]
                    for n in range(2):
                        stop = None
                        for k in range(8):
                            aft = []
                            if k == 0:
                                aft.append(xg_add)
                                if n == 0 and qt == 0:
                                    aft.append(ctx_tok)
                            stop = P.op("tensor", lambda e, n=n, k=k, qt=qt: e.matmul(
                                o_ps[:, 512:1024],
                                ctxT[:, k, qt * 128:(qt + 1) * 128],
                                oWT_b[:, k, n * 512:(n + 1) * 512],
                                start=(k == 0), stop=(k == 7)), after=aft, inc=[pe])[pe]
                        xg_add = P.op("vector", lambda e, n=n, xg=xg: e.tensor_tensor(
                            out=xg[:, n * 512:(n + 1) * 512],
                            in0=xg[:, n * 512:(n + 1) * 512],
                            in1=o_ps[:, 512:1024], op=ALU.add),
                            after=(stop, gpt), inc=[dve])[dve]
                    xg_out[qt] = P.op("sync", lambda e, b=b, qg=qg, xg=xg: e.dma_start(
                        out=dst[b, qg * 128:(qg + 1) * 128, :], in_=xg[:]),
                        after=(xg_add,), inc=[sA[qt]], dma=True)[sA[qt]]
                    oproj_tok = stop

            def a2_tail(r):
                nonlocal pv_tok, ctx_tok
                g, h, qt = r // 32, (r // 2) % 16, r % 2
                pp = r % 2
                trstop = [None, None]
                for c in range(16):
                    tb = c // 8
                    aft = []
                    if c == 0:
                        aft.append(scale_tok[pp])
                    if c % 8 == 0:
                        aft.append(trev_tok[tb])
                    tok = P.op("tensor", lambda e, c=c, tb=tb, pp=pp: e.transpose(
                        tr_ps[:, tb * 1024 + (c % 8) * 128:
                              tb * 1024 + (c % 8) * 128 + 128],
                        pbufs[pp][:, c * 128:(c + 1) * 128], id_t[:]),
                        after=aft, inc=[pe])[pe]
                    if c % 8 == 7:
                        trstop[tb] = tok
                transp_tok2[pp] = tok
                for tb in range(2):
                    aft = [trstop[tb]]
                    if qt == 0 and tb == 0:
                        aft.append(pv_tok)
                    trev_tok[tb] = P.op("scalar", lambda e, tb=tb, qt=qt: e.copy(
                        PTb[:, tb * 8:(tb + 1) * 8, qt * 128:(qt + 1) * 128],
                        tr_ps[:, tb * 1024: tb * 1024 + 1024]
                        .rearrange("p (a b) -> p a b", a=8)),
                        after=aft, inc=[act])[act]
                ev_last[qt] = trev_tok[1]
                if qt == 1:
                    hrow = (h % 2) * 64
                    pvstop = None
                    for cc in range(NT):
                        aft = []
                        if cc == 0:
                            aft = [ev_last[0], ev_last[1]]
                            if h % 2 == 0:
                                aft.append(ctx_tok)
                        pvstop = P.op("tensor", lambda e, cc=cc, hrow=hrow, h=h:
                                      e.matmul(
                            o_ps[hrow:hrow + 64, 0:256],
                            Vt[:, cc, h * 64:(h + 1) * 64], PTb[:, cc, :],
                            start=(cc == 0), stop=(cc == 15)), after=aft, inc=[pe])[pe]
                    pv_tok = pvstop
                    if h % 2 == 1:
                        hm = h // 2
                        aft = [pvstop]
                        if hm == 0:
                            aft.append(oproj_tok)
                        ctx_tok = P.op("scalar", lambda e, hm=hm: e.copy(
                            ctxT[:, hm, :], o_ps[:, 0:256]), after=aft, inc=[act])[act]
                    if h == 15:
                        a2_oproj(g)

            R = 8 * H * 2
            for r in range(R + 1):
                if r < R:
                    sstops[r] = a2_scores(r)
                    a2_mask(r, sstops[r])
                if r > 0:
                    a2_tail(r - 1)
            P.join()
        P.join()
        p1.close()

    emit_attn_a(out if upto == 1 else xres)
    if upto == 1:
        P.join()
        P.emit()
        return nc, stack

    def emit_ffn(src, dst, tag):
        TIL = BPC * NT
        f1 = ExitStack()
        wgq_b = sb(f1, f"wgq_b{tag}", [128, 8, FF], BF16)
        wuq_b = sb(f1, f"wuq_b{tag}", [128, 8, FF], BF16)
        xt_f = [sb(f1, f"fx{tag}{i}", [128, D], F32) for i in range(2)]
        xhat_f = sb(f1, f"xhat_f{tag}", [128, D], F32)
        xq_i8 = sb(f1, f"xq_i8{tag}", [128, D], I8)
        xq_bf = [sb(f1, f"xq_bf{tag}{i}", [128, D], BF16) for i in range(2)]
        xqT = sb(f1, f"xqT{tag}", [128, 8, 128], BF16)
        g_f = sb(f1, f"g_f{tag}", [128, FF], F32)
        u_f = sb(f1, f"u_f{tag}", [128, FF], F32)
        tq_i8 = sb(f1, f"tq_i8{tag}", [128, FF], I8)
        tq_bf = [sb(f1, f"tq_bf{tag}{i}", [128, FF], BF16) for i in range(2)]
        qst = sb(f1, f"qst{tag}", [128, 16], F32)
        lnA = sb(f1, f"lnA{tag}", [128, 2, 4], F32)
        for c in range(8):
            P.op("sync", lambda e, c=c: e.dma_start(
                out=wgq_b[:, c, :], in_=wq_scr["g"][c * 128:(c + 1) * 128, :]),
                inc=[dW], dma=True)
            wtok_all = P.op("sync", lambda e, c=c: e.dma_start(
                out=wuq_b[:, c, :], in_=wq_scr["u"][c * 128:(c + 1) * 128, :]),
                inc=[dW], dma=True)[dW]
        xt_free = [(), ()]
        ld_tok = [None, None]
        bank_tok = [None, None, None, None]
        trb_tok = [None, None]
        tq_out_tok = [None, None]
        gu_tok = None
        gf_done = None
        quant_tok = [None, None]
        sc_tok = [None, None]
        transp_tok = [None, None]
        up7_tok = [None, None]
        ln_prev = [{}, {}]

        def stageA(t):
            b, tt = divmod(t, NT)
            p = t % 2
            c0 = p * 8
            ld_tok[p] = P.op("sync", lambda e, b=b, tt=tt, p=p: e.dma_start(
                out=xt_f[p][:], in_=src[b, tt * 128:(tt + 1) * 128, :]),
                after=xt_free[p], inc=[dA[p]], dma=True)[dA[p]]
            r = ln_t(xt_f[p][:], xhat_f[:], D, eps5[:], lnA[:, p, :], ln_prev[p],
                     dep_dve=(ld_tok[p],), dep_act=(ld_tok[p],))
            xt_free[p] = (r["fin"], r["sq"])
            if dbg == "ffn" and tag == 0:
                P.op("sync", lambda e, b=b, tt=tt: e.dma_start(
                    out=dq_x[b, tt * 128:(tt + 1) * 128, :], in_=xhat_f[:]),
                    after=(r["fin"],), inc=[st], dma=True)
            P.op("vector", lambda e, c0=c0: e.reduce_max(
                out=qst[:, c0:c0 + 1], in_=xhat_f[:], axis=AX.X,
                apply_absolute_value=True), after=[dve], inc=[dve])
            P.op("vector", lambda e, c0=c0: e.tensor_scalar(
                out=qst[:, c0:c0 + 1], in0=qst[:, c0:c0 + 1], scalar1=1e-5,
                scalar2=None, op0=ALU.max), after=[dve], inc=[dve])
            P.op("vector", lambda e, c0=c0: e.reciprocal(
                out=qst[:, c0 + 1:c0 + 2], in_=qst[:, c0:c0 + 1]),
                after=[dve], inc=[dve])
            P.op("vector", lambda e, c0=c0: e.tensor_scalar(
                out=qst[:, c0 + 1:c0 + 2], in0=qst[:, c0 + 1:c0 + 2], scalar1=127.0,
                scalar2=None, op0=ALU.mult), after=[dve], inc=[dve])
            P.op("vector", lambda e, c0=c0: e.tensor_scalar(
                out=xq_i8[:], in0=xhat_f[:], scalar1=qst[:, c0 + 1:c0 + 2],
                scalar2=None, op0=ALU.mult), after=[dve], inc=[dve])
            quant_tok[p] = P.op("vector", lambda e, p=p: e.tensor_copy(
                xq_bf[p][:], xq_i8[:]), after=(transp_tok[p], dve), inc=[dve])[dve]
            if dbg == "ffn" and tag == 0:
                P.op("sync", lambda e, b=b, tt=tt, p=p: e.dma_start(
                    out=dq_xq[b, tt * 128:(tt + 1) * 128, :], in_=xq_bf[p][:]),
                    after=(quant_tok[p],), inc=[st], dma=True)
            P.op("vector", lambda e, c0=c0: e.tensor_scalar(
                out=qst[:, c0 + 4:c0 + 5], in0=qst[:, c0:c0 + 1],
                scalar1=1.0 / 127.0, scalar2=None, op0=ALU.mult),
                after=[dve], inc=[dve])
            P.op("vector", lambda e, c0=c0: e.tensor_tensor(
                out=qst[:, c0 + 2:c0 + 3], in0=qst[:, c0 + 4:c0 + 5],
                in1=w_scales[:, 0:1], op=ALU.mult),
                after=(up7_tok[p], dve), inc=[dve])
            sc_tok[p] = P.op("vector", lambda e, c0=c0: e.tensor_tensor(
                out=qst[:, c0 + 3:c0 + 4], in0=qst[:, c0 + 4:c0 + 5],
                in1=w_scales[:, 1:2], op=ALU.mult), after=[dve], inc=[dve])[dve]

        def stageB(t):
            nonlocal transp_tok
            p = t % 2
            c0 = p * 8
            trstop = [None, None]
            ev = [None, None]
            for m in range(8):
                bank = m // 4
                aft = []
                if m == 0:
                    aft.append(quant_tok[p])
                if m % 4 == 0:
                    aft.append(trb_tok[bank])
                tok = P.op("tensor", lambda e, m=m, bank=bank, p=p: e.transpose(
                    tr_ps[:, bank * 1024 + (m % 4) * 128:
                          bank * 1024 + (m % 4) * 128 + 128],
                    xq_bf[p][:, m * 128:(m + 1) * 128], id_t[:]),
                    after=aft, inc=[pe])[pe]
                if m % 4 == 3:
                    trstop[bank] = tok
            transp_tok[p] = tok
            for bank in range(2):
                ev[bank] = trb_tok[bank] = P.op("scalar", lambda e, bank=bank: e.copy(
                    xqT[:, bank * 4:(bank + 1) * 4, :],
                    tr_ps[:, bank * 1024: bank * 1024 + 512]
                    .rearrange("p (a b) -> p a b", a=4)),
                    after=(trstop[bank],), inc=[act])[act]
            for which, wq_b, dstf, scol, fn in (
                    (0, wgq_b, g_f, 2, AF.Silu), (1, wuq_b, u_f, 3, AF.Copy)):
                for n in range(8):
                    bk = (which * 8 + n) % 4
                    ps = big_ps[:, bk * 512:bk * 512 + 512]
                    stop = None
                    for k in range(8):
                        aft = []
                        if k == 0:
                            aft.append(bank_tok[bk])
                            if which == 0 and n == 0:
                                aft.append(ev[0])
                        if k == 4 and which == 0 and n == 0:
                            aft.append(ev[1])
                        if t == 0 and which == 0 and n == 0 and k == 0:
                            aft.append(wtok_all)
                        stop = P.op("tensor", lambda e, n=n, k=k, wq_b=wq_b, ps=ps:
                                    e.matmul(ps, xqT[:, k, :],
                                             wq_b[:, k, n * 512:(n + 1) * 512],
                                             start=(k == 0), stop=(k == 7)),
                                    after=aft, inc=[pe])[pe]
                    aft = [stop, sc_tok[p]]
                    if which == 0 and n == 0:
                        aft.append(gf_done)
                    bank_tok[bk] = P.op("scalar", lambda e, n=n, ps=ps, dstf=dstf,
                                        scol=scol, fn=fn, c0=c0: e.activation(
                        out=dstf[:, n * 512:(n + 1) * 512], in_=ps, func=fn, bias=0.0,
                        scale=qst[:, c0 + scol:c0 + scol + 1]),
                        after=aft, inc=[act])[act]
                    if which == 1 and n == 7:
                        up7_tok[p] = bank_tok[bk]

        def stageC(t):
            nonlocal gu_tok, gf_done
            b, tt = divmod(t, NT)
            p = t % 2
            c0 = p * 8
            gu_tok = P.op("vector", lambda e: e.tensor_tensor(
                out=g_f[:], in0=g_f[:], in1=u_f[:], op=ALU.mult),
                after=(up7_tok[p],), inc=[dve])[dve]
            if dbg == "ffn" and tag == 0:
                P.op("sync", lambda e, b=b, tt=tt: e.dma_start(
                    out=dq_gu[b, tt * 128:(tt + 1) * 128, :], in_=g_f[:]),
                    after=(gu_tok,), inc=[st], dma=True)
            P.op("vector", lambda e, c0=c0: e.reduce_max(
                out=qst[:, c0 + 5:c0 + 6], in_=g_f[:], axis=AX.X,
                apply_absolute_value=True), after=[dve], inc=[dve])
            P.op("vector", lambda e, c0=c0: e.tensor_scalar(
                out=qst[:, c0 + 5:c0 + 6], in0=qst[:, c0 + 5:c0 + 6], scalar1=1e-5,
                scalar2=None, op0=ALU.max), after=[dve], inc=[dve])
            P.op("vector", lambda e, b=b, tt=tt, c0=c0: e.tensor_scalar(
                out=stq[:, b * NT + tt:b * NT + tt + 1], in0=qst[:, c0 + 5:c0 + 6],
                scalar1=1.0 / 127.0, scalar2=None, op0=ALU.mult),
                after=[dve], inc=[dve])
            P.op("vector", lambda e, c0=c0: e.reciprocal(
                out=qst[:, c0 + 6:c0 + 7], in_=qst[:, c0 + 5:c0 + 6]),
                after=[dve], inc=[dve])
            P.op("vector", lambda e, c0=c0: e.tensor_scalar(
                out=qst[:, c0 + 6:c0 + 7], in0=qst[:, c0 + 6:c0 + 7], scalar1=127.0,
                scalar2=None, op0=ALU.mult), after=[dve], inc=[dve])
            gf_done = P.op("vector", lambda e, c0=c0: e.tensor_scalar(
                out=tq_i8[:], in0=g_f[:], scalar1=qst[:, c0 + 6:c0 + 7],
                scalar2=None, op0=ALU.mult), after=[dve], inc=[dve])[dve]
            cast = P.op("vector", lambda e, p=p: e.tensor_copy(
                tq_bf[p][:], tq_i8[:]), after=(tq_out_tok[p], dve), inc=[dve])[dve]
            tq_out_tok[p] = P.op("sync", lambda e, b=b, tt=tt, p=p: e.dma_start(
                out=tq_scr[b, tt * 128:(tt + 1) * 128, :], in_=tq_bf[p][:]),
                after=(cast,), inc=[sA[p]], dma=True)[sA[p]]

        for i in range(TIL + 1):
            if i < TIL:
                stageA(i)
            if i > 0:
                stageB(i - 1)
                stageC(i - 1)
        P.join()
        f1.close()
        f2 = ExitStack()
        wdq_b = sb(f2, f"wdq_b{tag}", [128, 32, D], BF16)
        tqi = [sb(f2, f"tqi{tag}{i}", [128, FF], BF16) for i in range(2)]
        tqT = sb(f2, f"tqT{tag}", [128, 32, 128], BF16)
        xt2 = [sb(f2, f"fx2{tag}{i}", [128, D], F32) for i in range(2)]
        d_f = sb(f2, f"d_f{tag}", [128, D], F32)
        s2st = sb(f2, f"s2st{tag}", [128, 2], F32)
        for c in range(32):
            wtok2_all = P.op("sync", lambda e, c=c: e.dma_start(
                out=wdq_b[:, c, :], in_=wq_scr["d"][c * 128:(c + 1) * 128, :]),
                inc=[dW], dma=True)[dW]
        ldq_tok = [None, None]
        ldx2_tok = [None, None]
        transp2_tok = [None, None]
        trb2_tok = [None, None]
        st14_tok = [None, None]
        st14_guard = [None, None]
        o_scale_tok = None
        add_prev = None
        add_tok = [None, None]
        out_tok = [None, None]

        def stageA2(t):
            b, tt = divmod(t, NT)
            p = t % 2
            ldq_tok[p] = P.op("sync", lambda e, b=b, tt=tt, p=p: e.dma_start(
                out=tqi[p][:], in_=tq_scr[b, tt * 128:(tt + 1) * 128, :]),
                after=(transp2_tok[p],), inc=[dA[p]], dma=True)[dA[p]]
            ldx2_tok[p] = P.op("sync", lambda e, b=b, tt=tt, p=p: e.dma_start(
                out=xt2[p][:], in_=src[b, tt * 128:(tt + 1) * 128, :]),
                after=(out_tok[p],), inc=[dB[p]], dma=True)[dB[p]]
            st14_tok[p] = P.op("vector", lambda e, b=b, tt=tt, p=p: e.tensor_tensor(
                out=s2st[:, p:p + 1], in0=stq[:, b * NT + tt:b * NT + tt + 1],
                in1=w_scales[:, 2:3], op=ALU.mult),
                after=(st14_guard[p],), inc=[dve])[dve]

        def stageB2(t):
            nonlocal o_scale_tok, add_prev
            b, tt = divmod(t, NT)
            p = t % 2
            ev2 = [None] * 8
            tok = None
            for m in range(32):
                g4 = m // 4
                bank = g4 % 2
                aft = []
                if m == 0:
                    aft.append(ldq_tok[p])
                if m % 4 == 0:
                    aft.append(trb2_tok[bank])
                tok = P.op("tensor", lambda e, m=m, bank=bank, p=p: e.transpose(
                    tr_ps[:, bank * 1024 + (m % 4) * 128:
                          bank * 1024 + (m % 4) * 128 + 128],
                    tqi[p][:, m * 128:(m + 1) * 128], id_t[:]),
                    after=aft, inc=[pe])[pe]
                if m % 4 == 3:
                    ev2[g4] = trb2_tok[bank] = P.op(
                        "scalar", lambda e, g4=g4, bank=bank: e.copy(
                            tqT[:, g4 * 4:(g4 + 1) * 4, :],
                            tr_ps[:, bank * 1024: bank * 1024 + 512]
                            .rearrange("p (a b) -> p a b", a=4)),
                        after=(tok,), inc=[act])[act]
            transp2_tok[p] = tok
            stop = None
            for n in range(2):
                ps = o_ps[:, n * 512:(n + 1) * 512]
                for k in range(32):
                    aft = []
                    if k == 0 and n == 0:
                        aft.append(o_scale_tok)
                    if n == 0 and k % 4 == 0:
                        aft.append(ev2[k // 4])
                    if t == 0 and n == 0 and k == 0:
                        aft.append(wtok2_all)
                    stop = P.op("tensor", lambda e, n=n, k=k, ps=ps: e.matmul(
                        ps, tqT[:, k, :], wdq_b[:, k, n * 512:(n + 1) * 512],
                        start=(k == 0), stop=(k == 31)), after=aft, inc=[pe])[pe]
            o_scale_tok = P.op("scalar", lambda e, p=p: e.activation(
                out=d_f[:], in_=o_ps[:], func=AF.Copy, bias=0.0,
                scale=s2st[:, p:p + 1]),
                after=(stop, st14_tok[p], add_prev), inc=[act])[act]
            st14_guard[p] = o_scale_tok
            add_prev = add_tok[p] = P.op("vector", lambda e, p=p: e.tensor_tensor(
                out=xt2[p][:], in0=xt2[p][:], in1=d_f[:], op=ALU.add),
                after=(o_scale_tok, ldx2_tok[p]), inc=[dve])[dve]
            out_tok[p] = P.op("sync", lambda e, b=b, tt=tt, p=p: e.dma_start(
                out=dst[b, tt * 128:(tt + 1) * 128, :], in_=xt2[p][:]),
                after=(add_tok[p],), inc=[sA[p]], dma=True)[sA[p]]

        for i in range(TIL + 1):
            if i < TIL:
                stageA2(i)
            if i > 0:
                stageB2(i - 1)
        P.join()
        f2.close()

    emit_ffn(xres, out if upto == 2 else xres, 0)
    if upto == 2:
        P.join()
        P.emit()
        return nc, stack

    def emit_attn_b(dst):
        b3 = ExitStack()
        bias_t = {}
        for nm in ("bq_bc", "bk_bc", "bv_bc", "bo_bc", "b2_bc", "g1_bc", "be1_bc",
                   "g2_bc", "be2_bc"):
            bias_t[nm] = sb(b3, nm + "_t", [128, D], F32)
            P.op("sync", lambda e, nm=nm: e.dma_start(out=bias_t[nm][:], in_=bia[nm]),
                 inc=[ld], dma=True)
        xt_f = [sb(b3, f"bx{i}", [128, D], F32) for i in range(2)]
        ao_keep = sb(b3, "ao_keep", [128, NT, D], BF16)
        xh_bf = sb(b3, "xh_bf", [128, D], BF16)
        xhT = sb(b3, "xhT", [128, 8, 128], BF16)
        o_f = sb(b3, "o_f3", [128, D], F32)
        ao_f = sb(b3, "ao_f", [128, D], F32)

        kv_ps = big_ps[:, 1536:2048]

        def emit_elunorm(src_f, dst_bf, tmp_f, eq_f, sq_f, s16):
            P.op("vector", lambda e, src_f=src_f: e.tensor_scalar(
                out=tmp_f[:], in0=src_f[:], scalar1=0.0, scalar2=None, op0=ALU.min),
                after=[dve, act], inc=[dve])
            P.op("scalar", lambda e: e.activation(out=eq_f[:], in_=tmp_f[:], func=AF.Exp,
                                                  bias=0.0, scale=1.0),
                 after=[act, dve], inc=[act])
            P.op("vector", lambda e, src_f=src_f: e.scalar_tensor_tensor(
                out=eq_f[:], in0=src_f[:], scalar=0.0, in1=eq_f[:], op0=ALU.max,
                op1=ALU.add), after=[dve, act], inc=[dve])
            P.op("vector", lambda e: e.tensor_scalar(
                out=eq_f[:], in0=eq_f[:], scalar1=1.0, scalar2=None, op0=ALU.subtract),
                after=[dve], inc=[dve])
            P.op("scalar", lambda e: e.activation(out=sq_f[:], in_=eq_f[:], func=AF.Square,
                                                  bias=0.0, scale=1.0),
                 after=[act, dve], inc=[act])
            P.op("vector", lambda e: e.tensor_reduce(
                out=s16[:], in_=sq_f[:].rearrange("p (h d) -> p h d", h=16),
                axis=AX.X, op=ALU.add), after=[dve, act], inc=[dve])
            P.op("scalar", lambda e: e.activation(out=s16[:], in_=s16[:], func=AF.Ln,
                                                  bias=0.0, scale=1.0),
                 after=[act, dve], inc=[act])
            P.op("scalar", lambda e: e.activation(out=s16[:], in_=s16[:], func=AF.Exp,
                                                  bias=0.0, scale=-0.5),
                 after=[act], inc=[act])
            return P.op("vector", lambda e, dst_bf=dst_bf: e.tensor_tensor(
                out=dst_bf[:].rearrange("p (h d) -> p h d", h=16),
                in0=eq_f[:].rearrange("p (h d) -> p h d", h=16),
                in1=s16[:].unsqueeze(2).to_broadcast([128, 16, 64]), op=ALU.mult),
                after=[dve, act, pe], inc=[dve])

        for b in range(BPC):
            b12 = ExitStack()
            wstage = sb(b12, f"wstage3{b}", [128, D], F32)
            w0b_b = sb(b12, f"w0b_b{b}", [128, 8, D], BF16)
            Wk_b = sb(b12, f"Wk_b{b}", [128, 8, D], BF16)
            Wv_b = sb(b12, f"Wv_b{b}", [128, 8, D], BF16)
            qn_keep = sb(b12, f"qn_keep{b}", [128, NT, D], BF16)
            q_f = sb(b12, f"q_f{b}", [128, D], F32)
            k_f = sb(b12, f"k_f{b}", [128, D], F32)
            v_bf = sb(b12, f"v_bf{b}", [128, D], BF16)
            xh_f = sb(b12, f"xh_f{b}", [128, D], F32)
            tmp_f = sb(b12, f"tmp_f{b}", [128, D], F32)
            eq_f = sb(b12, f"eq_f{b}", [128, D], F32)
            sq_f = sb(b12, f"sq_f{b}", [128, D], F32)
            s16 = sb(b12, f"s16{b}", [128, 16], F32)
            n_bf = sb(b12, f"n_bf{b}", [128, D], BF16)
            kv_bf = sb(b12, f"kv_bf{b}", [128, 512], BF16)
            ctxbT = sb(b12, f"ctxbT{b}", [128, 8, 128], BF16)
            elun = lambda sf, db: emit_elunorm(sf, db, tmp_f, eq_f, sq_f, s16)
            conv_w(Wk_b, bWk, wstage, D, D)
            conv_w(Wv_b, bWv, wstage, D, D)
            conv_w(w0b_b, bWq, wstage, D, D)
            for tt in range(NT):
                xt = xt_f[tt % 2]
                P.op("sync", lambda e, b=b, tt=tt, xt=xt: e.dma_start(
                    out=xt[:], in_=xres[b, tt * 128:(tt + 1) * 128, :]),
                    after=[dve, act], inc=[ld], dma=True)
                emit_ln(xt[:], xh_bf[:], D, 1e-5, dep=(ld,))
                emit_transposes(xh_bf, 8, None,
                                lambda g4: xhT[:, g4 * 4:(g4 + 1) * 4, :])
                for W_b, bias_nm, dstt in ((w0b_b, "bq_bc", q_f), (Wk_b, "bk_bc", k_f),
                                           (Wv_b, "bv_bc", v_bf)):
                    for n in range(2):
                        ps = big_ps[:, n * 512:n * 512 + 512]
                        for k in range(8):
                            aft = [act, pe, dve, gp] if k == 0 else [pe]
                            P.op("tensor", lambda e, n=n, k=k, W_b=W_b, ps=ps: e.matmul(
                                ps, xhT[:, k, :], W_b[:, k, n * 512:(n + 1) * 512],
                                start=(k == 0), stop=(k == 7)), after=aft, inc=[pe])
                    P.op("vector", lambda e, dstt=dstt, bias_nm=bias_nm: e.tensor_tensor(
                        out=dstt[:], in0=big_ps[:, 0:1024], in1=bias_t[bias_nm][:],
                        op=ALU.add), after=[pe, dve], inc=[dve])
                elun(q_f, n_bf)
                P.op("vector", lambda e, tt=tt: e.tensor_copy(qn_keep[:, tt, :], n_bf[:]),
                     after=[dve], inc=[dve])
                if dbg:
                    P.op("vector", lambda e: e.tensor_copy(tmp_f[:], n_bf[:]),
                         after=[dve], inc=[dve])
                    P.op("sync", lambda e, b=b, tt=tt: e.dma_start(
                        out=qn_dbg[b, tt * 128:(tt + 1) * 128, :], in_=tmp_f[:]),
                        after=[dve], inc=[st], dma=True)
                elun(k_f, n_bf)
                for h in range(H):
                    hrow = (h % 2) * 64
                    hm = h // 2
                    aft = [dve, pe] if h == 0 else [pe]
                    P.op("tensor", lambda e, h=h, hrow=hrow, hm=hm, tt=tt: e.matmul(
                        kv_ps[hrow:hrow + 64, hm * 64:(hm + 1) * 64],
                        n_bf[:, h * 64:(h + 1) * 64], v_bf[:, h * 64:(h + 1) * 64],
                        start=(tt == 0 and h < 2), stop=(tt == NT - 1)),
                        after=aft, inc=[pe])
            P.op("scalar", lambda e: e.activation(out=kv_bf[:], in_=kv_ps, func=AF.Copy,
                                                  bias=0.0, scale=0.125),
                 after=[pe, act], inc=[act])
            if dbg:
                P.op("vector", lambda e: e.tensor_copy(xh_f[:, 0:512], kv_bf[:]),
                     after=[act, dve], inc=[dve])
                P.op("sync", lambda e, b=b: e.dma_start(
                    out=kv_dbg[b], in_=xh_f[:, 0:512]), after=[dve], inc=[st], dma=True)
            conv_w(w0b_b, bWo, wstage, D, D)
            Wo_b = w0b_b
            for tt in range(NT):
                xt = xt_f[tt % 2]
                P.op("sync", lambda e, b=b, tt=tt, xt=xt: e.dma_start(
                    out=xt[:], in_=xres[b, tt * 128:(tt + 1) * 128, :]),
                    after=[dve, act], inc=[ld], dma=True)
                emit_ln(xt[:], xh_f[:], D, 1e-5, dep=(ld,))
                P.op("vector", lambda e, tt=tt: e.tensor_copy(xh_bf[:], qn_keep[:, tt, :]),
                     after=[dve, pe], inc=[dve])
                emit_transposes(xh_bf, 8, None,
                                lambda g4: xhT[:, g4 * 4:(g4 + 1) * 4, :])
                for h in range(H):
                    hrow = (h % 2) * 64
                    hm = h // 2
                    aft = [act, pe] if h == 0 else [pe]
                    P.op("tensor", lambda e, hrow=hrow, hm=hm, h=h: e.matmul(
                        big_ps[hrow:hrow + 64, hm * 128:hm * 128 + 128],
                        kv_bf[hrow:hrow + 64, hm * 64:(hm + 1) * 64],
                        xhT[hrow:hrow + 64, h // 2, :],
                        start=True, stop=True), after=aft, inc=[pe])
                P.op("scalar", lambda e: e.copy(
                    ctxbT[:].rearrange("p a b -> p (a b)"), big_ps[:, 0:1024]),
                    after=[pe, act], inc=[act])
                for n in range(2):
                    for k in range(8):
                        aft = [act, pe, dve] if k == 0 else [pe]
                        P.op("tensor", lambda e, n=n, k=k: e.matmul(
                            o_ps[:, n * 512:(n + 1) * 512], ctxbT[:, k, :],
                            Wo_b[:, k, n * 512:(n + 1) * 512],
                            start=(k == 0), stop=(k == 7)), after=aft, inc=[pe])
                P.op("scalar", lambda e: e.copy(o_f[:], o_ps[:]),
                     after=[pe, act, dve], inc=[act])
                if dbg:
                    P.op("sync", lambda e, b=b, tt=tt: e.dma_start(
                        out=ctx_dbg[b, tt * 128:(tt + 1) * 128, :], in_=o_f[:]),
                        after=[act, dve], inc=[st], dma=True)
                P.op("vector", lambda e: e.tensor_tensor(
                    out=o_f[:], in0=o_f[:], in1=bias_t["bo_bc"][:], op=ALU.add),
                    after=[act, dve, st], inc=[dve])
                P.op("vector", lambda e: e.tensor_tensor(
                    out=o_f[:], in0=o_f[:], in1=xh_f[:], op=ALU.add),
                    after=[dve], inc=[dve])
                emit_ln(o_f[:], ao_f[:], D, 1e-12)
                P.op("vector", lambda e: e.tensor_tensor(
                    out=ao_f[:], in0=ao_f[:], in1=bias_t["g1_bc"][:], op=ALU.mult),
                    after=[dve], inc=[dve])
                P.op("vector", lambda e: e.tensor_tensor(
                    out=ao_f[:], in0=ao_f[:], in1=bias_t["be1_bc"][:],
                    op=ALU.add), after=[dve, st], inc=[dve])
                P.op("sync", lambda e, b=b, tt=tt: e.dma_start(
                    out=ao_scr[b, tt * 128:(tt + 1) * 128, :], in_=ao_f[:]),
                    after=[dve], inc=[st], dma=True)
                P.op("vector", lambda e, tt=tt: e.tensor_copy(ao_keep[:, tt, :], ao_f[:]),
                     after=[dve], inc=[dve])
            P.join()
            b12.close()
            bfa = ExitStack()
            W1_b = sb(bfa, f"W1_b{b}", [128, 8, FF], BF16)
            b1_t = sb(bfa, f"b1_t{b}", [128, FF], BF16)
            wstage4 = sb(bfa, f"wstage4{b}", [128, 2048], F32)
            hg_bf = [sb(bfa, f"hg_bf{b}{i}", [128, FF], BF16) for i in range(2)]
            aoT = sb(bfa, f"aoT{b}", [128, 8, 128], BF16)
            hpre = sb(bfa, f"hpre{b}", [128, 2048], F32)
            for half in range(2):
                P.op("sync", lambda e, half=half: e.dma_start(
                    out=wstage4[:], in_=bia["b1_bc"][:, half * 2048:(half + 1) * 2048]),
                    after=[dve], inc=[ld], dma=True)
                P.op("vector", lambda e, half=half: e.tensor_copy(
                    b1_t[:, half * 2048:(half + 1) * 2048], wstage4[:]),
                    after=[ld], inc=[dve])
            for c in range(8):
                for half in range(2):
                    P.op("sync", lambda e, c=c, half=half: e.dma_start(
                        out=wstage4[:],
                        in_=bW1[c * 128:(c + 1) * 128, half * 2048:(half + 1) * 2048]),
                        after=[dve], inc=[ld], dma=True)
                    P.op("vector", lambda e, c=c, half=half: e.tensor_copy(
                        W1_b[:, c, half * 2048:(half + 1) * 2048], wstage4[:]),
                        after=[ld, pe, act], inc=[dve])
            wtok_b3a = dve.tok()
            trb3_tok = [None, None]
            b3bank_tok = [None, None, None, None]
            hpre_tok = [None, None, None, None]
            hg_free = [None, None]
            for tt in range(NT):
                p = tt % 2
                trstop = [None, None]
                ev = [None, None]
                for m in range(8):
                    bank = m // 4
                    aft = []
                    if m % 4 == 0:
                        aft.append(trb3_tok[bank])
                    tok = P.op("tensor", lambda e, m=m, bank=bank, tt=tt: e.transpose(
                        tr_ps[:, bank * 1024 + (m % 4) * 128:
                              bank * 1024 + (m % 4) * 128 + 128],
                        ao_keep[:, tt, m * 128:(m + 1) * 128], id_t[:]),
                        after=aft, inc=[pe])[pe]
                    if m % 4 == 3:
                        trstop[bank] = tok
                for bank in range(2):
                    ev[bank] = trb3_tok[bank] = P.op(
                        "scalar", lambda e, bank=bank: e.copy(
                            aoT[:, bank * 4:(bank + 1) * 4, :],
                            tr_ps[:, bank * 1024: bank * 1024 + 512]
                            .rearrange("p (a b) -> p a b", a=4)),
                        after=(trstop[bank],), inc=[act])[act]
                hg = hg_bf[p]
                gel = None
                for n in range(8):
                    bk = n % 4
                    ps = big_ps[:, bk * 512:bk * 512 + 512]
                    stop = None
                    for k in range(8):
                        aft = []
                        if k == 0:
                            aft.append(b3bank_tok[bk])
                            if n == 0:
                                aft.append(ev[0])
                                if tt == 0:
                                    aft.append(wtok_b3a)
                        if k == 4 and n == 0:
                            aft.append(ev[1])
                        stop = P.op("tensor", lambda e, n=n, k=k, ps=ps: e.matmul(
                            ps, aoT[:, k, :], W1_b[:, k, n * 512:(n + 1) * 512],
                            start=(k == 0), stop=(k == 7)), after=aft, inc=[pe])[pe]
                    hp = hpre[:, bk * 512:bk * 512 + 512]
                    b3bank_tok[bk] = P.op("vector", lambda e, n=n, ps=ps, hp=hp:
                                          e.tensor_tensor(
                        out=hp, in0=ps, in1=b1_t[:, n * 512:(n + 1) * 512],
                        op=ALU.add), after=(stop, hpre_tok[bk]), inc=[dve])[dve]
                    aft3 = [b3bank_tok[bk]]
                    if n == 0:
                        aft3.append(hg_free[p])
                    hpre_tok[bk] = gel = P.op("scalar", lambda e, n=n, hg=hg, hp=hp:
                                              e.activation(
                        out=hg[:, n * 512:(n + 1) * 512], in_=hp, func=AF.Gelu,
                        bias=0.0, scale=1.0), after=aft3, inc=[act])[act]
                hg_free[p] = P.op("sync", lambda e, b=b, tt=tt, hg=hg: e.dma_start(
                    out=tq_scr[b, tt * 128:(tt + 1) * 128, :], in_=hg[:]),
                    after=(gel,), inc=[sA[p]], dma=True)[sA[p]]
            P.join()
            bfa.close()
            bfb = ExitStack()
            W2_b = sb(bfb, f"W2_b{b}", [128, 32, D], BF16)
            wstage5 = sb(bfb, f"wstage5{b}", [128, D], F32)
            hgi = [sb(bfb, f"hgi{b}{i}", [128, FF], BF16) for i in range(2)]
            hgT = sb(bfb, f"hgT{b}", [128, 32, 128], BF16)
            d_f = sb(bfb, f"d_f3{b}", [128, D], F32)
            aor_f = [sb(bfb, f"aor_f{b}{i}", [128, D], F32) for i in range(2)]
            lnB = sb(bfb, f"lnB{b}", [128, 2, 4], F32)
            conv_w(W2_b, bW2, wstage5, FF, D)
            wtok_b3b = dve.tok()
            lnB_prev = [{}, {}]
            transpB_tok = [None, None]
            trbB_tok = [None, None]
            aor_free = [None, None]
            xout_tok = [None, None]
            dscale_tok = None
            df_free = ()
            for tt in range(NT):
                p = tt % 2
                hg = hgi[p]
                ldh = P.op("sync", lambda e, b=b, tt=tt, hg=hg: e.dma_start(
                    out=hg[:], in_=tq_scr[b, tt * 128:(tt + 1) * 128, :]),
                    after=(transpB_tok[p],), inc=[dA[p]], dma=True)[dA[p]]
                lda = P.op("sync", lambda e, b=b, tt=tt, p=p: e.dma_start(
                    out=aor_f[p][:], in_=ao_scr[b, tt * 128:(tt + 1) * 128, :]),
                    after=(aor_free[p],), inc=[dB[p]], dma=True)[dB[p]]
                xt = xt_f[p]
                ldx = P.op("sync", lambda e, b=b, tt=tt, xt=xt: e.dma_start(
                    out=xt[:], in_=xres[b, tt * 128:(tt + 1) * 128, :]),
                    after=(xout_tok[p],), inc=[dC[p]], dma=True)[dC[p]]
                ev2 = [None] * 8
                tok = None
                for m in range(32):
                    g4 = m // 4
                    bank = g4 % 2
                    aft = []
                    if m == 0:
                        aft.append(ldh)
                    if m % 4 == 0:
                        aft.append(trbB_tok[bank])
                    tok = P.op("tensor", lambda e, m=m, bank=bank, hg=hg: e.transpose(
                        tr_ps[:, bank * 1024 + (m % 4) * 128:
                              bank * 1024 + (m % 4) * 128 + 128],
                        hg[:, m * 128:(m + 1) * 128], id_t[:]),
                        after=aft, inc=[pe])[pe]
                    if m % 4 == 3:
                        ev2[g4] = trbB_tok[bank] = P.op(
                            "scalar", lambda e, g4=g4, bank=bank: e.copy(
                                hgT[:, g4 * 4:(g4 + 1) * 4, :],
                                tr_ps[:, bank * 1024: bank * 1024 + 512]
                                .rearrange("p (a b) -> p a b", a=4)),
                            after=(tok,), inc=[act])[act]
                transpB_tok[p] = tok
                stop = None
                for n in range(2):
                    for k in range(32):
                        aft = []
                        if k == 0 and n == 0:
                            aft.append(dscale_tok)
                            if tt == 0:
                                aft.append(wtok_b3b)
                        if n == 0 and k % 4 == 0:
                            aft.append(ev2[k // 4])
                        stop = P.op("tensor", lambda e, n=n, k=k: e.matmul(
                            o_ps[:, n * 512:(n + 1) * 512], hgT[:, k, :],
                            W2_b[:, k, n * 512:(n + 1) * 512],
                            start=(k == 0), stop=(k == 31)), after=aft, inc=[pe])[pe]
                dscale_tok = P.op("vector", lambda e: e.tensor_tensor(
                    out=d_f[:], in0=o_ps[:], in1=bias_t["b2_bc"][:], op=ALU.add),
                    after=(stop, dve) + tuple(df_free), inc=[dve])[dve]
                aor_free[p] = P.op("vector", lambda e, p=p: e.tensor_tensor(
                    out=d_f[:], in0=d_f[:], in1=aor_f[p][:], op=ALU.add),
                    after=(lda, dve), inc=[dve])[dve]
                r = ln_t(d_f[:], o_f[:], D, eps12[:], lnB[:, p, :], lnB_prev[p],
                         dep_act=(aor_free[p],))
                df_free = (r["fin"], r["sq"])
                P.op("vector", lambda e: e.tensor_tensor(
                    out=o_f[:], in0=o_f[:], in1=bias_t["g2_bc"][:], op=ALU.mult),
                    after=[dve], inc=[dve])
                P.op("vector", lambda e: e.tensor_tensor(
                    out=o_f[:], in0=o_f[:], in1=bias_t["be2_bc"][:], op=ALU.add),
                    after=[dve], inc=[dve])
                fin = P.op("vector", lambda e, xt=xt: e.tensor_tensor(
                    out=xt[:], in0=xt[:], in1=o_f[:], op=ALU.add),
                    after=(ldx, dve), inc=[dve])[dve]
                xout_tok[p] = P.op("sync", lambda e, b=b, tt=tt, xt=xt: e.dma_start(
                    out=dst[b, tt * 128:(tt + 1) * 128, :], in_=xt[:]),
                    after=(fin,), inc=[sA[p]], dma=True)[sA[p]]
            P.join()
            bfb.close()
        P.join()
        b3.close()

    emit_attn_b(out if upto == 3 else xres)
    if upto == 3:
        P.join()
        P.emit()
        return nc, stack

    emit_ffn(xres, out, 1)
    P.join()
    P.emit()
    return nc, stack


def _host_layouts(inputs):
    f = lambda a: np.ascontiguousarray(a, dtype=np.float32)
    L = {}
    L["ident"] = np.eye(128, dtype=ml_dtypes.bfloat16)
    L["qWT"] = f(np.asarray(inputs["L0_attn_a_qW"]).T)
    L["kWT"] = f(np.asarray(inputs["L0_attn_a_kW"]).T)
    L["vWT"] = f(np.asarray(inputs["L0_attn_a_vW"]).T)
    L["oWT"] = f(np.asarray(inputs["L0_attn_a_oW"]).T)
    L["qbT"] = f(np.asarray(inputs["L0_attn_a_qb"]).reshape(8, 128).T)
    L["kbT"] = f(np.asarray(inputs["L0_attn_a_kb"]).reshape(8, 128).T)
    L["vb_bc"] = f(np.broadcast_to(np.asarray(inputs["L0_attn_a_vb"]), (128, D)))
    L["ob_bc"] = f(np.broadcast_to(np.asarray(inputs["L0_attn_a_ob"]), (128, D)))
    L["skT"] = f(np.asarray(inputs["L0_attn_a_static_keys"])
                 .reshape(8, 8, 128).transpose(2, 1, 0))
    L["wgT"] = f(np.asarray(inputs["L0_ffn_w_gate"]).T)
    L["wuT"] = f(np.asarray(inputs["L0_ffn_w_up"]).T)
    L["wdT"] = f(np.asarray(inputs["L0_ffn_w_down"]).T)
    L["bWq"] = f(np.asarray(inputs["L0_attn_b_Wq"]))
    L["bWk"] = f(np.asarray(inputs["L0_attn_b_Wk"]))
    L["bWv"] = f(np.asarray(inputs["L0_attn_b_Wv"]))
    L["bWo"] = f(np.asarray(inputs["L0_attn_b_Wo"]))
    L["bW1"] = f(np.asarray(inputs["L0_attn_b_W1"]))
    L["bW2"] = f(np.asarray(inputs["L0_attn_b_W2"]))
    for nm, key in (("bq_bc", "L0_attn_b_bq"), ("bk_bc", "L0_attn_b_bk"),
                    ("bv_bc", "L0_attn_b_bv"), ("bo_bc", "L0_attn_b_bo"),
                    ("b2_bc", "L0_attn_b_b2"), ("g1_bc", "L0_attn_b_g1"),
                    ("be1_bc", "L0_attn_b_be1"), ("g2_bc", "L0_attn_b_g2"),
                    ("be2_bc", "L0_attn_b_be2")):
        L[nm] = f(np.broadcast_to(np.asarray(inputs[key]), (128, D)))
    L["b1_bc"] = f(np.broadcast_to(np.asarray(inputs["L0_attn_b_b1"]), (128, FF)))
    return L


def kernel(**inputs):
    x = np.ascontiguousarray(np.asarray(inputs["L0_attn_a_x"], dtype=np.float32))
    L = _host_layouts(inputs)
    nc, stack = build_program(upto=4)
    in_maps = []
    for c in range(NCORES):
        m = dict(L)
        m["x"] = np.ascontiguousarray(x[c * BPC:(c + 1) * BPC])
        in_maps.append(m)
    res = run_bass_kernel_spmd(nc, in_maps, core_ids=list(range(NCORES)))
    stack.close()
    return np.concatenate([r["out"] for r in res.results], axis=0)
